# version 16
# baseline (speedup 1.0000x reference)
import contextlib
import numpy as np
import ml_dtypes
import concourse.bass as bass
import concourse.mybir as mybir
from concourse.bass_utils import run_bass_kernel_spmd

F32 = mybir.dt.float32
BF16 = mybir.dt.bfloat16
I32 = mybir.dt.int32
AF = mybir.ActivationFunctionType
ALU = mybir.AluOpType
AX = mybir.AxisListType

MAGIC = 12582912.0
NB = np.dtype(ml_dtypes.bfloat16)

D = 1024
T = 2048
DC = D // 128
FF = 3072
HID = 4096


class Prog:
    ENG = ("sync", "tensor", "vector", "scalar", "gpsimd")

    def __init__(self):
        self.nc = bass.Bass("TRN2", target_bir_lowering=False, debug=False)
        self.stack = contextlib.ExitStack()
        self.ops = {e: [] for e in self.ENG}
        self.semh = {}
        self.semc = {}

    def sem(self, name):
        if name not in self.semh:
            self.semh[name] = self.stack.enter_context(self.nc.semaphore(name))
            self.semc[name] = 0
        return self.semh[name]

    def sbuf(self, name, shape, dtype, stack=None):
        return (stack or self.stack).enter_context(
            self.nc.sbuf_tensor(name, list(shape), dtype))

    def psum(self, name, shape, dtype=F32):
        return self.stack.enter_context(self.nc.psum_tensor(name, list(shape), dtype))

    def dram(self, name, shape, dtype, kind):
        return self.nc.dram_tensor(name, list(shape), dtype, kind=kind)

    def raw(self, engine, fn):
        self.ops[engine].append(fn)

    def wait(self, engine, sem, val):
        if val <= 0:
            return
        h = self.sem(sem)
        self.ops[engine].append(lambda eng, h=h, v=val: eng.wait_ge(h, v))

    def op(self, engine, fn, sem=None, k=None):
        if sem is None:
            sem = {"sync": "qs", "gpsimd": "qg"}.get(engine, engine[0])
        if k is None:
            k = 16 if sem.startswith("q") or sem == "cc" else 1
        self.sem(sem)
        self.semc[sem] += k
        h = self.semh[sem]
        self.ops[engine].append(lambda eng, h=h, k=k, fn=fn: fn(eng).then_inc(h, k))
        return self.semc[sem]

    def opn(self, engine, fn):
        self.ops[engine].append(lambda eng, fn=fn: fn(eng))

    def dma(self, queue, out, in_, sem=None):
        return self.op(queue, lambda eng, o=out, i=in_: eng.dma_start(out=o, in_=i),
                       sem=sem)

    def mark(self, sem):
        return self.semc.get(sem, 0)

    def barrier(self, engines=None):
        engines = engines or self.ENG
        marks = [(s, c) for s, c in self.semc.items() if c > 0]
        for e in engines:
            for s, c in marks:
                self.wait(e, s, c)

    def finalize(self):
        with self.nc.Block() as block:
            for e in self.ENG:
                ops = self.ops[e]
                if not ops:
                    continue
                def body(eng, ops=ops):
                    for f in ops:
                        f(eng)
                getattr(block, e)(body)
        self.stack.close()
        return self.nc



def emit_ln_stats(p, ps, xb, sq, ones_col):
    for kc in range(DC):
        sm = p.op("scalar", lambda eng, kc=kc: eng.activation(sq[:], xb[kc][:], AF.Square))
        p.wait("tensor", "s", sm)
        for tb in range(T // 512):
            p.opn("tensor", lambda eng, kc=kc, tb=tb:
                  eng.matmul(ps[0:1, tb * 512:(tb + 1) * 512], ones_col[:],
                             xb[kc][:, tb * 512:(tb + 1) * 512],
                             start=(kc == 0), stop=(kc == DC - 1)))
            if tb < T // 512 - 1:
                p.opn("tensor", lambda eng, kc=kc, tb=tb:
                      eng.matmul(ps[0:1, T + tb * 512:T + (tb + 1) * 512],
                                 ones_col[:], sq[:, tb * 512:(tb + 1) * 512],
                                 start=(kc == 0), stop=(kc == DC - 1)))
            else:
                p.op("tensor", lambda eng, kc=kc, tb=tb:
                     eng.matmul(ps[0:1, T + tb * 512:T + (tb + 1) * 512],
                                ones_col[:], sq[:, tb * 512:(tb + 1) * 512],
                                start=(kc == 0), stop=(kc == DC - 1)), sem="t")
        p.wait("scalar", "t", p.mark("t"))
    return p.mark("t")


def emit_ln_rows(p, ps, R, eps1_ap, eps2_ap, combine_rms):
    mean, var, rstd, rho, tmp = (R[k] for k in ("mean", "var", "rstd", "rho", "tmp"))
    p.op("vector", lambda eng: eng.tensor_scalar(mean[:], ps[0:1, 0:T],
                                                 1.0 / D, None, ALU.mult))
    vm = p.op("vector", lambda eng: eng.tensor_scalar(tmp[:], ps[0:1, T:2 * T],
                                                      1.0 / D, None, ALU.mult))
    p.wait("scalar", "v", vm)
    sm = p.op("scalar", lambda eng: eng.activation(var[:], mean[:], AF.Square))
    p.wait("vector", "s", sm)
    vm = p.op("vector", lambda eng: eng.tensor_sub(var[:], tmp[:], var[:]))
    p.wait("scalar", "v", vm)
    sm = p.op("scalar", lambda eng: eng.activation(tmp[:], var[:], AF.Sqrt,
                                                   bias=eps1_ap, scale=1.0))
    p.wait("vector", "s", sm)
    vm = p.op("vector", lambda eng: eng.reciprocal(rstd[:], tmp[:]))
    if combine_rms:
        p.wait("scalar", "v", vm)
        sm = p.op("scalar", lambda eng: eng.activation(tmp[:], rstd[:], AF.Square))
        p.wait("vector", "s", sm)
        vm = p.op("vector", lambda eng: eng.tensor_mul(tmp[:], tmp[:], var[:]))
        p.wait("scalar", "v", vm)
        sm = p.op("scalar", lambda eng: eng.activation(tmp[:], tmp[:], AF.Sqrt,
                                                       bias=eps2_ap, scale=1.0))
        p.wait("vector", "s", sm)
        vm = p.op("vector", lambda eng: eng.reciprocal(tmp[:], tmp[:]))
        p.wait("vector", "v", vm)
        vm = p.op("vector", lambda eng: eng.tensor_mul(rho[:], rstd[:], tmp[:]))
        p.wait("vector", "v", vm)
        vm = p.op("vector", lambda eng: eng.scalar_tensor_tensor(
            var[:], mean[:], -1.0, rho[:], ALU.mult, ALU.mult))
    vm = p.op("vector", lambda eng: eng.scalar_tensor_tensor(
        tmp[:], mean[:], -1.0, rstd[:], ALU.mult, ALU.mult))
    return vm


def emit_ln_rows_plain(p, ps, mean, rstd, tmp, eps1_ap):
    p.op("vector", lambda eng: eng.tensor_scalar(mean[:], ps[0:1, 0:T],
                                                 1.0 / D, None, ALU.mult))
    vm = p.op("vector", lambda eng: eng.tensor_scalar(tmp[:], ps[0:1, T:2 * T],
                                                      1.0 / D, None, ALU.mult))
    p.wait("scalar", "v", vm)
    sm = p.op("scalar", lambda eng: eng.activation(rstd[:], mean[:], AF.Square))
    p.wait("vector", "s", sm)
    vm = p.op("vector", lambda eng: eng.tensor_sub(tmp[:], tmp[:], rstd[:]))
    p.wait("scalar", "v", vm)
    sm = p.op("scalar", lambda eng: eng.activation(tmp[:], tmp[:], AF.Sqrt,
                                                   bias=eps1_ap, scale=1.0))
    p.wait("vector", "s", sm)
    def _recip(eng):
        with p.nc.allow_low_precision(reason="bf16 LN rows"):
            return eng.reciprocal(rstd[:], tmp[:])
    vm = p.op("vector", _recip)
    p.wait("vector", "v", vm)
    vm = p.op("vector", lambda eng: eng.scalar_tensor_tensor(
        tmp[:], mean[:], -1.0, rstd[:], ALU.mult, ALU.mult))
    return vm


def emit_ln_rows4(p, ps, mrow, var, rstd, rho, eps1_ap, eps2_ap):
    vm = p.op("vector", lambda eng: eng.tensor_scalar(mrow[:], ps[0:1, 0:T],
                                                      1.0 / D, None, ALU.mult))
    p.wait("scalar", "v", vm)
    sm = p.op("scalar", lambda eng: eng.activation(var[:], mrow[:], AF.Square))
    p.wait("vector", "s", sm)
    vm = p.op("vector", lambda eng: eng.scalar_tensor_tensor(
        var[:], ps[0:1, T:2 * T], 1.0 / D, var[:], ALU.mult, ALU.subtract))
    p.wait("scalar", "v", vm)
    sm = p.op("scalar", lambda eng: eng.activation(rho[:], var[:], AF.Sqrt,
                                                   bias=eps1_ap, scale=1.0))
    p.wait("vector", "s", sm)
    vm = p.op("vector", lambda eng: eng.reciprocal(rstd[:], rho[:]))
    p.wait("scalar", "v", vm)
    sm = p.op("scalar", lambda eng: eng.activation(rho[:], rstd[:], AF.Square))
    p.wait("vector", "s", sm)
    vm = p.op("vector", lambda eng: eng.tensor_mul(rho[:], rho[:], var[:]))
    p.wait("scalar", "v", vm)
    sm = p.op("scalar", lambda eng: eng.activation(rho[:], rho[:], AF.Sqrt,
                                                   bias=eps2_ap, scale=1.0))
    p.wait("vector", "s", sm)
    vm = p.op("vector", lambda eng: eng.reciprocal(rho[:], rho[:]))
    p.wait("vector", "v", vm)
    vm = p.op("vector", lambda eng: eng.tensor_mul(rho[:], rstd[:], rho[:]))
    p.wait("vector", "v", vm)
    vm = p.op("vector", lambda eng: eng.scalar_tensor_tensor(
        var[:], mrow[:], -1.0, rho[:], ALU.mult, ALU.mult))
    vm = p.op("vector", lambda eng: eng.scalar_tensor_tensor(
        mrow[:], mrow[:], -1.0, rstd[:], ALU.mult, ALU.mult))
    return vm


def emit_ln_rows_plain3(p, ps, scr, rstd, shp, eps1_ap):
    vm = p.op("vector", lambda eng: eng.tensor_scalar(shp[:], ps[0:1, 0:T],
                                                      1.0 / D, None, ALU.mult))
    p.wait("scalar", "v", vm)
    sm = p.op("scalar", lambda eng: eng.activation(scr[:], shp[:], AF.Square))
    p.wait("vector", "s", sm)
    vm = p.op("vector", lambda eng: eng.scalar_tensor_tensor(
        scr[:], ps[0:1, T:2 * T], 1.0 / D, scr[:], ALU.mult, ALU.subtract))
    p.wait("scalar", "v", vm)
    sm = p.op("scalar", lambda eng: eng.activation(scr[:], scr[:], AF.Sqrt,
                                                   bias=eps1_ap, scale=1.0))
    p.wait("vector", "s", sm)
    vm = p.op("vector", lambda eng: eng.reciprocal(rstd[:], scr[:]))
    p.wait("vector", "v", vm)
    vm = p.op("vector", lambda eng: eng.scalar_tensor_tensor(
        shp[:], shp[:], -1.0, rstd[:], ALU.mult, ALU.mult))
    return vm


def emit_row_hilo(p, row, dst):
    return p.op("vector", lambda eng: eng.tensor_copy(dst[0:1, 0:T], row[0:1, 0:T]))


def emit_bcast_apply(p, ps, w_row_slice, scale_row, shift_row, xin, out_tile,
                     pre_wait=(), rows_b=None, ones_b=None):
    if rows_b is not None:
        sc, sh = rows_b
        for tb in range(T // 512):
            for src, off in ((sc, 0), (sh, T)):
                fn = (lambda eng, tb=tb, src=src, off=off:
                      eng.matmul(ps[:, off + tb * 512:off + (tb + 1) * 512],
                                 ones_b, src[0:1, tb * 512:(tb + 1) * 512],
                                 start=True, stop=True, skip_group_check=True))
                if tb == T // 512 - 1 and off == T:
                    p.op("tensor", fn, sem="t")
                else:
                    p.opn("tensor", fn)
    else:
        for tb in range(T // 512):
            p.opn("tensor", lambda eng, tb=tb:
                  eng.matmul(ps[:, tb * 512:(tb + 1) * 512], w_row_slice,
                             scale_row[0:1, tb * 512:(tb + 1) * 512],
                             start=True, stop=True, skip_group_check=True))
            if tb < T // 512 - 1:
                p.opn("tensor", lambda eng, tb=tb:
                      eng.matmul(ps[:, T + tb * 512:T + (tb + 1) * 512], w_row_slice,
                                 shift_row[0:1, tb * 512:(tb + 1) * 512],
                                 start=True, stop=True, skip_group_check=True))
            else:
                p.op("tensor", lambda eng, tb=tb:
                     eng.matmul(ps[:, T + tb * 512:T + (tb + 1) * 512], w_row_slice,
                                shift_row[0:1, tb * 512:(tb + 1) * 512],
                                start=True, stop=True, skip_group_check=True), sem="t")
    tm = p.mark("t")
    p.wait("vector", "t", tm)
    for sem, val in pre_wait:
        p.wait("vector", sem, val)
    p.op("vector", lambda eng: eng.tensor_mul(out_tile[:], xin[:], ps[:, 0:T]))
    vm = p.op("vector", lambda eng: eng.tensor_add(out_tile[:], out_tile[:],
                                                   ps[:, T:2 * T]))
    p.wait("tensor", "v", vm)
    return vm


def emit_load_w_bf16(p, hbm_ap, kdim, ndim, stage, out_tiles, stage_guard=0):
    vm = stage_guard
    for kc in range(kdim // 128):
        if vm:
            p.wait("sync", "v", vm)
        lm = p.dma("sync", stage[:, 0:ndim], hbm_ap[kc * 128:(kc + 1) * 128, :])
        p.wait("vector", "qs", lm)
        vm = p.op("vector", lambda eng, kc=kc: eng.tensor_copy(
            out_tiles[kc][:], stage[:, 0:ndim]))
    return vm


def emit_load_xbf(p, xT, stage, xb):
    vm = 0
    for kc in range(DC):
        if vm:
            p.wait("sync", "v", vm)
        lm = p.dma("sync", stage[:, 0:T], xT[kc * 128:(kc + 1) * 128, :])
        p.wait("vector", "qs", lm)
        vm = p.op("vector", lambda eng, kc=kc: eng.tensor_copy(xb[kc][:],
                                                               stage[:, 0:T]))
    return vm



def build_LA(dbg=False):
    p = Prog()
    xT = p.dram("xT", [D, T], F32, "ExternalInput").ap()
    wg_h = p.dram("Wg", [D, D], F32, "ExternalInput").ap()
    wv_h = p.dram("Wv", [D, D], F32, "ExternalInput").ap()
    wd_h = p.dram("Wd", [D, D], F32, "ExternalInput").ap()
    wmix_h = p.dram("wmix_row", [1, D], F32, "ExternalInput").ap()
    bias3_h = p.dram("bias3", [3 * DC, 128], F32, "ExternalInput").ap()
    xhat_o = p.dram("xhat", [D, T], BF16, "ExternalOutput").ap()
    h_o = p.dram("h", [D, T], BF16, "ExternalOutput").ap()
    A_o = p.dram("Acum", [D, T], BF16, "ExternalOutput").ap()

    xb = [p.sbuf(f"xb{i}", [128, T], BF16) for i in range(DC)]
    xh = p.sbuf("xh", [128, T], BF16)
    stage = p.sbuf("stage", [128, T], F32)
    wbf = {w: [p.sbuf(f"w{w}{i}", [128, D], BF16) for i in range(DC)]
           for w in "gvd"}
    sq = p.sbuf("sq", [128, T], BF16)
    ones_col = p.sbuf("ones_col", [128, 1], BF16)
    ones_row = p.sbuf("ones_row", [1, 128], F32)
    w_row = p.sbuf("w_row", [1, D], F32)
    bias_pp = p.sbuf("bias_pp", [128, 3 * DC], F32)
    R = {k: p.sbuf("row_" + k, [1, T], F32)
         for k in ("mean", "var", "rstd", "rho", "tmp")}
    eps1 = p.sbuf("eps1", [1, 1], F32)
    eps2 = p.sbuf("eps2", [1, 1], F32)
    sg = p.sbuf("sg", [128, T], BF16)
    tv = p.sbuf("tv", [128, T], BF16)
    a_f = p.sbuf("a_f", [128, T], F32)
    xs_f = p.sbuf("xs_f", [128, T], F32)
    hA = p.sbuf("hA", [128, 2 * T], BF16)
    ps = p.psum("ps", [128, 4096], F32)

    p.op("vector", lambda eng: eng.memset(ones_col[:], 1.0))
    p.op("vector", lambda eng: eng.memset(ones_row[:], 1.0))
    p.op("vector", lambda eng: eng.memset(eps1[:], 1e-5))
    p.op("vector", lambda eng: eng.memset(eps2[:], 1e-6))
    p.dma("sync", w_row[:], wmix_h)
    for r in range(3 * DC):
        p.dma("sync", bias_pp[:, r:r + 1],
              bias3_h[r:r + 1, :].rearrange("o d -> d o"))
    ldv = p.mark("qs")

    xb_done = emit_load_xbf(p, xT, stage, xb)
    if dbg:
        dbg_o = p.dram("dbg", [6, T], F32, "ExternalOutput").ap()
        dbg_xb = p.dram("dbg_xb", [D, T], BF16, "ExternalOutput").ap()
        p.wait("gpsimd", "v", xb_done)
        for kc in range(DC):
            p.dma("gpsimd", dbg_xb[kc * 128:(kc + 1) * 128, :], xb[kc][:])

    p.wait("scalar", "v", xb_done)
    p.wait("scalar", "qs", ldv)
    tm = emit_ln_stats(p, ps, xb, sq, ones_col)
    p.wait("vector", "t", tm)
    if dbg:
        p.wait("vector", "t", tm)
        sc = p.sbuf("dbg_sq", [1, 2 * T], F32)
        dvm = p.op("vector", lambda eng: eng.tensor_copy(sc[:], ps[0:1, 0:2 * T]))
        p.wait("gpsimd", "v", dvm)
        p.dma("gpsimd", dbg_o[0:1, :], sc[0:1, 0:T])
        p.dma("gpsimd", dbg_o[1:2, :], sc[0:1, T:2 * T])
    rowm = emit_ln_rows(p, ps, R, eps1[:], eps2[:], combine_rms=True)
    p.wait("tensor", "v", rowm)
    if dbg:
        p.wait("gpsimd", "v", rowm)
        p.dma("gpsimd", dbg_o[2:3, :], R["rstd"][:])
        p.dma("gpsimd", dbg_o[3:4, :], R["tmp"][:])
        p.dma("gpsimd", dbg_o[4:5, :], R["rho"][:])
        dmm = p.dma("gpsimd", dbg_o[5:6, :], R["var"][:])
        p.wait("tensor", "qg", dmm)

    dm_x = 0
    for dc in range(DC):
        vm = emit_bcast_apply(p, ps, ones_row[0:1, 0:128], R["rstd"], R["tmp"],
                              xb[dc], xh, pre_wait=(("qg", dm_x),))
        p.wait("gpsimd", "v", vm)
        dm_x = p.dma("gpsimd", xhat_o[dc * 128:(dc + 1) * 128, :], xh[:])
        emit_bcast_apply(p, ps, w_row[0:1, dc * 128:(dc + 1) * 128],
                         R["rho"], R["var"], xb[dc], xb[dc])
    xn_done = p.mark("v")

    wm = xb_done
    for w, hbm in zip("gvd", (wg_h, wv_h, wd_h)):
        wm = emit_load_w_bf16(p, hbm, D, D, stage, wbf[w], stage_guard=wm)
    p.wait("tensor", "v", wm)

    def proj(w, dc, b0):
        for kc in range(DC):
            for tb in range(T // 512):
                p.op("tensor", lambda eng, kc=kc, tb=tb, dc=dc, b0=b0, w=w:
                     eng.matmul(ps[:, (b0 + tb) * 512:(b0 + tb + 1) * 512],
                                wbf[w][kc][:, dc * 128:(dc + 1) * 128],
                                xb[kc][:, tb * 512:(tb + 1) * 512],
                                start=(kc == 0), stop=(kc == DC - 1)), sem="t")
        return p.mark("t")

    for dc in range(DC):
        tm = proj("g", dc, 0)
        p.wait("scalar", "t", tm)
        smg = p.op("scalar", lambda eng, dc=dc: eng.activation(
            sg[:], ps[:, 0:T], AF.Sigmoid, bias=bias_pp[:, dc:dc + 1], scale=1.0))
        tm = proj("v", dc, 4)
        p.wait("scalar", "t", tm)
        smv = p.op("scalar", lambda eng, dc=dc: eng.activation(
            tv[:], ps[:, T:2 * T], AF.Tanh,
            bias=bias_pp[:, DC + dc:DC + dc + 1], scale=1.0))
        p.wait("tensor", "s", smg)
        tm = proj("d", dc, 0)
        p.wait("scalar", "t", tm)
        smd = p.op("scalar", lambda eng, dc=dc: eng.activation(
            stage[:, 0:T], ps[:, 0:T], AF.Sigmoid,
            bias=bias_pp[:, 2 * DC + dc:2 * DC + dc + 1], scale=1.0))
        p.wait("tensor", "s", smd)
        p.wait("tensor", "s", smv)
        p.wait("vector", "s", smd)
        p.op("vector", lambda eng: eng.tensor_scalar(
            a_f[:], stage[:, 0:T], 0.998, 0.001, ALU.mult, ALU.add))
        p.op("vector", lambda eng: eng.tensor_mul(xs_f[:], sg[:], tv[:]))
        p.op("vector", lambda eng: eng.tensor_tensor_scan(
            hA[:, 0:T], a_f[:], xs_f[:], 0.0, ALU.mult, ALU.add))
        vm = p.op("vector", lambda eng: eng.tensor_tensor_scan(
            hA[:, T:2 * T], a_f[:], a_f[:], 1.0, ALU.mult, ALU.bypass))
        p.wait("gpsimd", "v", vm)
        p.dma("gpsimd", h_o[dc * 128:(dc + 1) * 128, :], hA[:, 0:T])
        dm = p.dma("gpsimd", A_o[dc * 128:(dc + 1) * 128, :], hA[:, T:2 * T])
        p.wait("vector", "qg", dm)
        p.wait("scalar", "qg", dm)
    p.wait("gpsimd", "qg", p.mark("qg"))
    return p.finalize()



def build_LB():
    GRP = 6
    NGRP = FF // 128 // GRP
    p = Prog()
    xT = p.dram("xT", [D, T], F32, "ExternalInput").ap()
    xhat_h = p.dram("xhat", [D, T], BF16, "ExternalInput").ap()
    h_h = p.dram("h", [D, T], BF16, "ExternalInput").ap()
    A_h = p.dram("Acum", [D, T], BF16, "ExternalInput").ap()
    carry_h = p.dram("carry", [128, DC], F32, "ExternalInput").ap()
    wffn_h = p.dram("wffn_row", [1, D], F32, "ExternalInput").ap()
    wgate_h = p.dram("W_gate", [D, FF], F32, "ExternalInput").ap()
    wup_h = p.dram("W_up", [D, FF], F32, "ExternalInput").ap()
    wout_h = p.dram("W_out", [FF, D], F32, "ExternalInput").ap()
    x1_o = p.dram("x1", [D, T], BF16, "ExternalOutput").ap()

    xb = [p.sbuf(f"xb{i}", [128, T], BF16) for i in range(DC)]
    xm = [p.sbuf(f"xm{i}", [128, T], BF16) for i in range(DC)]
    yn = [p.sbuf(f"yn{i}", [128, T], BF16) for i in range(DC)]
    da = [p.sbuf(f"da{i}", [128, T], BF16) for i in range(DC)]
    mt = [p.sbuf(f"mt{i}", [128, T], BF16) for i in range(GRP)]
    wo_sl = [p.sbuf(f"wo{i}", [128, D], BF16) for i in range(GRP)]
    wg_sl = p.sbuf("wg_sl", [128, D], BF16)
    wu_sl = p.sbuf("wu_sl", [128, D], BF16)
    stage = p.sbuf("stage", [128, T], F32)
    tmpb = p.sbuf("tmpb", [128, T], BF16)
    sq = p.sbuf("sq", [128, T], BF16)
    ones_col = p.sbuf("ones_col", [128, 1], BF16)
    w_row = p.sbuf("w_row", [1, D], F32)
    carry = p.sbuf("carry_sb", [128, DC], F32)
    R = {k: p.sbuf("row_" + k, [1, T], F32) for k in ("q", "rr")}
    eps2 = p.sbuf("eps2", [1, 1], F32)
    ps = p.psum("ps", [128, 4096], F32)

    p.op("vector", lambda eng: eng.memset(ones_col[:], 1.0))
    p.op("vector", lambda eng: eng.memset(eps2[:], 1e-6))
    p.dma("sync", w_row[:], wffn_h)
    p.dma("sync", carry[:], carry_h)
    ldv = p.mark("qs")

    for dc in range(DC):
        l1 = p.dma("sync", xm[dc][:], h_h[dc * 128:(dc + 1) * 128, :])
        l2 = p.dma("sync", tmpb[:], A_h[dc * 128:(dc + 1) * 128, :])
        p.wait("vector", "qs", max(l1, l2, ldv))
        vm = p.op("vector", lambda eng, dc=dc: eng.scalar_tensor_tensor(
            xm[dc][:], tmpb[:], carry[:, dc:dc + 1], xm[dc][:], ALU.mult, ALU.add))
        p.wait("sync", "v", vm)
        l3 = p.dma("sync", tmpb[:], xhat_h[dc * 128:(dc + 1) * 128, :])
        p.wait("vector", "qs", l3)
        vm = p.op("vector", lambda eng, dc=dc: eng.tensor_add(
            xm[dc][:], xm[dc][:], tmpb[:]))
        p.wait("sync", "v", vm)
    xm_done = p.mark("v")

    xb_done = emit_load_xbf(p, xT, stage, xb)

    p.wait("scalar", "v", xm_done)
    for kc in range(DC):
        sm = p.op("scalar", lambda eng, kc=kc: eng.activation(sq[:], xm[kc][:], AF.Square))
        p.wait("tensor", "s", sm)
        for tb in range(T // 512):
            p.op("tensor", lambda eng, kc=kc, tb=tb:
                 eng.matmul(ps[0:1, tb * 512:(tb + 1) * 512], ones_col[:],
                            sq[:, tb * 512:(tb + 1) * 512],
                            start=(kc == 0), stop=(kc == DC - 1)), sem="t")
        p.wait("scalar", "t", p.mark("t"))
    tm = p.mark("t")
    p.wait("vector", "t", tm)
    vm = p.op("vector", lambda eng: eng.tensor_scalar(
        R["q"][:], ps[0:1, 0:T], 1.0 / D, None, ALU.mult))
    p.wait("scalar", "v", vm)
    sm = p.op("scalar", lambda eng: eng.activation(R["q"][:], R["q"][:], AF.Sqrt,
                                                   bias=eps2[:], scale=1.0))
    p.wait("vector", "s", sm)
    rowm = p.op("vector", lambda eng: eng.reciprocal(R["rr"][:], R["q"][:]))
    p.wait("tensor", "v", rowm)

    for dc in range(DC):
        for tb in range(T // 512):
            p.op("tensor", lambda eng, dc=dc, tb=tb:
                 eng.matmul(ps[:, tb * 512:(tb + 1) * 512],
                            w_row[0:1, dc * 128:(dc + 1) * 128],
                            R["rr"][0:1, tb * 512:(tb + 1) * 512],
                            start=True, stop=True, skip_group_check=True), sem="t")
        tm = p.mark("t")
        p.wait("vector", "t", tm)
        vm = p.op("vector", lambda eng, dc=dc: eng.tensor_mul(
            yn[dc][:], xm[dc][:], ps[:, 0:T]))
        p.wait("tensor", "v", vm)

    p.wait("sync", "v", p.mark("v"))
    for grp in range(NGRP):
        for gi in range(GRP):
            fc = grp * GRP + gi
            lm = p.dma("sync", stage[:, 0:D].rearrange("p (a n) -> p a n", n=128),
                       wgate_h[:, fc * 128:(fc + 1) * 128]
                       .rearrange("(a p) n -> p a n", p=128))
            p.wait("vector", "qs", lm)
            vg = p.op("vector", lambda eng: eng.tensor_copy(wg_sl[:], stage[:, 0:D]))
            p.wait("sync", "v", vg)
            lm = p.dma("sync", stage[:, 0:D].rearrange("p (a n) -> p a n", n=128),
                       wup_h[:, fc * 128:(fc + 1) * 128]
                       .rearrange("(a p) n -> p a n", p=128))
            p.wait("vector", "qs", lm)
            vu = p.op("vector", lambda eng: eng.tensor_copy(wu_sl[:], stage[:, 0:D]))
            p.wait("sync", "v", vu)
            lm = p.dma("sync", stage[:, 0:D], wout_h[fc * 128:(fc + 1) * 128, :])
            p.wait("vector", "qs", lm)
            vo = p.op("vector", lambda eng, gi=gi: eng.tensor_copy(
                wo_sl[gi][:], stage[:, 0:D]))
            p.wait("sync", "v", vo)
            p.wait("tensor", "v", vg)
            for kc in range(DC):
                for tb in range(T // 512):
                    p.op("tensor", lambda eng, kc=kc, tb=tb:
                         eng.matmul(ps[:, tb * 512:(tb + 1) * 512],
                                    wg_sl[:, kc * 128:(kc + 1) * 128],
                                    yn[kc][:, tb * 512:(tb + 1) * 512],
                                    start=(kc == 0), stop=(kc == DC - 1)), sem="t")
            tm = p.mark("t")
            p.wait("scalar", "t", tm)
            sm = p.op("scalar", lambda eng: eng.activation(tmpb[:], ps[:, 0:T], AF.Silu))
            p.wait("tensor", "s", sm)
            p.wait("tensor", "v", vu)
            for kc in range(DC):
                for tb in range(T // 512):
                    p.op("tensor", lambda eng, kc=kc, tb=tb:
                         eng.matmul(ps[:, T + tb * 512:T + (tb + 1) * 512],
                                    wu_sl[:, kc * 128:(kc + 1) * 128],
                                    yn[kc][:, tb * 512:(tb + 1) * 512],
                                    start=(kc == 0), stop=(kc == DC - 1)), sem="t")
            tm = p.mark("t")
            p.wait("vector", "t", tm)
            p.wait("vector", "s", sm)
            vm = p.op("vector", lambda eng, gi=gi: eng.tensor_mul(
                mt[gi][:], tmpb[:], ps[:, T:2 * T]))
            p.wait("tensor", "v", vm)
            p.wait("scalar", "v", vm)
        mdone = p.mark("v")
        p.wait("tensor", "v", mdone)
        for dc in range(DC):
            for gi in range(GRP):
                for tb in range(T // 512):
                    fn = (lambda eng, gi=gi, tb=tb, dc=dc:
                          eng.matmul(ps[:, tb * 512:(tb + 1) * 512],
                                     wo_sl[gi][:, dc * 128:(dc + 1) * 128],
                                     mt[gi][:, tb * 512:(tb + 1) * 512],
                                     start=(gi == 0), stop=(gi == GRP - 1)))
                    if gi == GRP - 1 and tb == T // 512 - 1:
                        p.op("tensor", fn, sem="t")
                    else:
                        p.opn("tensor", fn)
            tm = p.mark("t")
            p.wait("vector", "t", tm)
            if grp == 0:
                vm = p.op("vector", lambda eng, dc=dc: eng.tensor_copy(
                    da[dc][:], ps[:, 0:T]))
            else:
                vm = p.op("vector", lambda eng, dc=dc: eng.tensor_add(
                    da[dc][:], da[dc][:], ps[:, 0:T]))
            p.wait("tensor", "v", vm)
    p.wait("vector", "v", p.mark("v"))
    dm = 0
    for dc in range(DC):
        p.op("vector", lambda eng, dc=dc: eng.tensor_add(
            da[dc][:], da[dc][:], xm[dc][:]))
        vm = p.op("vector", lambda eng, dc=dc: eng.tensor_add(
            da[dc][:], da[dc][:], xb[dc][:]))
        p.wait("gpsimd", "v", vm)
        dm = p.dma("gpsimd", x1_o[dc * 128:(dc + 1) * 128, :], da[dc][:])
    p.wait("gpsimd", "qg", p.mark("qg"))
    return p.finalize()



def build_LC(dbg=False):
    p = Prog()
    x1_h = p.dram("x1", [D, T], BF16, "ExternalInput").ap()
    pcol_h = p.dram("pcol", [128, DC], BF16, "ExternalInput").ap()
    wrT_h = p.dram("W_rT", [D, D], F32, "ExternalInput").ap()
    wkT_h = p.dram("W_kT", [D, D], F32, "ExternalInput").ap()
    wvT_h = p.dram("W_vT", [D, D], F32, "ExternalInput").ap()
    mu2_h = p.dram("mu2", [2 * DC, 128], F32, "ExternalInput").ap()
    x2_o = p.dram("x2", [D, T], BF16, "ExternalOutput").ap()

    xb = [p.sbuf(f"xb{i}", [128, T], BF16) for i in range(DC)]
    xr = [p.sbuf(f"xr{i}", [128, T], BF16) for i in range(DC)]
    xk = [p.sbuf(f"xk{i}", [128, T], BF16) for i in range(DC)]
    rt = [p.sbuf(f"rt{i}", [128, T], BF16) for i in range(DC)]
    ka = [p.sbuf(f"ka{i}", [128, T], BF16) for i in range(DC)]
    xh2 = p.sbuf("xh2", [128, T], BF16)
    wbf = [p.sbuf(f"wbf{i}", [128, D], BF16) for i in range(DC)]
    stage = p.sbuf("stage", [128, D], F32)
    sq = None
    ones_col = p.sbuf("ones_col", [128, 1], BF16)
    ones_row = p.sbuf("ones_row", [1, 128], BF16)
    ones_rowf = p.sbuf("ones_rowf", [1, 128], F32)
    mu_pp = p.sbuf("mu_pp", [128, 4 * DC], F32)
    pcol = p.sbuf("pcol_sb", [128, DC], BF16)
    pchat = p.sbuf("pchat", [128, DC], BF16)
    prow = p.sbuf("prow", [1, 2 * DC], F32)
    pval = p.sbuf("pval", [1, 4], F32)
    ppb = p.sbuf("ppb", [128, 2], F32)
    R = {k: p.sbuf("row_" + k, [1, T], BF16) for k in ("mean", "rstd", "tmp")}
    eps1 = p.sbuf("eps1", [1, 1], F32)
    ps = p.psum("ps", [128, 4096], F32)

    p.op("vector", lambda eng: eng.memset(ones_col[:], 1.0))
    p.op("vector", lambda eng: eng.memset(ones_row[:], 1.0))
    p.op("vector", lambda eng: eng.memset(ones_rowf[:], 1.0))
    p.op("vector", lambda eng: eng.memset(eps1[:], 1e-5))
    for r in range(DC):
        p.dma("sync", mu_pp[:, r:r + 1], mu2_h[r:r + 1, :].rearrange("o d -> d o"))
        p.dma("sync", mu_pp[:, 2 * DC + r:2 * DC + r + 1],
              mu2_h[DC + r:DC + r + 1, :].rearrange("o d -> d o"))
    lm = p.dma("sync", pcol[:], pcol_h)
    p.wait("vector", "qs", lm)
    p.op("vector", lambda eng: eng.tensor_scalar(
        mu_pp[:, DC:2 * DC], mu_pp[:, 0:DC], -1.0, 1.0, ALU.mult, ALU.add))
    p.op("vector", lambda eng: eng.tensor_scalar(
        mu_pp[:, 3 * DC:4 * DC], mu_pp[:, 2 * DC:3 * DC], -1.0, 1.0, ALU.mult, ALU.add))

    for dc in range(DC):
        p.dma("sync", xb[dc][:], x1_h[dc * 128:(dc + 1) * 128, :])
    x_ld = p.mark("qs")
    p.wait("scalar", "qs", x_ld)
    p.wait("vector", "qs", x_ld)

    tm = emit_ln_stats(p, ps, xb, xh2, ones_col)
    p.wait("vector", "t", tm)
    rowm = emit_ln_rows_plain(p, ps, R["mean"], R["rstd"], R["tmp"], eps1[:])

    sm = p.op("scalar", lambda eng: eng.activation(xh2[:, 0:DC], pcol[:], AF.Square))
    p.wait("tensor", "s", sm)
    p.wait("tensor", "qs", lm)
    p.wait("tensor", "v", rowm)
    p.op("tensor", lambda eng: eng.matmul(ps[0:1, 0:DC], ones_col[:],
                                          pcol[:], start=True, stop=True,
                                          skip_group_check=True), sem="t")
    tm2 = p.op("tensor", lambda eng: eng.matmul(ps[0:1, DC:2 * DC],
                                                ones_col[:], xh2[:, 0:DC],
                                                start=True, stop=True,
                                                skip_group_check=True), sem="t")
    p.wait("vector", "t", tm2)
    vm = p.op("vector", lambda eng: eng.tensor_copy(prow[:], ps[0:1, 0:2 * DC]))
    p.wait("vector", "v", vm)
    vm = p.op("vector", lambda eng: eng.reduce_sum(pval[0:1, 0:1], prow[0:1, 0:DC], axis=AX.X))
    vm = p.op("vector", lambda eng: eng.reduce_sum(pval[0:1, 1:2], prow[0:1, DC:2 * DC], axis=AX.X))
    p.wait("vector", "v", vm)
    vm = p.op("vector", lambda eng: eng.tensor_scalar(
        pval[0:1, 2:3], pval[0:1, 0:1], 1.0 / D, None, ALU.mult))
    p.wait("scalar", "v", vm)
    sm = p.op("scalar", lambda eng: eng.activation(pval[0:1, 3:4], pval[0:1, 2:3], AF.Square))
    p.wait("vector", "s", sm)
    vm = p.op("vector", lambda eng: eng.scalar_tensor_tensor(
        pval[0:1, 3:4], pval[0:1, 1:2], 1.0 / D, pval[0:1, 3:4], ALU.mult, ALU.subtract))
    p.wait("scalar", "v", vm)
    sm = p.op("scalar", lambda eng: eng.activation(pval[0:1, 3:4], pval[0:1, 3:4],
                                                   AF.Sqrt, bias=eps1[:], scale=1.0))
    p.wait("vector", "s", sm)
    vm = p.op("vector", lambda eng: eng.reciprocal(pval[0:1, 3:4], pval[0:1, 3:4]))
    p.wait("vector", "v", vm)
    vm2 = p.op("vector", lambda eng: eng.scalar_tensor_tensor(
        pval[0:1, 2:3], pval[0:1, 2:3], -1.0, pval[0:1, 3:4], ALU.mult, ALU.mult))
    p.wait("tensor", "v", vm2)
    p.op("tensor", lambda eng: eng.matmul(ps[:, 0:1],
                                          ones_rowf[0:1, :], pval[0:1, 3:4],
                                          start=True, stop=True,
                                          skip_group_check=True), sem="t")
    tm2 = p.op("tensor", lambda eng: eng.matmul(ps[:, 1:2],
                                                ones_rowf[0:1, :], pval[0:1, 2:3],
                                                start=True, stop=True,
                                                skip_group_check=True), sem="t")
    p.wait("vector", "t", tm2)
    vm = p.op("vector", lambda eng: eng.tensor_copy(ppb[:], ps[:, 0:2]))
    p.wait("vector", "v", vm)
    vm = p.op("vector", lambda eng: eng.scalar_tensor_tensor(
        pchat[:], pcol[:], ppb[:, 0:1], ppb[:, 1:2].broadcast_to([128, DC]),
        ALU.mult, ALU.add))

    if dbg:
        dxh = p.dram("dbg_xh", [D, T], BF16, "ExternalOutput").ap()
        dxr = p.dram("dbg_xr", [D, T], BF16, "ExternalOutput").ap()
        dxk = p.dram("dbg_xk", [D, T], BF16, "ExternalOutput").ap()
        drt = p.dram("dbg_rt", [D, T], BF16, "ExternalOutput").ap()
        dka = p.dram("dbg_ka", [D, T], BF16, "ExternalOutput").ap()
    p.wait("tensor", "v", p.mark("v"))
    for dc in range(DC):
        avm = emit_bcast_apply(p, ps, ones_row[0:1, 0:128], R["rstd"], R["tmp"],
                               xb[dc], xh2)
        if dbg:
            p.wait("gpsimd", "v", avm)
            dmx = p.dma("gpsimd", dxh[dc * 128:(dc + 1) * 128, :], xh2[:])
            p.wait("vector", "qg", dmx)
        vm = p.op("vector", lambda eng, dc=dc: eng.tensor_scalar(
            xr[dc][:, 1:T], xh2[:, 0:T - 1], mu_pp[:, DC + dc:DC + dc + 1],
            None, ALU.mult))
        vm = p.op("vector", lambda eng, dc=dc: eng.tensor_scalar(
            xr[dc][:, 0:1], pchat[:, dc:dc + 1], mu_pp[:, DC + dc:DC + dc + 1],
            None, ALU.mult))
        vm = p.op("vector", lambda eng, dc=dc: eng.tensor_scalar(
            xk[dc][:, 1:T], xh2[:, 0:T - 1], mu_pp[:, 3 * DC + dc:3 * DC + dc + 1],
            None, ALU.mult))
        vm = p.op("vector", lambda eng, dc=dc: eng.tensor_scalar(
            xk[dc][:, 0:1], pchat[:, dc:dc + 1], mu_pp[:, 3 * DC + dc:3 * DC + dc + 1],
            None, ALU.mult))
        vm = p.op("vector", lambda eng, dc=dc: eng.scalar_tensor_tensor(
            xr[dc][:], xh2[:], mu_pp[:, dc:dc + 1], xr[dc][:], ALU.mult, ALU.add))
        vm = p.op("vector", lambda eng, dc=dc: eng.scalar_tensor_tensor(
            xk[dc][:], xh2[:], mu_pp[:, 2 * DC + dc:2 * DC + dc + 1], xk[dc][:],
            ALU.mult, ALU.add))
        p.wait("scalar", "v", vm)
        if dbg:
            p.wait("gpsimd", "v", vm)
            dmx = p.dma("gpsimd", dxr[dc * 128:(dc + 1) * 128, :], xr[dc][:])
            dmx = p.dma("gpsimd", dxk[dc * 128:(dc + 1) * 128, :], xk[dc][:])
    mix_done = p.mark("v")

    def proj_from(src, wtiles, dc, b0):
        for kc in range(DC):
            for tb in range(T // 512):
                p.op("tensor", lambda eng, kc=kc, tb=tb, dc=dc, b0=b0:
                     eng.matmul(ps[:, (b0 + tb) * 512:(b0 + tb + 1) * 512],
                                wtiles[kc][:, dc * 128:(dc + 1) * 128],
                                src[kc][:, tb * 512:(tb + 1) * 512],
                                start=(kc == 0), stop=(kc == DC - 1)), sem="t")
        return p.mark("t")

    wm = emit_load_w_bf16(p, wrT_h, D, D, stage, wbf, stage_guard=0)
    p.wait("tensor", "v", wm)
    for dc in range(DC):
        b0 = 0 if dc % 2 == 0 else 4
        tm = proj_from(xr, wbf, dc, b0)
        p.wait("scalar", "t", tm)
        sm = p.op("scalar", lambda eng, dc=dc, b0=b0: eng.activation(
            rt[dc][:], ps[:, b0 * 512:b0 * 512 + T], AF.Sigmoid))
        p.wait("tensor", "s", sm)
        if dbg:
            p.wait("gpsimd", "s", sm)
            p.dma("gpsimd", drt[dc * 128:(dc + 1) * 128, :], rt[dc][:])
    p.wait("vector", "t", p.mark("t"))
    wm = emit_load_w_bf16(p, wkT_h, D, D, stage, wbf, stage_guard=wm)
    p.wait("tensor", "v", wm)
    for dc in range(DC):
        b0 = 0 if dc % 2 == 0 else 4
        tm = proj_from(xk, wbf, dc, b0)
        p.wait("scalar", "t", tm)
        sm = p.op("scalar", lambda eng, dc=dc, b0=b0: eng.activation(
            xh2[:], ps[:, b0 * 512:b0 * 512 + T], AF.Relu))
        p.wait("scalar", "s", sm)
        sm = p.op("scalar", lambda eng, dc=dc: eng.activation(
            ka[dc][:], xh2[:], AF.Square))
        p.wait("tensor", "s", sm)
        if dbg:
            p.wait("gpsimd", "s", sm)
            p.dma("gpsimd", dka[dc * 128:(dc + 1) * 128, :], ka[dc][:])
    p.wait("vector", "t", p.mark("t"))
    wm = emit_load_w_bf16(p, wvT_h, D, D, stage, wbf, stage_guard=wm)
    p.wait("tensor", "v", wm)
    dm = 0
    for dc in range(DC):
        b0 = 0 if dc % 2 == 0 else 4
        tm = proj_from(ka, wbf, dc, b0)
        p.wait("vector", "t", tm)
        if dm:
            p.wait("vector", "qg", dm)
        vm = p.op("vector", lambda eng, dc=dc, b0=b0: eng.tensor_mul(
            xh2[:], rt[dc][:], ps[:, b0 * 512:b0 * 512 + T]))
        vm = p.op("vector", lambda eng, dc=dc: eng.tensor_add(
            xh2[:], xh2[:], xb[dc][:]))
        p.wait("tensor", "v", vm)
        p.wait("gpsimd", "v", vm)
        dm = p.dma("gpsimd", x2_o[dc * 128:(dc + 1) * 128, :], xh2[:])
    p.wait("gpsimd", "qg", p.mark("qg"))
    return p.finalize()



def build_LD():
    p = Prog()
    x2_h = p.dram("x2", [D, T], BF16, "ExternalInput").ap()
    wdkvT_h = p.dram("W_dkvT", [D, 128], F32, "ExternalInput").ap()
    xh3_o = p.dram("xhat3", [D, T], BF16, "ExternalOutput").ap()
    ckv_o = p.dram("ckv", [128, T], BF16, "ExternalOutput").ap()

    xb = [p.sbuf(f"xb{i}", [128, T], BF16) for i in range(DC)]
    x3 = [p.sbuf(f"x3{i}", [128, T], BF16) for i in range(DC)]
    wdkv = [p.sbuf(f"wdkv{i}", [128, 128], BF16) for i in range(DC)]
    stage = p.sbuf("stage", [128, 128], F32)
    sq = p.sbuf("sq", [128, T], BF16)
    ckv_sb = p.sbuf("ckv_sb", [128, T], BF16)
    ones_col = p.sbuf("ones_col", [128, 1], BF16)
    ones_row = p.sbuf("ones_row", [1, 128], F32)
    R = {k: p.sbuf("row_" + k, [1, T], F32) for k in ("mean", "rstd", "tmp")}
    eps1 = p.sbuf("eps1", [1, 1], F32)
    ps = p.psum("ps", [128, 4096], F32)

    p.op("vector", lambda eng: eng.memset(ones_col[:], 1.0))
    p.op("vector", lambda eng: eng.memset(ones_row[:], 1.0))
    p.op("vector", lambda eng: eng.memset(eps1[:], 1e-5))
    for dc in range(DC):
        p.dma("sync", xb[dc][:], x2_h[dc * 128:(dc + 1) * 128, :])
    ldm = p.mark("qs")
    wm = 0
    for kc in range(DC):
        if wm:
            p.wait("sync", "v", wm)
        lm = p.dma("sync", stage[:], wdkvT_h[kc * 128:(kc + 1) * 128, :])
        p.wait("vector", "qs", lm)
        wm = p.op("vector", lambda eng, kc=kc: eng.tensor_copy(wdkv[kc][:], stage[:]))
    p.wait("scalar", "qs", ldm)
    p.wait("scalar", "v", p.mark("v"))
    tm = emit_ln_stats(p, ps, xb, sq, ones_col)
    p.wait("vector", "t", tm)
    rowm = emit_ln_rows_plain(p, ps, R["mean"], R["rstd"], R["tmp"], eps1[:])
    p.wait("tensor", "v", rowm)
    dmx = 0
    for dc in range(DC):
        vm = emit_bcast_apply(p, ps, ones_row[0:1, 0:128], R["rstd"], R["tmp"],
                              xb[dc], x3[dc])
        p.wait("gpsimd", "v", vm)
        dmx = p.dma("gpsimd", xh3_o[dc * 128:(dc + 1) * 128, :], x3[dc][:])
    p.wait("tensor", "v", wm)
    for tb in range(T // 512):
        for kc in range(DC):
            p.op("tensor", lambda eng, kc=kc, tb=tb:
                 eng.matmul(ps[:, tb * 512:(tb + 1) * 512], wdkv[kc][:],
                            x3[kc][:, tb * 512:(tb + 1) * 512],
                            start=(kc == 0), stop=(kc == DC - 1)), sem="t")
    tm = p.mark("t")
    p.wait("vector", "t", tm)
    vm = p.op("vector", lambda eng: eng.tensor_copy(ckv_sb[:], ps[:, 0:T]))
    p.wait("gpsimd", "v", vm)
    p.dma("gpsimd", ckv_o, ckv_sb[:])
    p.wait("gpsimd", "qg", p.mark("qg"))
    return p.finalize()



TT = 4096
NKT = TT // 128
NQB = TT // 512

def build_LE(dbg=False):
    p = Prog()
    x3_h = p.dram("xhat3", [D, TT], BF16, "ExternalInput").ap()
    ckv_h = p.dram("ckv", [128, TT], BF16, "ExternalInput").ap()
    wdq_h = p.dram("W_dq", [128, D], F32, "ExternalInput").ap()
    wuqT_h = p.dram("W_uqT_p", [128, D], F32, "ExternalInput").ap()
    wdqp_h = p.dram("W_dq_p", [128, D], F32, "ExternalInput").ap()
    wukp_h = p.dram("W_uk_p", [D, 128], F32, "ExternalInput").ap()
    wuv_h = p.dram("W_uv", [D, 128], F32, "ExternalInput").ap()
    woTp_h = p.dram("W_oT_p", [D, D], F32, "ExternalInput").ap()
    ident_h = p.dram("ident", [128, 128], BF16, "ExternalInput").ap()
    identf_h = p.dram("identf", [128, 128], F32, "ExternalInput").ap()
    mask4_h = p.dram("mask4", [4, 128, 512], F32, "ExternalInput").ap()
    y_o = p.dram("y", [512, TT], BF16, "ExternalOutput").ap()
    rz_o = p.dram("rz", [8, 32, 128], BF16, "ExternalOutput").ap()

    x3f = [p.sbuf(f"x3f{i}", [128, TT], BF16) for i in range(DC)]
    ckv_fm = p.sbuf("ckv_fm", [128, TT], BF16)
    ckv_tm = [p.sbuf(f"ckvtm{i}", [128, 128], BF16) for i in range(NKT)]
    mown = [p.sbuf(f"mown{i}", [128, D], BF16) for i in range(DC)]
    vown = p.sbuf("vown", [128, 512], BF16)
    at1 = [p.sbuf(f"at1_{i}", [128, D], BF16) for i in range(4)]
    at2 = [p.sbuf(f"at2_{i}", [128, 512], BF16) for i in range(DC)]
    kloc = [p.sbuf(f"kloc{i}", [128, 128], BF16) for i in range(4)]
    wtmp = p.sbuf("wtmp", [128, D], BF16)
    stage = p.sbuf("stage", [128, D], F32)
    ident = p.sbuf("ident_sb", [128, 128], BF16)
    identf = p.sbuf("identf_sb", [128, 128], F32)
    mask4 = p.sbuf("mask4_sb", [128, 4 * 512], F32)
    pexp = p.sbuf("pexp", [128, 2048], BF16)
    pT = p.sbuf("pT", [128, 16 * 512], BF16)
    qlb = p.sbuf("qlb", [128, 512], BF16)
    sqq = p.sbuf("sqq", [128, 512], BF16)
    dql = p.sbuf("dql", [128, 512], BF16)
    stats = p.sbuf("stats", [128, 8], F32)
    numu = p.sbuf("numu", [128, 8], F32)
    r2pp = p.sbuf("r2pp", [128, 1], F32)
    zc = p.sbuf("zc", [128, 8 * 32], F32)
    zpart = p.sbuf("zpart", [128, 1], F32)
    ctx_sb = p.sbuf("ctx_sb", [128, 512], BF16)
    y_sb = p.sbuf("y_sb", [128, 512], BF16)
    rzt = p.sbuf("rzt", [32, 128], BF16)
    ones_col = p.sbuf("ones_col", [128, 1], BF16)
    sqrow = p.sbuf("sqrow", [128, 512], BF16)
    r2row = p.sbuf("r2row", [1, TT // 512], F32)
    ones_rowf = p.sbuf("ones_rowf", [1, 128], F32)
    ps = p.psum("ps", [128, 4096], F32)
    ptb = ps[:, 3584:4096].bitcast(BF16)

    p.op("vector", lambda eng: eng.memset(ones_col[:], 1.0))
    p.op("vector", lambda eng: eng.memset(ones_rowf[:], 1.0))
    p.op("vector", lambda eng: eng.memset(zc[:], 0.0))
    p.dma("sync", ident[:], ident_h)
    p.dma("sync", identf[:], identf_h)
    for i in range(4):
        p.dma("sync", mask4[:, i * 512:(i + 1) * 512], mask4_h[i])
    for dc in range(DC):
        p.dma("sync", x3f[dc][:], x3_h[dc * 128:(dc + 1) * 128, :])
    lm = p.dma("sync", ckv_fm[:], ckv_h)
    in_ld = p.mark("qs")

    def loadw(hbm, ncols, dst, guard):
        if guard:
            p.wait("sync", "v", guard)
        lm = p.dma("sync", stage[:, 0:ncols], hbm)
        p.wait("vector", "qs", lm)
        return p.op("vector", lambda eng: eng.tensor_copy(dst[:, 0:ncols],
                                                          stage[:, 0:ncols]))
    wuqm = loadw(wuqT_h, D, wtmp, 0)
    wdqm = 0
    wdq_sb = p.sbuf("wdq_sb", [128, D], BF16)
    wdqm = loadw(wdq_h, D, wdq_sb, wuqm)
    p.wait("tensor", "v", wdqm)
    for jc in range(4):
        for nb in range(2):
            p.op("tensor", lambda eng, jc=jc, nb=nb:
                 eng.matmul(ps[:, nb * 512:(nb + 1) * 512],
                            wtmp[:, jc * 128:(jc + 1) * 128],
                            wdq_sb[:, nb * 512:(nb + 1) * 512],
                            start=True, stop=True, skip_group_check=True), sem="t")
        tm = p.mark("t")
        p.wait("vector", "t", tm)
        vm = p.op("vector", lambda eng, jc=jc: eng.tensor_copy(at1[jc][:], ps[:, 0:1024]))
        p.wait("tensor", "v", vm)
    p.wait("vector", "t", p.mark("t"))
    wdqm = loadw(wdqp_h, D, wdq_sb, wdqm)
    p.wait("tensor", "v", wdqm)
    for jc in range(DC):
        p.op("tensor", lambda eng, jc=jc:
             eng.matmul(ps[:, 0:512], wtmp[:, jc * 128:(jc + 1) * 128],
                        wdq_sb[:, 0:512],
                        start=True, stop=True, skip_group_check=True), sem="t")
        tm = p.mark("t")
        p.wait("vector", "t", tm)
        vm = p.op("vector", lambda eng, jc=jc: eng.tensor_copy(at2[jc][:], ps[:, 0:512]))
        p.wait("tensor", "v", vm)
    wukm = wdqm
    wuk_sb = [p.sbuf(f"wuk{i}", [128, 128], BF16) for i in range(DC)]
    for jc in range(DC):
        wukm = loadw(wukp_h[jc * 128:(jc + 1) * 128, :], 128, wuk_sb[jc], wukm)
    p.wait("tensor", "v", wukm)
    for ds in range(4):
        for jc in range(DC):
            p.op("tensor", lambda eng, ds=ds, jc=jc:
                 eng.matmul(ps[:, 1024 + ds * 128:1024 + (ds + 1) * 128],
                            at2[jc][:, ds * 128:(ds + 1) * 128], wuk_sb[jc][:],
                            start=(jc == 0), stop=(jc == DC - 1)), sem="t")
    tm = p.mark("t")
    p.wait("vector", "t", tm)
    for ds in range(4):
        vm = p.op("vector", lambda eng, ds=ds: eng.tensor_copy(
            kloc[ds][:], ps[:, 1024 + ds * 128:1024 + (ds + 1) * 128]))
    p.wait("tensor", "v", p.mark("v"))
    for hh in range(8):
        tile, off = hh // 2, 64 * (hh % 2)
        for d1c in range(DC):
            p.op("tensor", lambda eng, hh=hh, tile=tile, off=off, d1c=d1c:
                 eng.matmul(ps[:, 2048 + (d1c % 4) * 128:2048 + (d1c % 4) * 128 + 128],
                            at1[tile][off:off + 64, d1c * 128:(d1c + 1) * 128],
                            kloc[tile][off:off + 64, :],
                            start=True, stop=True, skip_group_check=True), sem="t")
            tm = p.mark("t")
            p.wait("vector", "t", tm)
            vm = p.op("vector", lambda eng, hh=hh, d1c=d1c: eng.tensor_copy(
                mown[d1c][:, hh * 128:(hh + 1) * 128],
                ps[:, 2048 + (d1c % 4) * 128:2048 + (d1c % 4) * 128 + 128]))
            p.wait("tensor", "v", vm)
    p.wait("vector", "t", p.mark("t"))
    wuvm = wukm
    wuv_sb = p.sbuf("wuv_sb", [128, 128], BF16)
    for cc in range(DC):
        wuvm = loadw(wuv_h[cc * 128:(cc + 1) * 128, :], 128, wuv_sb, wuvm)
        guard = loadw(woTp_h[cc * 128:(cc + 1) * 128, 0:512], 512, wtmp, wuvm)
        p.wait("tensor", "v", guard)
        p.op("tensor", lambda eng, cc=cc:
             eng.matmul(ps[:, 3072:3584], wuv_sb[:], wtmp[:, 0:512],
                        start=(cc == 0), stop=(cc == DC - 1)), sem="t")
        wuvm = guard
        p.wait("vector", "t", p.mark("t"))
    tm = p.mark("t")
    p.wait("vector", "t", tm)
    vm = p.op("vector", lambda eng: eng.tensor_copy(vown[:], ps[:, 3072:3584]))
    p.wait("tensor", "v", vm)
    if dbg:
        d_at1 = p.dram("d_at1", [4, 128, D], BF16, "ExternalOutput").ap()
        d_at2 = p.dram("d_at2", [DC, 128, 512], BF16, "ExternalOutput").ap()
        d_kloc = p.dram("d_kloc", [4, 128, 128], BF16, "ExternalOutput").ap()
        d_mown = p.dram("d_mown", [D, D], BF16, "ExternalOutput").ap()
        d_vown = p.dram("d_vown", [128, 512], BF16, "ExternalOutput").ap()
        d_ql = p.dram("d_ql", [128, 512], BF16, "ExternalOutput").ap()
        d_stats = p.dram("d_stats", [128, 8], F32, "ExternalOutput").ap()
        d_numu = p.dram("d_numu", [128, 8], F32, "ExternalOutput").ap()
        d_pexp = p.dram("d_pexp", [128, 512], BF16, "ExternalOutput").ap()
        d_ctx = p.dram("d_ctx", [128, 512], BF16, "ExternalOutput").ap()
        p.wait("gpsimd", "v", p.mark("v"))
        for i in range(4):
            p.dma("gpsimd", d_at1[i], at1[i][:])
            p.dma("gpsimd", d_kloc[i], kloc[i][:])
        for i in range(DC):
            p.dma("gpsimd", d_at2[i], at2[i][:])
        for dc in range(DC):
            p.dma("gpsimd", d_mown[dc * 128:(dc + 1) * 128, :], mown[dc][:])
        p.dma("gpsimd", d_vown, vown[:])

    p.wait("tensor", "qs", in_ld)
    for kt in range(NKT):
        p.op("tensor", lambda eng, kt=kt:
             eng.matmul(ptb[:, (kt % 8) * 128:(kt % 8) * 128 + 128],
                        ckv_fm[:, kt * 128:(kt + 1) * 128], ident[:],
                        is_transpose=True, skip_group_check=True), sem="t")
        tm = p.mark("t")
        p.wait("vector", "t", tm)
        vm = p.op("vector", lambda eng, kt=kt: eng.tensor_copy(
            ckv_tm[kt][:], ptb[:, (kt % 8) * 128:(kt % 8) * 128 + 128]))
        p.wait("tensor", "v", vm)

    p.wait("scalar", "qs", in_ld)
    r2m = p.op("vector", lambda eng: eng.memset(r2row[:], 0.0))
    for tb in range(TT // 512):
        sm = p.op("scalar", lambda eng, tb=tb: eng.activation(
            sqrow[:], ckv_fm[:, tb * 512:(tb + 1) * 512], AF.Square))
        p.wait("tensor", "s", sm)
        tmx = p.op("tensor", lambda eng:
                   eng.matmul(ps[0:1, 2048:2560], ones_col[:], sqrow[:],
                              start=True, stop=True, skip_group_check=True),
                   sem="t")
        p.wait("vector", "t", tmx)
        p.wait("vector", "v", r2m)
        r2m = p.op("vector", lambda eng, tb=tb: eng.reduce_max(
            r2row[0:1, tb:tb + 1], ps[0:1, 2048:2560], axis=AX.X))
        p.wait("tensor", "v", r2m)
        p.wait("scalar", "t", p.mark("t"))
    p.wait("vector", "v", r2m)
    r2m = p.op("vector", lambda eng: eng.reduce_max(
        r2row[0:1, 0:1], r2row[0:1, 0:TT // 512], axis=AX.X))
    p.wait("tensor", "v", r2m)
    p.op("tensor", lambda eng: eng.matmul(
        ps[:, 2048:2049], ones_rowf[0:1, :], r2row[0:1, 0:1],
        start=True, stop=True, skip_group_check=True), sem="t")
    tm = p.mark("t")
    p.wait("vector", "t", tm)
    r2m = p.op("vector", lambda eng: eng.tensor_copy(r2pp[:], ps[:, 2048:2049]))
    p.wait("tensor", "v", r2m)

    dmy = 0
    for hh in range(8):
        for qb in range(NQB):
            KB = qb + 1
            for dc in range(DC):
                p.op("tensor", lambda eng, dc=dc, hh=hh, qb=qb:
                     eng.matmul(ps[:, 2048:2560],
                                mown[dc][:, hh * 128:(hh + 1) * 128],
                                x3f[dc][:, qb * 512:(qb + 1) * 512],
                                start=(dc == 0), stop=(dc == DC - 1)), sem="t")
            tm = p.mark("t")
            p.wait("vector", "t", tm)
            vq = p.op("vector", lambda eng: eng.tensor_copy(qlb[:], ps[:, 2048:2560]))
            p.wait("tensor", "v", vq)
            if dbg and hh == 0 and qb == 0:
                p.wait("gpsimd", "v", vq)
                p.dma("gpsimd", d_ql, qlb[:])
            p.wait("scalar", "v", vq)
            sm = p.op("scalar", lambda eng: eng.activation(sqq[:], qlb[:], AF.Square))
            vd = p.op("vector", lambda eng, qb=qb: eng.tensor_mul(
                dql[:], qlb[:], ckv_fm[:, qb * 512:(qb + 1) * 512]))
            p.wait("tensor", "s", sm)
            p.wait("tensor", "v", vd)
            for qt in range(4):
                p.op("tensor", lambda eng, qt=qt:
                     eng.matmul(ps[:, 2048 + qt:2048 + qt + 1],
                                sqq[:, qt * 128:(qt + 1) * 128], ones_col[:],
                                start=True, stop=True, skip_group_check=True), sem="t")
                p.op("tensor", lambda eng, qt=qt:
                     eng.matmul(ps[:, 2052 + qt:2052 + qt + 1],
                                dql[:, qt * 128:(qt + 1) * 128], ones_col[:],
                                start=True, stop=True, skip_group_check=True), sem="t")
            tm = p.mark("t")
            p.wait("vector", "t", tm)
            vm = p.op("vector", lambda eng: eng.tensor_copy(stats[:], ps[:, 2048:2056]))
            p.wait("tensor", "v", vm)
            p.wait("vector", "v", vm)
            vm = p.op("vector", lambda eng: eng.tensor_scalar(
                numu[:, 4:8], stats[:, 0:4], r2pp[:, 0:1], None, ALU.mult))
            p.wait("scalar", "v", vm)
            sm = p.op("scalar", lambda eng: eng.activation(
                numu[:, 4:8], numu[:, 4:8], AF.Sqrt))
            p.wait("vector", "s", sm)
            vm = p.op("vector", lambda eng: eng.tensor_scalar(
                numu[:, 4:8], numu[:, 4:8], -0.125, 40.0, ALU.mult, ALU.add))
            p.wait("vector", "v", vm)
            vm = p.op("vector", lambda eng: eng.scalar_tensor_tensor(
                numu[:, 0:4], stats[:, 4:8], -0.125, numu[:, 4:8],
                ALU.mult, ALU.min))
            numu_m = vm
            if dbg and hh == 0 and qb == 0:
                p.wait("gpsimd", "v", numu_m)
                p.dma("gpsimd", d_stats, stats[:])
                p.dma("gpsimd", d_numu, numu[:])

            nchunk = (KB + 3) // 4
            first_pv = True
            for ch in range(nchunk):
                kb0 = ch * 4
                nkb = min(4, KB - kb0)
                ncols = nkb * 512
                for qt in range(4):
                    p.wait("tensor", "s", p.mark("s"))
                    for kb in range(nkb):
                        p.op("tensor", lambda eng, qt=qt, kb=kb, kb0=kb0, qb=qb:
                             eng.matmul(ps[:, kb * 512:(kb + 1) * 512],
                                        qlb[:, qt * 128:(qt + 1) * 128],
                                        ckv_fm[:, (kb0 + kb) * 512:(kb0 + kb + 1) * 512],
                                        start=True, stop=True,
                                        skip_group_check=True), sem="t")
                    tm = p.mark("t")
                    if kb0 + nkb == KB:
                        p.wait("vector", "t", tm)
                        vm = p.op("vector", lambda eng, qt=qt, nkb=nkb:
                                  eng.tensor_add(ps[:, (nkb - 1) * 512:nkb * 512],
                                                 ps[:, (nkb - 1) * 512:nkb * 512],
                                                 mask4[:, qt * 512:(qt + 1) * 512]))
                        p.wait("scalar", "v", vm)
                    else:
                        p.wait("scalar", "t", tm)
                    p.wait("scalar", "v", numu_m)
                    sm = p.op("scalar", lambda eng, qt=qt, ncols=ncols:
                              eng.activation(pexp[:, 0:ncols], ps[:, 0:ncols],
                                             AF.Exp, bias=numu[:, qt:qt + 1],
                                             scale=0.125, accum_out=zpart[:]))
                    p.wait("tensor", "s", sm)
                    p.wait("vector", "s", sm)
                    qtg = qb * 4 + qt
                    if ch == 0:
                        vm = p.op("vector", lambda eng, hh=hh, qtg=qtg:
                                  eng.tensor_copy(zc[:, hh * 32 + qtg:hh * 32 + qtg + 1],
                                                  zpart[:]))
                    else:
                        vm = p.op("vector", lambda eng, hh=hh, qtg=qtg:
                                  eng.tensor_add(zc[:, hh * 32 + qtg:hh * 32 + qtg + 1],
                                                 zc[:, hh * 32 + qtg:hh * 32 + qtg + 1],
                                                 zpart[:]))
                    p.wait("scalar", "v", vm)
                    if dbg and hh == 0 and qb == 0 and ch == 0 and qt == 0:
                        p.wait("gpsimd", "s", sm)
                        p.dma("gpsimd", d_pexp, pexp[:, 0:512])
                    for half in range(2):
                        kt0 = half * 8
                        nkt = min(8, ncols // 128 - kt0)
                        if nkt <= 0:
                            continue
                        for kt in range(nkt):
                            p.op("tensor", lambda eng, kt=kt, kt0=kt0, qt=qt:
                                 eng.matmul(ptb[:, kt * 128:(kt + 1) * 128],
                                            pexp[:, (kt0 + kt) * 128:(kt0 + kt + 1) * 128],
                                            ident[:], is_transpose=True,
                                            skip_group_check=True), sem="t")
                        tm = p.mark("t")
                        p.wait("vector", "t", tm)
                        vm = p.op("vector", lambda eng, kt0=kt0, nkt=nkt, qt=qt:
                                  eng.tensor_copy(
                                      pT[:].rearrange("p (kt q) -> p kt q", q=512)
                                      [:, kt0:kt0 + nkt, qt * 128:(qt + 1) * 128],
                                      ptb[:, 0:nkt * 128].rearrange(
                                          "p (kt q) -> p kt q", q=128)))
                        p.wait("tensor", "v", vm)
                    ptm = p.mark("v")
                p.wait("tensor", "v", ptm)
                nkt_c = ncols // 128
                last_ch = (ch == nchunk - 1)
                for kt in range(nkt_c):
                    p.op("tensor", lambda eng, kt=kt, kb0=kb0, first=first_pv,
                         stop=(last_ch and kt == nkt_c - 1):
                         eng.matmul(ps[:, 2560:3072], ckv_tm[kb0 * 4 + kt][:],
                                    pT[:, kt * 512:(kt + 1) * 512],
                                    start=(first and kt == 0),
                                    stop=stop, skip_group_check=True), sem="t")
                first_pv = False
                tm = p.mark("t")
                p.wait("vector", "t", tm)
            tm = p.mark("t")
            p.wait("vector", "t", tm)
            vc = p.op("vector", lambda eng: eng.tensor_copy(ctx_sb[:], ps[:, 2560:3072]))
            p.wait("tensor", "v", vc)
            if dbg and hh == 0 and qb == 0:
                p.wait("gpsimd", "v", vc)
                dmc = p.dma("gpsimd", d_ctx, ctx_sb[:])
                p.wait("vector", "qg", dmc)
            off = 64 * (hh % 2)
            p.op("tensor", lambda eng, hh=hh, off=off:
                 eng.matmul(ps[off:off + 64, 3072:3584],
                            vown[:, hh * 64:(hh + 1) * 64], ctx_sb[:],
                            start=True, stop=True, skip_group_check=True), sem="t")
            tm = p.mark("t")
            p.wait("vector", "t", tm)
            if dmy:
                p.wait("vector", "qg", dmy)
            vy = p.op("vector", lambda eng, off=off: eng.tensor_copy(
                y_sb[off:off + 64, :], ps[off:off + 64, 3072:3584]))
            p.wait("tensor", "v", vy)
            p.wait("gpsimd", "v", vy)
            dmy = p.dma("gpsimd", y_o[hh * 64:(hh + 1) * 64, qb * 512:(qb + 1) * 512],
                        y_sb[off:off + 64, :])
    vm = p.op("vector", lambda eng: eng.reciprocal(zc[:], zc[:]))
    p.wait("tensor", "v", vm)
    for hh in range(8):
        p.op("tensor", lambda eng, hh=hh:
             eng.matmul(ps[0:32, 2048:2176], zc[:, hh * 32:(hh + 1) * 32],
                        identf[:], is_transpose=True,
                        skip_group_check=True), sem="t")
        tm = p.mark("t")
        p.wait("vector", "t", tm)
        vm = p.op("vector", lambda eng, hh=hh: eng.tensor_copy(
            rzt[:], ps[0:32, 2048:2176]))
        p.wait("gpsimd", "v", vm)
        dmz = p.dma("gpsimd", rz_o[hh], rzt[:])
        p.wait("tensor", "qg", dmz)
    p.wait("gpsimd", "qg", p.mark("qg"))
    return p.finalize()



def emit_quant(p, tile, W, am, inv, tq, eng="vector", src=None):
    nb = W // 32
    if src is None:
        src = tile
    vm = p.op("vector", lambda eng_: eng_.reduce_max(
        am[:, 0:nb], src[:, 0:W].rearrange("p (n k) -> p n k", k=32),
        axis=AX.X, apply_absolute_value=True))
    p.wait("vector", "v", vm)
    vm = p.op("vector", lambda eng_: eng_.tensor_scalar_max(
        am[:, 0:nb], am[:, 0:nb], 2.0 ** -30))
    p.wait("vector", "v", vm)
    vm = p.op("vector", lambda eng_: eng_.tensor_single_scalar(
        am[:, 0:nb].bitcast(I32), am[:, 0:nb].bitcast(I32),
        0x7F800000, ALU.bitwise_and))
    p.wait("vector", "v", vm)
    vm = p.op("vector", lambda eng_: eng_.reciprocal(inv[:, 0:nb], am[:, 0:nb]))
    p.wait("vector", "v", vm)
    vm = p.op("vector", lambda eng_: eng_.tensor_scalar_mul(
        inv[:, 0:nb], inv[:, 0:nb], 8.0))
    p.wait(eng, "v", vm)
    e0 = eng[0]
    vm = p.op(eng, lambda eng_: eng_.tensor_tensor(
        tq[:, 0:W].rearrange("p (n k) -> p n k", k=32),
        src[:, 0:W].rearrange("p (n k) -> p n k", k=32),
        inv[:, 0:nb].broadcast_to([128, nb, 32]), ALU.mult))
    p.wait(eng, e0, vm)
    vm = p.op(eng, lambda eng_: eng_.tensor_scalar(
        tq[:, 0:W], tq[:, 0:W], MAGIC, MAGIC, ALU.add, ALU.subtract))
    p.wait(eng, e0, vm)
    vm = p.op(eng, lambda eng_: eng_.scalar_tensor_tensor(
        tile[:, 0:W].rearrange("p (n k) -> p n k", k=32),
        tq[:, 0:W].rearrange("p (n k) -> p n k", k=32), 0.125,
        am[:, 0:nb].broadcast_to([128, nb, 32]), ALU.mult, ALU.mult))
    p.wait("vector", e0, vm)
    return vm



def build_LF(dbg=False):
    NT = T // 128
    p = Prog()
    x2_h = p.dram("x2", [D, T], BF16, "ExternalInput").ap()
    ym_h = p.dram("ym", [D, T], BF16, "ExternalInput").ap()
    rza_h = p.dram("rza", [16, T], BF16, "ExternalInput").ap()
    ind_h = p.dram("ind", [16, D], BF16, "ExternalInput").ap()
    wfc_h = p.dram("w_fc", [HID, D], F32, "ExternalInput").ap()
    bfc_h = p.dram("bfc_row", [1, HID], F32, "ExternalInput").ap()
    wproj_h = p.dram("w_proj", [D, HID], F32, "ExternalInput").ap()
    bproj_h = p.dram("bproj", [DC, 128], F32, "ExternalInput").ap()
    ident_h = p.dram("ident", [128, 128], BF16, "ExternalInput").ap()
    out_h = p.dram("out", [T, D], F32, "ExternalOutput").ap()

    x3 = [p.sbuf(f"x3_{i}", [128, T], BF16) for i in range(DC)]
    x3tm = [p.sbuf(f"x3tm{i}", [128, D], BF16) for i in range(NT)]
    xq = [p.sbuf(f"xq{i}", [128, T], BF16) for i in range(DC)]
    acc = [p.sbuf(f"acc{i}", [128, T], BF16) for i in range(DC)]
    hqT = [p.sbuf(f"hqT{i}", [128, T], BF16) for i in range(4)]
    wfcqT = [p.sbuf(f"wfcqT{i}", [128, 1024], BF16) for i in range(DC)]
    wpqT = [p.sbuf(f"wpqT{i}", [128, D], BF16) for i in range(8)]
    wq_c = p.sbuf("wq_c", [128, 1024], BF16)
    f32buf = p.sbuf("f32buf", [128, 1024], F32)
    stage = f32buf
    rza_sb = p.sbuf("rza_sb", [16, T], BF16)
    ind_sb = p.sbuf("ind_sb", [16, D], BF16)
    bfc_sb = p.sbuf("bfc_sb", [1, HID], BF16)
    bproj_pp = p.sbuf("bproj_pp", [128, DC], F32)
    ones1 = p.sbuf("ones1", [1, 128], BF16)
    ident = p.sbuf("ident_sb", [128, 128], BF16)
    am = p.sbuf("am_q", [128, 32], F32)
    inv = p.sbuf("inv_q", [128, 32], F32)
    tq = p.sbuf("tq_q", [128, 1024], F32)
    xhat = p.sbuf("xhat_t", [128, D], BF16)
    xhat32 = f32buf
    h32 = f32buf
    sq = f32buf
    ln4 = p.sbuf("ln4", [128, 6], F32)
    eps1 = p.sbuf("eps1", [128, 1], F32)
    ps = p.psum("ps", [128, 4096], F32)
    ptb = ps[:, 3584:4096].bitcast(BF16)
    ystage = tq[:].bitcast(BF16)

    p.op("vector", lambda eng: eng.memset(ones1[:], 1.0))
    p.op("vector", lambda eng: eng.memset(eps1[:], 1e-5))
    p.dma("sync", ident[:], ident_h)
    p.dma("sync", rza_sb[:], rza_h)
    p.dma("sync", ind_sb[:], ind_h)
    lm = p.dma("sync", stage[0:1, 0:1024], bfc_h[0:1, 0:1024])
    p.wait("vector", "qs", p.mark("qs"))
    vm = p.op("vector", lambda eng: eng.tensor_copy(bfc_sb[0:1, 0:1024],
                                                    stage[0:1, 0:1024]))
    for piece in range(1, 4):
        p.wait("sync", "v", vm)
        lm = p.dma("sync", stage[0:1, 0:1024], bfc_h[0:1, piece * 1024:(piece + 1) * 1024])
        p.wait("vector", "qs", lm)
        vm = p.op("vector", lambda eng, piece=piece: eng.tensor_copy(
            bfc_sb[0:1, piece * 1024:(piece + 1) * 1024], stage[0:1, 0:1024]))
    for r in range(DC):
        p.dma("sync", bproj_pp[:, r:r + 1], bproj_h[r:r + 1, :].rearrange("o d -> d o"))
    for dc in range(DC):
        p.dma("sync", x3[dc][:], x2_h[dc * 128:(dc + 1) * 128, :])
    ld0 = p.mark("qs")

    p.wait("tensor", "qs", ld0)
    dmy = 0
    for dc in range(DC):
        for tb in range(T // 512):
            p.op("tensor", lambda eng, dc=dc, tb=tb:
                 eng.matmul(ps[:, tb * 512:(tb + 1) * 512],
                            ind_sb[:, dc * 128:(dc + 1) * 128],
                            rza_sb[:, tb * 512:(tb + 1) * 512],
                            start=True, stop=True, skip_group_check=True), sem="t")
        tm = p.mark("t")
        lmy = p.dma("sync", ystage, ym_h[dc * 128:(dc + 1) * 128, :])
        p.wait("vector", "t", tm)
        p.wait("vector", "qs", lmy)
        vm = p.op("vector", lambda eng: eng.tensor_mul(ystage, ystage, ps[:, 0:T]))
        p.wait("vector", "v", vm)
        vm = p.op("vector", lambda eng, dc=dc: eng.tensor_add(
            x3[dc][:], x3[dc][:], ystage))
        p.wait("tensor", "v", vm)
        p.wait("sync", "v", vm)
    x3_done = p.mark("v")

    p.wait("tensor", "v", x3_done)
    x3tm_m = [0] * NT
    for tt in range(NT):
        for dc in range(DC):
            p.op("tensor", lambda eng, tt=tt, dc=dc:
                 eng.matmul(ptb[:, (dc % 4) * 128:(dc % 4) * 128 + 128],
                            x3[dc][:, tt * 128:(tt + 1) * 128], ident[:],
                            is_transpose=True, skip_group_check=True), sem="t")
            tm = p.mark("t")
            p.wait("vector", "t", tm)
            vm = p.op("vector", lambda eng, tt=tt, dc=dc: eng.tensor_copy(
                x3tm[tt][:, dc * 128:(dc + 1) * 128],
                ptb[:, (dc % 4) * 128:(dc % 4) * 128 + 128]))
            p.wait("tensor", "v", vm)
            x3tm_m[tt] = vm

    for tt in range(NT):
        p.wait("scalar", "v", x3tm_m[tt])
        sm = p.op("scalar", lambda eng, tt=tt: eng.activation(sq[:], x3tm[tt][:], AF.Square))
        vm = p.op("vector", lambda eng, tt=tt: eng.reduce_sum(
            ln4[:, 0:1], x3tm[tt][:], axis=AX.X))
        p.wait("vector", "s", sm)
        vm = p.op("vector", lambda eng: eng.reduce_sum(ln4[:, 1:2], sq[:], axis=AX.X))
        p.wait("vector", "v", vm)
        vm = p.op("vector", lambda eng: eng.tensor_scalar(
            ln4[:, 2:3], ln4[:, 0:1], 1.0 / D, None, ALU.mult))
        p.wait("scalar", "v", vm)
        sm = p.op("scalar", lambda eng: eng.activation(ln4[:, 3:4], ln4[:, 2:3], AF.Square))
        p.wait("vector", "s", sm)
        vm = p.op("vector", lambda eng: eng.scalar_tensor_tensor(
            ln4[:, 3:4], ln4[:, 1:2], 1.0 / D, ln4[:, 3:4], ALU.mult, ALU.subtract))
        p.wait("scalar", "v", vm)
        sm = p.op("scalar", lambda eng: eng.activation(ln4[:, 4:5], ln4[:, 3:4],
                                                       AF.Sqrt, bias=eps1[:], scale=1.0))
        p.wait("vector", "s", sm)
        vm = p.op("vector", lambda eng: eng.reciprocal(ln4[:, 4:5], ln4[:, 4:5]))
        p.wait("vector", "v", vm)
        vm = p.op("vector", lambda eng: eng.scalar_tensor_tensor(
            ln4[:, 5:6], ln4[:, 2:3], -1.0, ln4[:, 4:5], ALU.mult, ALU.mult))
        p.wait("vector", "v", vm)
        vm = p.op("vector", lambda eng, tt=tt: eng.scalar_tensor_tensor(
            xhat32[:], x3tm[tt][:], ln4[:, 4:5],
            ln4[:, 5:6].broadcast_to([128, D]), ALU.mult, ALU.add))
        p.wait("vector", "v", vm)
        emit_quant(p, xhat, D, am, inv, tq, src=xhat32)
        p.wait("tensor", "v", p.mark("v"))
        for dc in range(DC):
            p.op("tensor", lambda eng, tt=tt, dc=dc:
                 eng.matmul(ptb[:, (dc % 4) * 128:(dc % 4) * 128 + 128],
                            xhat[:, dc * 128:(dc + 1) * 128], ident[:],
                            is_transpose=True, skip_group_check=True), sem="t")
            tm = p.mark("t")
            p.wait("vector", "t", tm)
            vm = p.op("vector", lambda eng, tt=tt, dc=dc: eng.tensor_copy(
                xq[dc][:, tt * 128:(tt + 1) * 128],
                ptb[:, (dc % 4) * 128:(dc % 4) * 128 + 128]))
            p.wait("tensor", "v", vm)
        p.wait("scalar", "v", p.mark("v"))

    if dbg:
        d_xq = p.dram("d_xq", [D, T], BF16, "ExternalOutput").ap()
        d_hraw = p.dram("d_hraw", [128, 1024], BF16, "ExternalOutput").ap()
        d_hq = p.dram("d_hq", [128, 1024], BF16, "ExternalOutput").ap()
        d_acc0 = p.dram("d_acc0", [128, T], BF16, "ExternalOutput").ap()
        p.wait("gpsimd", "v", p.mark("v"))
        for dc in range(DC):
            p.dma("gpsimd", d_xq[dc * 128:(dc + 1) * 128, :], xq[dc][:])
    wlm = 0
    for grp in range(4):
        p.wait("sync", "s", p.mark("s"))
        p.wait("sync", "v", p.mark("v"))
        for ci in range(8):
            r0 = grp * 1024 + ci * 128
            if wlm:
                p.wait("sync", "v", wlm)
            lmw = p.dma("sync", stage[:, 0:D], wfc_h[r0:r0 + 128, :])
            p.wait("vector", "qs", lmw)
            p.wait("vector", "t", p.mark("t"))
            wlm = emit_quant(p, wq_c, D, am, inv, tq, src=stage)
            p.wait("tensor", "v", wlm)
            for kc in range(DC):
                p.op("tensor", lambda eng, kc=kc:
                     eng.matmul(ptb[:, (kc % 4) * 128:(kc % 4) * 128 + 128],
                                wq_c[:, kc * 128:(kc + 1) * 128], ident[:],
                                is_transpose=True, skip_group_check=True), sem="t")
                tm = p.mark("t")
                p.wait("vector", "t", tm)
                vm = p.op("vector", lambda eng, kc=kc, ci=ci: eng.tensor_copy(
                    wfcqT[kc][:, ci * 128:(ci + 1) * 128],
                    ptb[:, (kc % 4) * 128:(kc % 4) * 128 + 128]))
                p.wait("tensor", "v", vm)
            wlm = p.mark("v")
        for ec in range(DC):
            if wlm:
                p.wait("sync", "v", wlm)
            lmw = p.dma("sync", stage[:, 0:1024],
                        wproj_h[ec * 128:(ec + 1) * 128, grp * 1024:(grp + 1) * 1024])
            p.wait("vector", "qs", lmw)
            p.wait("vector", "t", p.mark("t"))
            wlm = emit_quant(p, wq_c, 1024, am, inv, tq, src=stage)
            p.wait("tensor", "v", wlm)
            for nc8 in range(DC):
                p.op("tensor", lambda eng, nc8=nc8:
                     eng.matmul(ptb[:, (nc8 % 4) * 128:(nc8 % 4) * 128 + 128],
                                wq_c[:, nc8 * 128:(nc8 + 1) * 128], ident[:],
                                is_transpose=True, skip_group_check=True), sem="t")
                tm = p.mark("t")
                p.wait("vector", "t", tm)
                vm = p.op("vector", lambda eng, nc8=nc8, ec=ec: eng.tensor_copy(
                    wpqT[nc8][:, ec * 128:(ec + 1) * 128],
                    ptb[:, (nc8 % 4) * 128:(nc8 % 4) * 128 + 128]))
                p.wait("tensor", "v", vm)
            wlm = p.mark("v")

        for tt in range(NT):
            for kc in range(DC):
                for nb in range(2):
                    p.op("tensor", lambda eng, tt=tt, kc=kc, nb=nb:
                         eng.matmul(ps[:, nb * 512:(nb + 1) * 512],
                                    xq[kc][:, tt * 128:(tt + 1) * 128],
                                    wfcqT[kc][:, nb * 512:(nb + 1) * 512],
                                    start=(kc == 0), stop=False,
                                    skip_group_check=True), sem="t")
            for nb in range(2):
                p.op("tensor", lambda eng, tt=tt, nb=nb, grp=grp:
                     eng.matmul(ps[:, nb * 512:(nb + 1) * 512],
                                ones1[:],
                                bfc_sb[0:1, grp * 1024 + nb * 512:grp * 1024 + (nb + 1) * 512],
                                start=False, stop=True,
                                skip_group_check=True), sem="t")
            tm = p.mark("t")
            p.wait("scalar", "t", tm)
            sm = p.op("scalar", lambda eng, tt=tt: eng.activation(
                h32[:], ps[:, 0:1024], AF.Gelu))
            p.wait("tensor", "s", sm)
            if dbg and grp == 0 and tt == 0:
                p.wait("gpsimd", "s", sm)
                dmh = p.dma("gpsimd", d_hraw, h32[:])
                p.wait("vector", "qg", dmh)
            p.wait("vector", "s", sm)
            qm = emit_quant(p, x3tm[tt], D, am, inv, tq, src=h32)
            p.wait("scalar", "v", qm)
            if dbg and grp == 0 and tt == 0:
                p.wait("gpsimd", "v", qm)
                dmh = p.dma("gpsimd", d_hq, x3tm[0][:])
                p.wait("tensor", "qg", dmh)
        hq_done = p.mark("v")

        for half in range(2):
            p.wait("tensor", "v", hq_done if half == 0 else p.mark("v"))
            for tt in range(NT):
                for nc4 in range(4):
                    p.op("tensor", lambda eng, tt=tt, nc4=nc4, half=half:
                         eng.matmul(ptb[:, nc4 * 128:(nc4 + 1) * 128],
                                    x3tm[tt][:, half * 512 + nc4 * 128:half * 512 + (nc4 + 1) * 128],
                                    ident[:], is_transpose=True,
                                    skip_group_check=True), sem="t")
                tm = p.mark("t")
                p.wait("vector", "t", tm)
                for nc4 in range(4):
                    vm = p.op("vector", lambda eng, tt=tt, nc4=nc4: eng.tensor_copy(
                        hqT[nc4][:, tt * 128:(tt + 1) * 128],
                        ptb[:, nc4 * 128:(nc4 + 1) * 128]))
                p.wait("tensor", "v", vm)
            for es in range(DC):
                for nc4 in range(4):
                    for tb in range(T // 512):
                        fn = (lambda eng, es=es, nc4=nc4, tb=tb, half=half:
                              eng.matmul(ps[:, 2048 + tb * 512:2048 + (tb + 1) * 512],
                                         wpqT[half * 4 + nc4][:, es * 128:(es + 1) * 128],
                                         hqT[nc4][:, tb * 512:(tb + 1) * 512],
                                         start=(nc4 == 0), stop=(nc4 == 3),
                                         skip_group_check=True))
                        if nc4 == 3 and tb == T // 512 - 1:
                            p.op("tensor", fn, sem="t")
                        else:
                            p.opn("tensor", fn)
                tm = p.mark("t")
                first = (grp == 0 and half == 0)
                if first:
                    p.wait("scalar", "t", tm)
                    sm = p.op("scalar", lambda eng, es=es: eng.activation(
                        acc[es][:], ps[:, 2048:4096], AF.Identity,
                        bias=bproj_pp[:, es:es + 1], scale=1.0))
                    p.wait("tensor", "s", sm)
                else:
                    p.wait("vector", "t", tm)
                    vm = p.op("vector", lambda eng, es=es: eng.tensor_add(
                        acc[es][:], acc[es][:], ps[:, 2048:4096]))
                    p.wait("tensor", "v", vm)

    if dbg:
        p.wait("gpsimd", "v", p.mark("v"))
        dmh = p.dma("gpsimd", d_acc0, acc[0][:])
        p.wait("vector", "qg", dmh)
    p.wait("tensor", "v", p.mark("v"))
    fstage = tq
    dmo = 0
    for dc in range(DC):
        vm = p.op("vector", lambda eng, dc=dc: eng.tensor_add(
            acc[dc][:], acc[dc][:], x3[dc][:]))
        p.wait("tensor", "v", vm)
    for tt in range(NT):
        for dc in range(DC):
            p.op("tensor", lambda eng, tt=tt, dc=dc:
                 eng.matmul(ptb[:, (dc % 4) * 128:(dc % 4) * 128 + 128],
                            acc[dc][:, tt * 128:(tt + 1) * 128], ident[:],
                            is_transpose=True, skip_group_check=True), sem="t")
            tm = p.mark("t")
            p.wait("vector", "t", tm)
            if dmo:
                p.wait("vector", "qg", dmo)
            vm = p.op("vector", lambda eng, tt=tt, dc=dc: eng.tensor_copy(
                fstage[:, dc * 128:(dc + 1) * 128],
                ptb[:, (dc % 4) * 128:(dc % 4) * 128 + 128]))
            p.wait("tensor", "v", vm)
        vm = p.mark("v")
        p.wait("gpsimd", "v", vm)
        dmo = p.dma("gpsimd", out_h[tt * 128:(tt + 1) * 128, :], fstage[:, 0:D])
    p.wait("gpsimd", "qg", p.mark("qg"))
    return p.finalize()





PAIRS = [[0, 1], [2, 3], [4, 5], [6, 7]]
GRP8 = [list(range(8))]


def build_fused():
    import contextlib as _ctx
    p = Prog()
    xT = p.dram("xT", [D, T], F32, "ExternalInput").ap()
    wg_h = p.dram("Wg", [D, D], F32, "ExternalInput").ap()
    wv_h = p.dram("Wv", [D, D], F32, "ExternalInput").ap()
    wd_h = p.dram("Wd", [D, D], F32, "ExternalInput").ap()
    wmix_h = p.dram("wmix_row", [1, D], F32, "ExternalInput").ap()
    bias3_h = p.dram("bias3", [3 * DC, 128], F32, "ExternalInput").ap()
    wffn_h = p.dram("wffn_row", [1, D], F32, "ExternalInput").ap()
    wgate_h = p.dram("W_gate", [D, FF], F32, "ExternalInput").ap()
    wup_h = p.dram("W_up", [D, FF], F32, "ExternalInput").ap()
    wout_h = p.dram("W_out", [FF, D], F32, "ExternalInput").ap()
    wrT_h = p.dram("W_rT", [D, D], F32, "ExternalInput").ap()
    wkT_h = p.dram("W_kT", [D, D], F32, "ExternalInput").ap()
    wvT_h = p.dram("W_vT", [D, D], F32, "ExternalInput").ap()
    mu2_h = p.dram("mu2", [2 * DC, 128], F32, "ExternalInput").ap()
    wdkvT_h = p.dram("W_dkvT", [D, 128], F32, "ExternalInput").ap()
    wdq_h = p.dram("W_dq", [128, D], F32, "ExternalInput").ap()
    wuqT_h = p.dram("W_uqT_p", [128, D], F32, "ExternalInput").ap()
    wdqp_h = p.dram("W_dq_p", [128, D], F32, "ExternalInput").ap()
    wukp_h = p.dram("W_uk_p", [D, 128], F32, "ExternalInput").ap()
    wuv_h = p.dram("W_uv", [D, 128], F32, "ExternalInput").ap()
    woTp_h = p.dram("W_oT_p", [D, D], F32, "ExternalInput").ap()
    ident_h = p.dram("ident", [128, 128], BF16, "ExternalInput").ap()
    identf_h = p.dram("identf", [128, 128], F32, "ExternalInput").ap()
    mask4_h = p.dram("mask4", [4, 128, 512], F32, "ExternalInput").ap()
    wfcsh_h = p.dram("wfc_sh", [512, D], F32, "ExternalInput").ap()
    bfc_h = p.dram("bfc_row", [1, HID], F32, "ExternalInput").ap()
    wpjsh_h = p.dram("wpj_sh", [128, HID], F32, "ExternalInput").ap()
    bproj_h = p.dram("bproj", [DC, 128], F32, "ExternalInput").ap()
    ind_h = p.dram("ind", [16, D], BF16, "ExternalInput").ap()
    flag_h = p.dram("flag", [128, 1], F32, "ExternalInput").ap()
    out_h = p.dram("out", [T, D], F32, "ExternalOutput").ap()

    a_dram = p.nc.dram_tensor("a_dram", [D, T], BF16)
    cc_carry_i = p.nc.dram_tensor("cc_carry_i", [128, DC], F32)
    cc_carry_o = p.nc.dram_tensor("cc_carry_o", [256, DC], F32)
    cc_pcol_i = p.nc.dram_tensor("cc_pcol_i", [128, DC], BF16)
    cc_pcol_o = p.nc.dram_tensor("cc_pcol_o", [256, DC], BF16)
    cc_gx_i = p.nc.dram_tensor("cc_gx_i", [D, T], BF16)
    cc_gx_o = p.nc.dram_tensor("cc_gx_o", [2 * D, T], BF16)
    cc_gc_i = p.nc.dram_tensor("cc_gc_i", [128, T], BF16)
    cc_gc_o = p.nc.dram_tensor("cc_gc_o", [256, T], BF16)
    cc_gy_i = p.nc.dram_tensor("cc_gy_i", [512, TT], BF16)
    cc_gy_o = p.nc.dram_tensor("cc_gy_o", [1024, TT], BF16)
    cc_gz_i = p.nc.dram_tensor("cc_gz_i", [256, 128], BF16)
    cc_gz_o = p.nc.dram_tensor("cc_gz_o", [512, 128], BF16)
    wfcT_i = p.nc.dram_tensor("wfcT_i", [1024, 512], BF16)
    wfcT_o = p.nc.dram_tensor("wfcT_o", [8192, 512], BF16)
    wpjT_i = p.nc.dram_tensor("wpjT_i", [4096, 128], BF16)
    wpjT_o = p.nc.dram_tensor("wpjT_o", [32768, 128], BF16)

    def gather(din, dout, groups=PAIRS):
        cm = p.op("gpsimd", lambda eng: eng.collective_compute(
            "AllGather", ALU.bypass, replica_groups=groups,
            ins=[din.ap().opt()], outs=[dout.ap().opt()]), sem="c2", k=1)
        p.wait("gpsimd", "c2", cm)
        return cm

    PR = _ctx.ExitStack()
    PR1 = _ctx.ExitStack()
    P12 = _ctx.ExitStack()
    stream = [p.sbuf(f"strm{i}", [128, T], BF16) for i in range(DC)]
    ones_col = p.sbuf("ones_col", [128, 1], BF16)
    ones_row = p.sbuf("ones_row", [1, 128], F32)
    ones_rowb = p.sbuf("ones_rowb", [1, 128], BF16)
    eps1 = p.sbuf("eps1", [1, 1], F32)
    eps2 = p.sbuf("eps2", [1, 1], F32)
    eps1c = p.sbuf("eps1c", [128, 1], F32)
    flag = p.sbuf("flag_sb", [128, 1], F32)
    ident = p.sbuf("ident_sb", [128, 128], BF16)
    stage = p.sbuf("stage_g", [128, 1024], F32)
    sq = p.sbuf("sq_g", [128, T], BF16)
    ps = p.psum("ps", [128, 4096], F32)
    ptb = ps[:, 3584:4096].bitcast(BF16)

    p.op("vector", lambda eng: eng.memset(ones_col[:], 1.0))
    p.op("vector", lambda eng: eng.memset(ones_row[:], 1.0))
    p.op("vector", lambda eng: eng.memset(ones_rowb[:], 1.0))
    p.op("vector", lambda eng: eng.memset(eps1[:], 1e-5))
    p.op("vector", lambda eng: eng.memset(eps2[:], 1e-6))
    p.op("vector", lambda eng: eng.memset(eps1c[:], 1e-5))
    p.dma("sync", ident[:], ident_h)
    p.dma("sync", flag[:], flag_h)

    P1 = _ctx.ExitStack()
    r_rstd = p.sbuf("r_rstd", [1, T], F32, stack=PR)
    r_shp = p.sbuf("r_shp", [1, T], F32, stack=PR)
    h = [p.sbuf(f"h{i}", [128, T], BF16, stack=P12) for i in range(DC)]
    r_var = p.sbuf("rw_var", [1, T], F32, stack=PR1)
    r_rho = p.sbuf("rw_rho", [1, T], F32, stack=PR1)
    xn = [p.sbuf(f"p1xn{i}", [128, T], BF16, stack=P1) for i in range(DC)]
    wbf = {w: [p.sbuf(f"p1w{w}{i}", [128, D], BF16, stack=P1) for i in range(DC)]
           for w in "gvd"}
    w_row = p.sbuf("p1wrow", [1, D], F32, stack=P1)
    bias_pp = p.sbuf("p1bias", [128, 3 * DC], F32, stack=P1)
    tv = sq
    a_f = p.sbuf("p1af", [128, T], F32, stack=P1)
    xs_f = p.sbuf("p1xs", [128, T], F32, stack=P1)
    asta = stage[:].bitcast(BF16)
    sg = asta

    p.dma("sync", w_row[:], wmix_h)
    for r in range(3 * DC):
        p.dma("sync", bias_pp[:, r:r + 1],
              bias3_h[r:r + 1, :].rearrange("o d -> d o"))
    ldv = p.mark("qs")
    vm = 0
    for kc in range(DC):
        for hf in range(2):
            if vm:
                p.wait("sync", "v", vm)
            lm = p.dma("sync", stage[:, 0:1024],
                       xT[kc * 128:(kc + 1) * 128, hf * 1024:(hf + 1) * 1024])
            p.wait("vector", "qs", lm)
            vm = p.op("vector", lambda eng, kc=kc, hf=hf: eng.tensor_copy(
                stream[kc][:, hf * 1024:(hf + 1) * 1024], stage[:, 0:1024]))
    xb_done = vm
    p.wait("scalar", "v", xb_done)
    p.wait("scalar", "qs", ldv)
    tm = emit_ln_stats(p, ps, stream, sq, ones_col)
    p.wait("vector", "t", tm)
    rowm = emit_ln_rows4(p, ps, r_shp, r_var, r_rstd, r_rho, eps1[:], eps2[:])
    p.wait("tensor", "v", rowm)
    for dc in range(DC):
        emit_bcast_apply(p, ps, w_row[0:1, dc * 128:(dc + 1) * 128],
                         r_rho, r_var, stream[dc], xn[dc])
    xn_done = p.mark("v")
    wm = xb_done
    for w, hbm in zip("gvd", (wg_h, wv_h, wd_h)):
        p.wait("vector", "v", xn_done)
        wm = emit_load_w_bf16(p, hbm, D, D, stage, wbf[w], stage_guard=wm)
    p.wait("tensor", "v", wm)

    def proj_p1(w, dc, b0):
        for kc in range(DC):
            for tb in range(T // 512):
                fn = (lambda eng, kc=kc, tb=tb, dc=dc, b0=b0, w=w:
                      eng.matmul(ps[:, (b0 + tb) * 512:(b0 + tb + 1) * 512],
                                 wbf[w][kc][:, dc * 128:(dc + 1) * 128],
                                 xn[kc][:, tb * 512:(tb + 1) * 512],
                                 start=(kc == 0), stop=(kc == DC - 1)))
                if kc == DC - 1 and tb == T // 512 - 1:
                    p.op("tensor", fn, sem="t")
                else:
                    p.opn("tensor", fn)
        return p.mark("t")

    for dc in range(DC):
        tm = proj_p1("g", dc, 0)
        p.wait("scalar", "t", tm)
        smg = p.op("scalar", lambda eng, dc=dc: eng.activation(
            sg[:], ps[:, 0:T], AF.Sigmoid, bias=bias_pp[:, dc:dc + 1], scale=1.0))
        tm = proj_p1("v", dc, 4)
        p.wait("scalar", "t", tm)
        smv = p.op("scalar", lambda eng, dc=dc: eng.activation(
            tv[:], ps[:, T:2 * T], AF.Tanh,
            bias=bias_pp[:, DC + dc:DC + dc + 1], scale=1.0))
        p.wait("tensor", "s", smg)
        tm = proj_p1("d", dc, 0)
        p.wait("scalar", "t", tm)
        smd = p.op("scalar", lambda eng, dc=dc: eng.activation(
            a_f[:], ps[:, 0:T], AF.Sigmoid,
            bias=bias_pp[:, 2 * DC + dc:2 * DC + dc + 1], scale=1.0))
        p.wait("tensor", "s", smd)
        p.wait("tensor", "s", smv)
        p.wait("vector", "s", smd)
        p.op("vector", lambda eng: eng.tensor_scalar(
            a_f[:], a_f[:], 0.998, 0.001, ALU.mult, ALU.add))
        p.op("vector", lambda eng: eng.tensor_mul(xs_f[:], sg[:], tv[:]))
        p.op("vector", lambda eng, dc=dc: eng.tensor_tensor_scan(
            h[dc][:], a_f[:], xs_f[:], 0.0, ALU.mult, ALU.add))
        vm = p.op("vector", lambda eng: eng.tensor_tensor_scan(
            asta, a_f[:], a_f[:], 1.0, ALU.mult, ALU.bypass))
        p.wait("gpsimd", "v", vm)
        dm = p.dma("gpsimd", a_dram.ap()[dc * 128:(dc + 1) * 128, :], asta)
        p.wait("vector", "qg", dm)
        p.wait("scalar", "qg", dm)
    p.barrier()
    P1.close()
    PR1.close()

    P2 = _ctx.ExitStack()
    GRP = 6
    NGRP = FF // 128 // GRP
    yn = [p.sbuf(f"p2yn{i}", [128, T], BF16, stack=P2) for i in range(DC)]
    mt = [p.sbuf(f"p2mt{i}", [128, T], BF16, stack=P2) for i in range(GRP)]
    wo_sl = [p.sbuf(f"p2wo{i}", [128, D], BF16, stack=P2) for i in range(GRP)]
    wg_sl = [p.sbuf(f"p2wg{i}", [128, D], BF16, stack=P2) for i in range(2)]
    wu_sl = [p.sbuf(f"p2wu{i}", [128, D], BF16, stack=P2) for i in range(2)]
    w_row2 = p.sbuf("p2wrow", [1, D], F32, stack=P2)
    carry = p.sbuf("p2carry", [128, DC], F32, stack=P2)
    tmpb = p.sbuf("p2tmpb", [128, T], BF16, stack=P2)

    p.dma("sync", w_row2[:], wffn_h)
    for dc in range(DC):
        p.op("vector", lambda eng, dc=dc: eng.tensor_copy(
            carry[:, dc:dc + 1], h[dc][:, T - 1:T]))
    vm = p.mark("v")
    p.wait("gpsimd", "v", vm)
    dm = p.dma("gpsimd", cc_carry_i.ap(), carry[:])
    p.wait("gpsimd", "qg", p.mark("qg"))
    gather(cc_carry_i, cc_carry_o)
    dm = p.dma("gpsimd", carry[:], cc_carry_o.ap()[0:128, :])
    p.wait("vector", "qg", p.mark("qg"))
    vm = p.op("vector", lambda eng: eng.tensor_scalar(
        carry[:], carry[:], flag[:, 0:1], None, ALU.mult))
    p.wait("vector", "v", vm)
    rb_s = p.sbuf("rb2s", [1, T], BF16, stack=P2)
    rb_t = p.sbuf("rb2t", [1, T], BF16, stack=P2)
    p.wait("vector", "t", p.mark("t"))
    emit_row_hilo(p, r_rstd, rb_s)
    hl = emit_row_hilo(p, r_shp, rb_t)
    p.wait("tensor", "v", hl)
    for dc in range(DC):
        lm = p.dma("sync", tmpb[:], a_dram.ap()[dc * 128:(dc + 1) * 128, :])
        p.wait("vector", "qs", lm)
        vm = p.op("vector", lambda eng, dc=dc: eng.scalar_tensor_tensor(
            h[dc][:], tmpb[:], carry[:, dc:dc + 1], h[dc][:], ALU.mult, ALU.add))
        p.wait("sync", "v", vm)
        emit_bcast_apply(p, ps, ones_row[0:1, 0:128], r_rstd, r_shp,
                         stream[dc], tmpb,
                         rows_b=(rb_s, rb_t), ones_b=ones_rowb[:])
        vm = p.op("vector", lambda eng, dc=dc: eng.tensor_add(
            h[dc][:], h[dc][:], tmpb[:]))
        p.wait("sync", "v", vm)
    xm = h
    xm_done = p.mark("v")

    p.wait("scalar", "v", xm_done)
    for kc in range(DC):
        sm = p.op("scalar", lambda eng, kc=kc: eng.activation(sq[:], xm[kc][:], AF.Square))
        p.wait("tensor", "s", sm)
        for tb in range(T // 512):
            p.op("tensor", lambda eng, kc=kc, tb=tb:
                 eng.matmul(ps[0:1, tb * 512:(tb + 1) * 512], ones_col[:],
                            sq[:, tb * 512:(tb + 1) * 512],
                            start=(kc == 0), stop=(kc == DC - 1)), sem="t")
        p.wait("scalar", "t", p.mark("t"))
    tm = p.mark("t")
    p.wait("vector", "t", tm)
    vm = p.op("vector", lambda eng: eng.tensor_scalar(
        r_shp[:], ps[0:1, 0:T], 1.0 / D, None, ALU.mult))
    p.wait("scalar", "v", vm)
    sm = p.op("scalar", lambda eng: eng.activation(r_shp[:], r_shp[:], AF.Sqrt,
                                                   bias=eps2[:], scale=1.0))
    p.wait("vector", "s", sm)
    rowm = p.op("vector", lambda eng: eng.reciprocal(r_rstd[:], r_shp[:]))
    p.wait("tensor", "v", rowm)
    for dc in range(DC):
        for tb in range(T // 512):
            fn = (lambda eng, dc=dc, tb=tb:
                  eng.matmul(ps[:, tb * 512:(tb + 1) * 512],
                             w_row2[0:1, dc * 128:(dc + 1) * 128],
                             r_rstd[0:1, tb * 512:(tb + 1) * 512],
                             start=True, stop=True, skip_group_check=True))
            if tb == T // 512 - 1:
                p.op("tensor", fn, sem="t")
            else:
                p.opn("tensor", fn)
        tm = p.mark("t")
        p.wait("vector", "t", tm)
        vm = p.op("vector", lambda eng, dc=dc: eng.tensor_mul(
            yn[dc][:], xm[dc][:], ps[:, 0:T]))
        p.wait("tensor", "v", vm)

    p.wait("sync", "v", p.mark("v"))
    _lc = [0]

    def ffn_load(grp2, gi2):
        fc = grp2 * GRP + gi2
        b = gi2 % 2
        wt = p.mark("t")
        if _lc[0]:
            p.wait("sync", "v", _lc[0])
        lm = p.dma("sync", stage[:, 0:D].rearrange("p (a n) -> p a n", n=128),
                   wgate_h[:, fc * 128:(fc + 1) * 128]
                   .rearrange("(a p) n -> p a n", p=128))
        p.wait("vector", "qs", lm)
        p.wait("vector", "t", wt)
        vg = p.op("vector", lambda eng, b=b: eng.tensor_copy(wg_sl[b][:],
                                                             stage[:, 0:D]))
        p.wait("sync", "v", vg)
        lm = p.dma("sync", stage[:, 0:D].rearrange("p (a n) -> p a n", n=128),
                   wup_h[:, fc * 128:(fc + 1) * 128]
                   .rearrange("(a p) n -> p a n", p=128))
        p.wait("vector", "qs", lm)
        vu = p.op("vector", lambda eng, b=b: eng.tensor_copy(wu_sl[b][:],
                                                             stage[:, 0:D]))
        p.wait("sync", "v", vu)
        lm = p.dma("sync", stage[:, 0:D], wout_h[fc * 128:(fc + 1) * 128, :])
        p.wait("vector", "qs", lm)
        vo = p.op("vector", lambda eng, gi2=gi2: eng.tensor_copy(
            wo_sl[gi2][:], stage[:, 0:D]))
        _lc[0] = vo
        return vg, vu, vo

    lds = {(0, 0): ffn_load(0, 0)}
    for grp in range(NGRP):
        for gi in range(GRP):
            b = gi % 2
            vg, vu, vo = lds[(grp, gi)]
            p.wait("tensor", "v", vg)
            for kc in range(DC):
                for tb in range(T // 512):
                    fn = (lambda eng, kc=kc, tb=tb, b=b:
                          eng.matmul(ps[:, tb * 512:(tb + 1) * 512],
                                     wg_sl[b][:, kc * 128:(kc + 1) * 128],
                                     yn[kc][:, tb * 512:(tb + 1) * 512],
                                     start=(kc == 0), stop=(kc == DC - 1)))
                    if kc == DC - 1 and tb == T // 512 - 1:
                        p.op("tensor", fn, sem="t")
                    else:
                        p.opn("tensor", fn)
            tm = p.mark("t")
            if gi + 1 < GRP:
                lds[(grp, gi + 1)] = ffn_load(grp, gi + 1)
            p.wait("scalar", "t", tm)
            sm = p.op("scalar", lambda eng: eng.activation(tmpb[:], ps[:, 0:T], AF.Silu))
            p.wait("tensor", "s", sm)
            p.wait("tensor", "v", vu)
            for kc in range(DC):
                for tb in range(T // 512):
                    fn = (lambda eng, kc=kc, tb=tb, b=b:
                          eng.matmul(ps[:, T + tb * 512:T + (tb + 1) * 512],
                                     wu_sl[b][:, kc * 128:(kc + 1) * 128],
                                     yn[kc][:, tb * 512:(tb + 1) * 512],
                                     start=(kc == 0), stop=(kc == DC - 1)))
                    if kc == DC - 1 and tb == T // 512 - 1:
                        p.op("tensor", fn, sem="t")
                    else:
                        p.opn("tensor", fn)
            tm = p.mark("t")
            p.wait("vector", "t", tm)
            p.wait("vector", "s", sm)
            vm = p.op("vector", lambda eng, gi=gi: eng.tensor_mul(
                mt[gi][:], tmpb[:], ps[:, T:2 * T]))
            p.wait("tensor", "v", vm)
            p.wait("scalar", "v", vm)
        mdone = p.mark("v")
        p.wait("tensor", "v", mdone)
        for dc in range(DC):
            for gi in range(GRP):
                for tb in range(T // 512):
                    fn = (lambda eng, gi=gi, tb=tb, dc=dc:
                          eng.matmul(ps[:, tb * 512:(tb + 1) * 512],
                                     wo_sl[gi][:, dc * 128:(dc + 1) * 128],
                                     mt[gi][:, tb * 512:(tb + 1) * 512],
                                     start=(gi == 0), stop=(gi == GRP - 1)))
                    if gi == GRP - 1 and tb == T // 512 - 1:
                        p.op("tensor", fn, sem="t")
                    else:
                        p.opn("tensor", fn)
            tm = p.mark("t")
            p.wait("vector", "t", tm)
            vm = p.op("vector", lambda eng, dc=dc: eng.tensor_add(
                xm[dc][:], xm[dc][:], ps[:, 0:T]))
            p.wait("tensor", "v", vm)
        if grp + 1 < NGRP:
            lds[(grp + 1, 0)] = ffn_load(grp + 1, 0)
    p.wait("vector", "v", p.mark("v"))
    for dc in range(DC):
        vm = p.op("vector", lambda eng, dc=dc: eng.tensor_add(
            stream[dc][:], stream[dc][:], xm[dc][:]))
    p.barrier()
    P2.close()
    P12.close()

    P3 = _ctx.ExitStack()
    xr = [p.sbuf(f"p3xr{i}", [128, T], BF16, stack=P3) for i in range(DC)]
    xk = [p.sbuf(f"p3xk{i}", [128, T], BF16, stack=P3) for i in range(DC)]
    rt = [p.sbuf(f"p3rt{i}", [128, T], BF16, stack=P3) for i in range(DC)]
    ka = xr
    xh2 = sq
    wbf3 = [p.sbuf(f"p3w{i}", [128, D], BF16, stack=P3) for i in range(DC)]
    mu_pp = p.sbuf("p3mu", [128, 4 * DC], F32, stack=P3)
    pcol = p.sbuf("p3pcol", [128, DC], BF16, stack=P3)
    pchat = p.sbuf("p3pchat", [128, DC], BF16, stack=P3)
    prow = p.sbuf("p3prow", [1, 2 * DC], F32, stack=P3)
    scr3 = p.sbuf("p3scr", [1, T], F32, stack=P3)
    pval = p.sbuf("p3pval", [1, 4], F32, stack=P3)
    ppb = p.sbuf("p3ppb", [128, 2], F32, stack=P3)

    for r in range(DC):
        p.dma("sync", mu_pp[:, r:r + 1], mu2_h[r:r + 1, :].rearrange("o d -> d o"))
        p.dma("sync", mu_pp[:, 2 * DC + r:2 * DC + r + 1],
              mu2_h[DC + r:DC + r + 1, :].rearrange("o d -> d o"))
    mul_ld = p.mark("qs")
    p.wait("vector", "qs", mul_ld)
    p.op("vector", lambda eng: eng.tensor_scalar(
        mu_pp[:, DC:2 * DC], mu_pp[:, 0:DC], -1.0, 1.0, ALU.mult, ALU.add))
    p.op("vector", lambda eng: eng.tensor_scalar(
        mu_pp[:, 3 * DC:4 * DC], mu_pp[:, 2 * DC:3 * DC], -1.0, 1.0, ALU.mult, ALU.add))
    for dc in range(DC):
        p.op("vector", lambda eng, dc=dc: eng.tensor_copy(
            pcol[:, dc:dc + 1], stream[dc][:, T - 1:T]))
    vm = p.mark("v")
    p.wait("gpsimd", "v", vm)
    p.dma("gpsimd", cc_pcol_i.ap(), pcol[:])
    p.wait("gpsimd", "qg", p.mark("qg"))
    gather(cc_pcol_i, cc_pcol_o)
    p.dma("gpsimd", pcol[:], cc_pcol_o.ap()[0:128, :])
    p.wait("vector", "qg", p.mark("qg"))
    vm = p.op("vector", lambda eng: eng.tensor_scalar(
        pcol[:], pcol[:], flag[:, 0:1], None, ALU.mult))
    p.wait("scalar", "v", vm)
    p.wait("tensor", "v", vm)

    tm = emit_ln_stats(p, ps, stream, sq, ones_col)
    p.wait("vector", "t", tm)
    rowm = emit_ln_rows_plain3(p, ps, scr3, r_rstd, r_shp, eps1[:])
    sm = p.op("scalar", lambda eng: eng.activation(xh2[:, 0:DC], pcol[:], AF.Square))
    p.wait("tensor", "s", sm)
    p.wait("tensor", "v", rowm)
    p.op("tensor", lambda eng: eng.matmul(ps[0:1, 0:DC], ones_col[:],
                                          pcol[:], start=True, stop=True,
                                          skip_group_check=True), sem="t")
    tm2 = p.op("tensor", lambda eng: eng.matmul(ps[0:1, DC:2 * DC],
                                                ones_col[:], xh2[:, 0:DC],
                                                start=True, stop=True,
                                                skip_group_check=True), sem="t")
    p.wait("vector", "t", tm2)
    vm = p.op("vector", lambda eng: eng.tensor_copy(prow[:], ps[0:1, 0:2 * DC]))
    p.wait("vector", "v", vm)
    vm = p.op("vector", lambda eng: eng.reduce_sum(pval[0:1, 0:1], prow[0:1, 0:DC], axis=AX.X))
    vm = p.op("vector", lambda eng: eng.reduce_sum(pval[0:1, 1:2], prow[0:1, DC:2 * DC], axis=AX.X))
    p.wait("vector", "v", vm)
    vm = p.op("vector", lambda eng: eng.tensor_scalar(
        pval[0:1, 2:3], pval[0:1, 0:1], 1.0 / D, None, ALU.mult))
    p.wait("scalar", "v", vm)
    sm = p.op("scalar", lambda eng: eng.activation(pval[0:1, 3:4], pval[0:1, 2:3], AF.Square))
    p.wait("vector", "s", sm)
    vm = p.op("vector", lambda eng: eng.scalar_tensor_tensor(
        pval[0:1, 3:4], pval[0:1, 1:2], 1.0 / D, pval[0:1, 3:4], ALU.mult, ALU.subtract))
    p.wait("scalar", "v", vm)
    sm = p.op("scalar", lambda eng: eng.activation(pval[0:1, 3:4], pval[0:1, 3:4],
                                                   AF.Sqrt, bias=eps1[:], scale=1.0))
    p.wait("vector", "s", sm)
    vm = p.op("vector", lambda eng: eng.reciprocal(pval[0:1, 3:4], pval[0:1, 3:4]))
    p.wait("vector", "v", vm)
    vm2 = p.op("vector", lambda eng: eng.scalar_tensor_tensor(
        pval[0:1, 2:3], pval[0:1, 2:3], -1.0, pval[0:1, 3:4], ALU.mult, ALU.mult))
    p.wait("tensor", "v", vm2)
    p.op("tensor", lambda eng: eng.matmul(ps[:, 0:1],
                                          ones_row[0:1, :], pval[0:1, 3:4],
                                          start=True, stop=True,
                                          skip_group_check=True), sem="t")
    tm2 = p.op("tensor", lambda eng: eng.matmul(ps[:, 1:2],
                                                ones_row[0:1, :], pval[0:1, 2:3],
                                                start=True, stop=True,
                                                skip_group_check=True), sem="t")
    p.wait("vector", "t", tm2)
    vm = p.op("vector", lambda eng: eng.tensor_copy(ppb[:], ps[:, 0:2]))
    p.wait("vector", "v", vm)
    vm = p.op("vector", lambda eng: eng.scalar_tensor_tensor(
        pchat[:], pcol[:], ppb[:, 0:1], ppb[:, 1:2].broadcast_to([128, DC]),
        ALU.mult, ALU.add))
    rb_s = p.sbuf("rb3s", [1, T], BF16, stack=P3)
    rb_t = p.sbuf("rb3t", [1, T], BF16, stack=P3)
    p.wait("vector", "t", p.mark("t"))
    emit_row_hilo(p, r_rstd, rb_s)
    emit_row_hilo(p, r_shp, rb_t)
    p.wait("tensor", "v", p.mark("v"))
    for dc in range(DC):
        emit_bcast_apply(p, ps, ones_row[0:1, 0:128], r_rstd, r_shp,
                         stream[dc], xh2,
                         rows_b=(rb_s, rb_t), ones_b=ones_rowb[:])
        vm = p.op("vector", lambda eng, dc=dc: eng.tensor_scalar(
            xr[dc][:, 1:T], xh2[:, 0:T - 1], mu_pp[:, DC + dc:DC + dc + 1],
            None, ALU.mult))
        vm = p.op("vector", lambda eng, dc=dc: eng.tensor_scalar(
            xr[dc][:, 0:1], pchat[:, dc:dc + 1], mu_pp[:, DC + dc:DC + dc + 1],
            None, ALU.mult))
        vm = p.op("vector", lambda eng, dc=dc: eng.tensor_scalar(
            xk[dc][:, 1:T], xh2[:, 0:T - 1], mu_pp[:, 3 * DC + dc:3 * DC + dc + 1],
            None, ALU.mult))
        vm = p.op("vector", lambda eng, dc=dc: eng.tensor_scalar(
            xk[dc][:, 0:1], pchat[:, dc:dc + 1], mu_pp[:, 3 * DC + dc:3 * DC + dc + 1],
            None, ALU.mult))
        vm = p.op("vector", lambda eng, dc=dc: eng.scalar_tensor_tensor(
            xr[dc][:], xh2[:], mu_pp[:, dc:dc + 1], xr[dc][:], ALU.mult, ALU.add))
        vm = p.op("vector", lambda eng, dc=dc: eng.scalar_tensor_tensor(
            xk[dc][:], xh2[:], mu_pp[:, 2 * DC + dc:2 * DC + dc + 1], xk[dc][:],
            ALU.mult, ALU.add))

    def proj_p3(src, dc, b0):
        for kc in range(DC):
            for tb in range(T // 512):
                fn = (lambda eng, kc=kc, tb=tb, dc=dc, b0=b0:
                      eng.matmul(ps[:, (b0 + tb) * 512:(b0 + tb + 1) * 512],
                                 wbf3[kc][:, dc * 128:(dc + 1) * 128],
                                 src[kc][:, tb * 512:(tb + 1) * 512],
                                 start=(kc == 0), stop=(kc == DC - 1)))
                if kc == DC - 1 and tb == T // 512 - 1:
                    p.op("tensor", fn, sem="t")
                else:
                    p.opn("tensor", fn)
        return p.mark("t")

    wm = emit_load_w_bf16(p, wrT_h, D, D, stage, wbf3, stage_guard=p.mark("v"))
    p.wait("tensor", "v", wm)
    for dc in range(DC):
        b0 = 0 if dc % 2 == 0 else 4
        tm = proj_p3(xr, dc, b0)
        p.wait("scalar", "t", tm)
        sm = p.op("scalar", lambda eng, dc=dc, b0=b0: eng.activation(
            rt[dc][:], ps[:, b0 * 512:b0 * 512 + T], AF.Sigmoid))
        p.wait("tensor", "s", sm)
    p.wait("vector", "t", p.mark("t"))
    wm = emit_load_w_bf16(p, wkT_h, D, D, stage, wbf3, stage_guard=wm)
    p.wait("tensor", "v", wm)
    for dc in range(DC):
        b0 = 0 if dc % 2 == 0 else 4
        tm = proj_p3(xk, dc, b0)
        p.wait("scalar", "t", tm)
        sm = p.op("scalar", lambda eng, dc=dc, b0=b0: eng.activation(
            xh2[:], ps[:, b0 * 512:b0 * 512 + T], AF.Relu))
        p.wait("scalar", "s", sm)
        sm = p.op("scalar", lambda eng, dc=dc: eng.activation(
            ka[dc][:], xh2[:], AF.Square))
        p.wait("tensor", "s", sm)
    p.wait("vector", "t", p.mark("t"))
    wm = emit_load_w_bf16(p, wvT_h, D, D, stage, wbf3, stage_guard=wm)
    p.wait("tensor", "v", wm)
    for dc in range(DC):
        b0 = 0 if dc % 2 == 0 else 4
        tm = proj_p3(ka, dc, b0)
        p.wait("vector", "t", tm)
        vm = p.op("vector", lambda eng, dc=dc, b0=b0: eng.tensor_mul(
            xh2[:], rt[dc][:], ps[:, b0 * 512:b0 * 512 + T]))
        p.wait("vector", "v", vm)
        vm = p.op("vector", lambda eng, dc=dc: eng.tensor_add(
            stream[dc][:], stream[dc][:], xh2[:]))
        p.wait("tensor", "v", vm)
    p.barrier()
    P3.close()

    P4 = _ctx.ExitStack()
    x3l = [p.sbuf(f"p4x3{i}", [128, T], BF16, stack=P4) for i in range(DC)]
    wdkv = [p.sbuf(f"p4wdkv{i}", [128, 128], BF16, stack=P4) for i in range(DC)]
    ckvl = p.sbuf("p4ckv", [128, T], BF16, stack=P4)
    scr4 = p.sbuf("p4scr", [1, T], F32, stack=P4)
    wm = 0
    for kc in range(DC):
        if wm:
            p.wait("sync", "v", wm)
        lm = p.dma("sync", stage[:, 0:128], wdkvT_h[kc * 128:(kc + 1) * 128, :])
        p.wait("vector", "qs", lm)
        wm = p.op("vector", lambda eng, kc=kc: eng.tensor_copy(wdkv[kc][:], stage[:, 0:128]))
    tm = emit_ln_stats(p, ps, stream, sq, ones_col)
    p.wait("vector", "t", tm)
    rowm = emit_ln_rows_plain3(p, ps, scr4, r_rstd, r_shp, eps1[:])
    rb_s = p.sbuf("rb4s", [1, T], BF16, stack=P4)
    rb_t = p.sbuf("rb4t", [1, T], BF16, stack=P4)
    p.wait("vector", "t", p.mark("t"))
    emit_row_hilo(p, r_rstd, rb_s)
    hl = emit_row_hilo(p, r_shp, rb_t)
    p.wait("tensor", "v", hl)
    dmx = 0
    for dc in range(DC):
        vm = emit_bcast_apply(p, ps, ones_row[0:1, 0:128], r_rstd, r_shp,
                              stream[dc], x3l[dc],
                              rows_b=(rb_s, rb_t), ones_b=ones_rowb[:])
        p.wait("gpsimd", "v", vm)
        dmx = p.dma("gpsimd", cc_gx_i.ap()[dc * 128:(dc + 1) * 128, :], x3l[dc][:])
    p.wait("tensor", "v", wm)
    for tb in range(T // 512):
        for kc in range(DC):
            fn = (lambda eng, kc=kc, tb=tb:
                  eng.matmul(ps[:, tb * 512:(tb + 1) * 512], wdkv[kc][:],
                             x3l[kc][:, tb * 512:(tb + 1) * 512],
                             start=(kc == 0), stop=(kc == DC - 1)))
            if tb == T // 512 - 1 and kc == DC - 1:
                p.op("tensor", fn, sem="t")
            else:
                p.opn("tensor", fn)
    tm = p.mark("t")
    p.wait("vector", "t", tm)
    vm = p.op("vector", lambda eng: eng.tensor_copy(ckvl[:], ps[:, 0:T]))
    p.wait("gpsimd", "v", vm)
    p.dma("gpsimd", cc_gc_i.ap(), ckvl[:])
    p.wait("gpsimd", "qg", p.mark("qg"))

    wsh = [p.sbuf(f"wq4s{i}", [128, 1024], BF16, stack=P4) for i in range(4)]
    wtq = p.sbuf("wq4tq", [128, 1024], F32, stack=P4)
    wam = p.sbuf("wq4am", [128, 32], F32, stack=P4)
    winv = p.sbuf("wq4in", [128, 32], F32, stack=P4)
    tevac = p.sbuf("wq4ev", [128, 1024], BF16, stack=P4)
    qm = wm
    for wrt in range(4):
        p.wait("sync", "v", qm)
        lw = p.dma("sync", stage[:, 0:1024], wfcsh_h[wrt * 128:(wrt + 1) * 128, :])
        p.wait("vector", "qs", lw)
        qm = emit_quant(p, wsh[wrt], 1024, wam, winv, wtq, src=stage)
    p.wait("tensor", "v", qm)
    dmw = 0
    for wk2 in range(4):
        for wsb in range(2):
            wkc = wk2 * 2 + wsb
            for wrt in range(4):
                fn = (lambda eng, wkc=wkc, wrt=wrt, wsb=wsb:
                      eng.matmul(ptb[:, (wsb * 4 + wrt) * 128:(wsb * 4 + wrt + 1) * 128],
                                 wsh[wrt][:, wkc * 128:(wkc + 1) * 128], ident[:],
                                 is_transpose=True, skip_group_check=True))
                if wsb == 1 and wrt == 3:
                    p.op("tensor", fn, sem="t")
                else:
                    p.opn("tensor", fn)
        tm = p.mark("t")
        p.wait("vector", "t", tm)
        if dmw:
            p.wait("vector", "qs", dmw)
        ev = p.op("vector", lambda eng: eng.tensor_copy(tevac[:], ptb[:, 0:1024]))
        p.wait("tensor", "v", ev)
        p.wait("sync", "v", ev)
        for wsb in range(2):
            wkc = wk2 * 2 + wsb
            dmw = p.dma("sync", wfcT_i.ap()[wkc * 128:(wkc + 1) * 128, :],
                        tevac[:, wsb * 512:wsb * 512 + 512])
    for wpc in range(4):
        p.wait("sync", "v", qm)
        lw = p.dma("sync", stage[:, 0:1024], wpjsh_h[:, wpc * 1024:(wpc + 1) * 1024])
        p.wait("vector", "qs", lw)
        p.wait("vector", "t", p.mark("t"))
        qm = emit_quant(p, wsh[wpc], 1024, wam, winv, wtq, src=stage)
    p.wait("tensor", "v", qm)
    for wpc in range(4):
        for wkc in range(8):
            fn = (lambda eng, wpc=wpc, wkc=wkc:
                  eng.matmul(ptb[:, wkc * 128:(wkc + 1) * 128],
                             wsh[wpc][:, wkc * 128:(wkc + 1) * 128], ident[:],
                             is_transpose=True, skip_group_check=True))
            if wkc == 7:
                p.op("tensor", fn, sem="t")
            else:
                p.opn("tensor", fn)
        tm = p.mark("t")
        p.wait("vector", "t", tm)
        p.wait("vector", "qs", dmw)
        ev = p.op("vector", lambda eng: eng.tensor_copy(tevac[:], ptb[:, 0:1024]))
        p.wait("tensor", "v", ev)
        p.wait("sync", "v", ev)
        for wkc in range(8):
            dmw = p.dma("sync",
                        wpjT_i.ap()[(wpc * 8 + wkc) * 128:(wpc * 8 + wkc + 1) * 128, :],
                        tevac[:, wkc * 128:(wkc + 1) * 128])

    gather(cc_gx_i, cc_gx_o)
    gather(cc_gc_i, cc_gc_o)
    p.barrier()
    P4.close()
    PR.close()

    P5 = _ctx.ExitStack()
    x3f = [p.sbuf(f"p5x3f{i}", [128, TT], BF16, stack=P5) for i in range(DC)]
    ckv_fm = p.sbuf("p5ckvfm", [128, TT], BF16, stack=P5)
    ckv_tm = [p.sbuf(f"p5ckvtm{i}", [128, 128], BF16, stack=P5) for i in range(NKT)]
    mown = [p.sbuf(f"p5mown{i}", [128, D], BF16, stack=P5) for i in range(DC)]
    vown = p.sbuf("p5vown", [128, 512], BF16, stack=P5)
    at1 = [p.sbuf(f"p5at1_{i}", [128, D], BF16, stack=P5) for i in range(4)]
    at2 = [p.sbuf(f"p5at2_{i}", [128, 512], BF16, stack=P5) for i in range(DC)]
    kloc = [p.sbuf(f"p5kloc{i}", [128, 128], BF16, stack=P5) for i in range(4)]
    wtmp = p.sbuf("p5wtmp", [128, D], BF16, stack=P5)
    identf = p.sbuf("p5identf", [128, 128], F32, stack=P5)
    mask4 = p.sbuf("p5mask4", [128, 4 * 512], F32, stack=P5)
    pexp = p.sbuf("p5pexp", [128, 2048], BF16, stack=P5)
    pT = p.sbuf("p5pT", [128, 16 * 512], BF16, stack=P5)
    qlb = [p.sbuf(f"p5qlb{i}", [128, 512], BF16, stack=P5) for i in range(2)]
    sqq = p.sbuf("p5sqq", [128, 512], BF16, stack=P5)
    dql = p.sbuf("p5dql", [128, 512], BF16, stack=P5)
    stats = p.sbuf("p5stats", [128, 16], F32, stack=P5)
    numu = p.sbuf("p5numu", [128, 16], F32, stack=P5)
    r2pp = p.sbuf("p5r2pp", [128, 1], F32, stack=P5)
    zc = p.sbuf("p5zc", [128, 8 * 32], F32, stack=P5)
    zpart = p.sbuf("p5zpart", [128, 1], F32, stack=P5)
    ctx_sb = p.sbuf("p5ctx", [128, 512], BF16, stack=P5)
    y_sb = p.sbuf("p5ysb", [128, 512], BF16, stack=P5)
    rzt = p.sbuf("p5rzt", [32, 128], BF16, stack=P5)
    sqrow = p.sbuf("p5sqrow", [128, 512], BF16, stack=P5)
    r2row = p.sbuf("p5r2row", [1, TT // 512], F32, stack=P5)
    wuk_sb = [p.sbuf(f"p5wuk{i}", [128, 128], BF16, stack=P5) for i in range(DC)]
    wdq_sb = p.sbuf("p5wdq", [128, D], BF16, stack=P5)
    wuv_sb = p.sbuf("p5wuv", [128, 128], BF16, stack=P5)

    p.dma("sync", identf[:], identf_h)
    for i in range(4):
        p.dma("sync", mask4[:, i * 512:(i + 1) * 512], mask4_h[i])
    for dc in range(DC):
        p.dma("sync", x3f[dc][:, 0:T], cc_gx_o.ap()[dc * 128:(dc + 1) * 128, :])
        p.dma("sync", x3f[dc][:, T:TT], cc_gx_o.ap()[D + dc * 128:D + (dc + 1) * 128, :])
    p.dma("sync", ckv_fm[:, 0:T], cc_gc_o.ap()[0:128, :])
    lm = p.dma("sync", ckv_fm[:, T:TT], cc_gc_o.ap()[128:256, :])
    in_ld = p.mark("qs")

    def loadw5(hbm, ncols, dst, guard):
        if guard:
            p.wait("sync", "v", guard)
        lm = p.dma("sync", stage[:, 0:ncols], hbm)
        p.wait("vector", "qs", lm)
        return p.op("vector", lambda eng: eng.tensor_copy(dst[:, 0:ncols],
                                                          stage[:, 0:ncols]))
    wuqm = loadw5(wuqT_h, D, wtmp, 0)
    wdqm = loadw5(wdq_h, D, wdq_sb, wuqm)
    p.wait("tensor", "v", wdqm)
    for jc in range(4):
        for nb in range(2):
            p.op("tensor", lambda eng, jc=jc, nb=nb:
                 eng.matmul(ps[:, nb * 512:(nb + 1) * 512],
                            wtmp[:, jc * 128:(jc + 1) * 128],
                            wdq_sb[:, nb * 512:(nb + 1) * 512],
                            start=True, stop=True, skip_group_check=True), sem="t")
        tm = p.mark("t")
        p.wait("vector", "t", tm)
        vm = p.op("vector", lambda eng, jc=jc: eng.tensor_copy(at1[jc][:], ps[:, 0:1024]))
        p.wait("tensor", "v", vm)
    p.wait("vector", "t", p.mark("t"))
    wdqm = loadw5(wdqp_h, D, wdq_sb, wdqm)
    p.wait("tensor", "v", wdqm)
    for jc in range(DC):
        p.op("tensor", lambda eng, jc=jc:
             eng.matmul(ps[:, 0:512], wtmp[:, jc * 128:(jc + 1) * 128],
                        wdq_sb[:, 0:512],
                        start=True, stop=True, skip_group_check=True), sem="t")
        tm = p.mark("t")
        p.wait("vector", "t", tm)
        vm = p.op("vector", lambda eng, jc=jc: eng.tensor_copy(at2[jc][:], ps[:, 0:512]))
        p.wait("tensor", "v", vm)
    wukm = wdqm
    for jc in range(DC):
        wukm = loadw5(wukp_h[jc * 128:(jc + 1) * 128, :], 128, wuk_sb[jc], wukm)
    p.wait("tensor", "v", wukm)
    for ds in range(4):
        for jc in range(DC):
            p.op("tensor", lambda eng, ds=ds, jc=jc:
                 eng.matmul(ps[:, 1024 + ds * 128:1024 + (ds + 1) * 128],
                            at2[jc][:, ds * 128:(ds + 1) * 128], wuk_sb[jc][:],
                            start=(jc == 0), stop=(jc == DC - 1)), sem="t")
    tm = p.mark("t")
    p.wait("vector", "t", tm)
    for ds in range(4):
        vm = p.op("vector", lambda eng, ds=ds: eng.tensor_copy(
            kloc[ds][:], ps[:, 1024 + ds * 128:1024 + (ds + 1) * 128]))
    p.wait("tensor", "v", p.mark("v"))
    for hh in range(8):
        tile, off = hh // 2, 64 * (hh % 2)
        for d1c in range(DC):
            p.op("tensor", lambda eng, hh=hh, tile=tile, off=off, d1c=d1c:
                 eng.matmul(ps[:, 2048 + (d1c % 4) * 128:2048 + (d1c % 4) * 128 + 128],
                            at1[tile][off:off + 64, d1c * 128:(d1c + 1) * 128],
                            kloc[tile][off:off + 64, :],
                            start=True, stop=True, skip_group_check=True), sem="t")
            tm = p.mark("t")
            p.wait("vector", "t", tm)
            vm = p.op("vector", lambda eng, hh=hh, d1c=d1c: eng.tensor_copy(
                mown[d1c][:, hh * 128:(hh + 1) * 128],
                ps[:, 2048 + (d1c % 4) * 128:2048 + (d1c % 4) * 128 + 128]))
            p.wait("tensor", "v", vm)
    p.wait("vector", "t", p.mark("t"))
    wuvm = wukm
    for cc in range(DC):
        wuvm = loadw5(wuv_h[cc * 128:(cc + 1) * 128, :], 128, wuv_sb, wuvm)
        guard = loadw5(woTp_h[cc * 128:(cc + 1) * 128, 0:512], 512, wtmp, wuvm)
        p.wait("tensor", "v", guard)
        p.op("tensor", lambda eng, cc=cc:
             eng.matmul(ps[:, 3072:3584], wuv_sb[:], wtmp[:, 0:512],
                        start=(cc == 0), stop=(cc == DC - 1)), sem="t")
        wuvm = guard
        p.wait("vector", "t", p.mark("t"))
    tm = p.mark("t")
    p.wait("vector", "t", tm)
    vm = p.op("vector", lambda eng: eng.tensor_copy(vown[:], ps[:, 3072:3584]))
    p.wait("tensor", "v", vm)

    p.wait("tensor", "qs", in_ld)
    for kt in range(NKT):
        p.op("tensor", lambda eng, kt=kt:
             eng.matmul(ptb[:, (kt % 8) * 128:(kt % 8) * 128 + 128],
                        ckv_fm[:, kt * 128:(kt + 1) * 128], ident[:],
                        is_transpose=True, skip_group_check=True), sem="t")
        tm = p.mark("t")
        p.wait("vector", "t", tm)
        vm = p.op("vector", lambda eng, kt=kt: eng.tensor_copy(
            ckv_tm[kt][:], ptb[:, (kt % 8) * 128:(kt % 8) * 128 + 128]))
        p.wait("tensor", "v", vm)
    p.wait("scalar", "qs", in_ld)
    r2m = p.op("vector", lambda eng: eng.memset(r2row[:], 0.0))
    for tb in range(TT // 512):
        sm = p.op("scalar", lambda eng, tb=tb: eng.activation(
            sqrow[:], ckv_fm[:, tb * 512:(tb + 1) * 512], AF.Square))
        p.wait("tensor", "s", sm)
        tmx = p.op("tensor", lambda eng:
                   eng.matmul(ps[0:1, 2048:2560], ones_col[:], sqrow[:],
                              start=True, stop=True, skip_group_check=True),
                   sem="t")
        p.wait("vector", "t", tmx)
        p.wait("vector", "v", r2m)
        r2m = p.op("vector", lambda eng, tb=tb: eng.reduce_max(
            r2row[0:1, tb:tb + 1], ps[0:1, 2048:2560], axis=AX.X))
        p.wait("tensor", "v", r2m)
        p.wait("scalar", "t", p.mark("t"))
    p.wait("vector", "v", r2m)
    r2m = p.op("vector", lambda eng: eng.reduce_max(
        r2row[0:1, 0:1], r2row[0:1, 0:TT // 512], axis=AX.X))
    p.wait("tensor", "v", r2m)
    p.op("tensor", lambda eng: eng.matmul(
        ps[:, 2048:2049], ones_row[0:1, :], r2row[0:1, 0:1],
        start=True, stop=True, skip_group_check=True), sem="t")
    tm = p.mark("t")
    p.wait("vector", "t", tm)
    r2m = p.op("vector", lambda eng: eng.tensor_copy(r2pp[:], ps[:, 2048:2049]))
    p.wait("tensor", "v", r2m)

    dmy = 0
    iters = [(hh, qb) for hh in range(8) for qb in range(NQB)]

    def emit_proA(it, slot):
        hh, qb = iters[it]
        for dc in range(DC):
            fn = (lambda eng, dc=dc, hh=hh, qb=qb:
                  eng.matmul(ps[:, 2048:2560],
                             mown[dc][:, hh * 128:(hh + 1) * 128],
                             x3f[dc][:, qb * 512:(qb + 1) * 512],
                             start=(dc == 0), stop=(dc == DC - 1),
                             skip_group_check=True))
            if dc == DC - 1:
                p.op("tensor", fn, sem="t")
            else:
                p.opn("tensor", fn)
        tm = p.mark("t")
        p.wait("vector", "t", tm)
        vq = p.op("vector", lambda eng, slot=slot: eng.tensor_copy(
            qlb[slot][:], ps[:, 2048:2560]))
        return vq

    def emit_proB(it, slot, vq):
        hh, qb = iters[it]
        S0 = slot * 8
        p.wait("scalar", "v", vq)
        sm = p.op("scalar", lambda eng, slot=slot: eng.activation(
            sqq[:], qlb[slot][:], AF.Square))
        vd = p.op("vector", lambda eng, qb=qb, slot=slot: eng.tensor_mul(
            dql[:], qlb[slot][:], ckv_fm[:, qb * 512:(qb + 1) * 512]))
        p.wait("tensor", "v", vq)
        p.wait("tensor", "s", sm)
        p.wait("tensor", "v", vd)
        for qt in range(4):
            p.opn("tensor", lambda eng, qt=qt:
                  eng.matmul(ps[:, 2048 + qt:2048 + qt + 1],
                             sqq[:, qt * 128:(qt + 1) * 128], ones_col[:],
                             start=True, stop=True, skip_group_check=True))
            fn = (lambda eng, qt=qt:
                  eng.matmul(ps[:, 2052 + qt:2052 + qt + 1],
                             dql[:, qt * 128:(qt + 1) * 128], ones_col[:],
                             start=True, stop=True, skip_group_check=True))
            if qt == 3:
                p.op("tensor", fn, sem="t")
            else:
                p.opn("tensor", fn)
        tm = p.mark("t")
        p.wait("vector", "t", tm)
        vm = p.op("vector", lambda eng, S0=S0: eng.tensor_copy(
            stats[:, S0:S0 + 8], ps[:, 2048:2056]))
        p.wait("tensor", "v", vm)
        p.wait("vector", "v", vm)
        vm = p.op("vector", lambda eng, S0=S0: eng.tensor_scalar(
            numu[:, S0 + 4:S0 + 8], stats[:, S0:S0 + 4], r2pp[:, 0:1], None,
            ALU.mult))
        p.wait("scalar", "v", vm)
        sm = p.op("scalar", lambda eng, S0=S0: eng.activation(
            numu[:, S0 + 4:S0 + 8], numu[:, S0 + 4:S0 + 8], AF.Ln))
        p.wait("scalar", "s", sm)
        sm = p.op("scalar", lambda eng, S0=S0: eng.activation(
            numu[:, S0 + 4:S0 + 8], numu[:, S0 + 4:S0 + 8], AF.Exp, scale=0.5))
        p.wait("vector", "s", sm)
        vm = p.op("vector", lambda eng, S0=S0: eng.tensor_scalar(
            numu[:, S0 + 4:S0 + 8], numu[:, S0 + 4:S0 + 8], -0.125, 40.0,
            ALU.mult, ALU.add))
        p.wait("vector", "v", vm)
        vm = p.op("vector", lambda eng, S0=S0: eng.scalar_tensor_tensor(
            numu[:, S0:S0 + 4], stats[:, S0 + 4:S0 + 8], -0.125,
            numu[:, S0 + 4:S0 + 8], ALU.mult, ALU.min))
        return vm

    def emit_body(it, slot, numu_m, start_s, nextA, nextB, prev_epi):
        nonlocal dmy
        hh, qb = iters[it]
        S0 = slot * 8
        KB = qb + 1
        nch2 = (KB + 1) // 2
        stages = [(c2, qt) for c2 in range(nch2) for qt in range(4)]
        ns = len(stages)
        exp_m = {}
        trans_m = {}
        evac_m = {}
        qk_marks = {}
        first_pv = [True]
        p.wait("tensor", "s", start_s)

        def emit_qk(si):
            c2, qt = stages[si]
            buf = si % 2
            kb0 = c2 * 2
            nkb = min(2, KB - kb0)
            if si - 2 >= 0:
                p.wait("tensor", "s", exp_m[si - 2])
            for kb in range(nkb):
                fn = (lambda eng, qt=qt, kb=kb, kb0=kb0, buf=buf, slot=slot:
                      eng.matmul(ps[:, (buf * 2 + kb) * 512:(buf * 2 + kb + 1) * 512],
                                 qlb[slot][:, qt * 128:(qt + 1) * 128],
                                 ckv_fm[:, (kb0 + kb) * 512:(kb0 + kb + 1) * 512],
                                 start=True, stop=True, skip_group_check=True))
                if kb == nkb - 1:
                    p.op("tensor", fn, sem="t")
                else:
                    p.opn("tensor", fn)
            return p.mark("t")

        def emit_exp(si):
            c2, qt = stages[si]
            buf = si % 2
            kb0 = c2 * 2
            nkb = min(2, KB - kb0)
            ncols = nkb * 512
            qk_m = qk_marks[si]
            if kb0 + nkb == KB:
                p.wait("vector", "t", qk_m)
                vm = p.op("vector", lambda eng, qt=qt, nkb=nkb, buf=buf:
                          eng.tensor_add(
                              ps[:, (buf * 2 + nkb - 1) * 512:(buf * 2 + nkb) * 512],
                              ps[:, (buf * 2 + nkb - 1) * 512:(buf * 2 + nkb) * 512],
                              mask4[:, qt * 512:(qt + 1) * 512]))
                p.wait("scalar", "v", vm)
            else:
                p.wait("scalar", "t", qk_m)
            p.wait("scalar", "v", numu_m)
            if si - 2 in trans_m:
                p.wait("scalar", "t", trans_m[si - 2])
            sm = p.op("scalar", lambda eng, qt=qt, ncols=ncols, buf=buf, S0=S0:
                      eng.activation(pexp[:, buf * 1024:buf * 1024 + ncols],
                                     ps[:, buf * 1024:buf * 1024 + ncols],
                                     AF.Exp, bias=numu[:, S0 + qt:S0 + qt + 1],
                                     scale=0.125, accum_out=zpart[:]))
            exp_m[si] = sm
            p.wait("vector", "s", sm)
            qtg = qb * 4 + qt
            if c2 == 0:
                vm = p.op("vector", lambda eng, hh=hh, qtg=qtg:
                          eng.tensor_copy(zc[:, hh * 32 + qtg:hh * 32 + qtg + 1],
                                          zpart[:]))
            else:
                vm = p.op("vector", lambda eng, hh=hh, qtg=qtg:
                          eng.tensor_add(zc[:, hh * 32 + qtg:hh * 32 + qtg + 1],
                                         zc[:, hh * 32 + qtg:hh * 32 + qtg + 1],
                                         zpart[:]))
            p.wait("scalar", "v", vm)

        def emit_trans(si):
            c2, qt = stages[si]
            buf = si % 2
            kb0 = c2 * 2
            nkb = min(2, KB - kb0)
            nkt = nkb * 4
            p.wait("tensor", "s", exp_m[si])
            if si - 1 in evac_m:
                p.wait("tensor", "v", evac_m[si - 1])
            for kt in range(nkt):
                fn = (lambda eng, kt=kt, buf=buf:
                      eng.matmul(ptb[:, kt * 128:(kt + 1) * 128],
                                 pexp[:, buf * 1024 + kt * 128:buf * 1024 + (kt + 1) * 128],
                                 ident[:], is_transpose=True,
                                 skip_group_check=True))
                if kt == nkt - 1:
                    p.op("tensor", fn, sem="t")
                else:
                    p.opn("tensor", fn)
            trans_m[si] = p.mark("t")
            p.wait("vector", "t", trans_m[si])
            base = (c2 % 2) * 8
            vm = p.op("vector", lambda eng, base=base, nkt=nkt, qt=qt:
                      eng.tensor_copy(
                          pT[:].rearrange("p (kt q) -> p kt q", q=512)
                          [:, base:base + nkt, qt * 128:(qt + 1) * 128],
                          ptb[:, 0:nkt * 128].rearrange("p (kt q) -> p kt q", q=128)))
            evac_m[si] = vm

        def emit_pv(si):
            c2, qt = stages[si]
            kb0 = c2 * 2
            nkb = min(2, KB - kb0)
            nkt_c = nkb * 4
            base = (c2 % 2) * 8
            p.wait("tensor", "v", evac_m[si])
            last_c2 = (c2 == nch2 - 1)
            for kt in range(nkt_c):
                fn = (lambda eng, kt=kt, kb0=kb0, base=base, first=first_pv[0],
                      stop=(last_c2 and kt == nkt_c - 1):
                      eng.matmul(ps[:, 2560:3072], ckv_tm[kb0 * 4 + kt][:],
                                 pT[:, (base + kt) * 512:(base + kt + 1) * 512],
                                 start=(first and kt == 0),
                                 stop=stop, skip_group_check=True))
                if kt == nkt_c - 1:
                    p.op("tensor", fn, sem="t")
                else:
                    p.opn("tensor", fn)
            first_pv[0] = False

        for si in range(ns + 1):
            if si < ns:
                qk_marks[si] = emit_qk(si)
            if si == 1:
                if prev_epi is not None:
                    prev_epi()
                if nextA is not None:
                    nextA()
            if si - 1 >= 0:
                sj = si - 1
                emit_exp(sj)
                emit_trans(sj)
                if stages[sj][1] == 3:
                    emit_pv(sj)
        end_s = p.mark("s")

        def epi():
            nonlocal dmy
            tm = p.mark("t")
            p.wait("vector", "t", tm)
            vc = p.op("vector", lambda eng: eng.tensor_copy(ctx_sb[:], ps[:, 2560:3072]))
            p.wait("tensor", "v", vc)
            off = 64 * (hh % 2)
            if epi_state["vy"]:
                p.wait("tensor", "v", epi_state["vy"])
            p.op("tensor", lambda eng, hh=hh, off=off:
                 eng.matmul(ps[off:off + 64, 3072:3584],
                            vown[:, hh * 64:(hh + 1) * 64], ctx_sb[:],
                            start=True, stop=True, skip_group_check=True), sem="t")
            tm = p.mark("t")
            p.wait("vector", "t", tm)
            if dmy:
                p.wait("vector", "qg", dmy)
            vy = p.op("vector", lambda eng, off=off: eng.tensor_copy(
                y_sb[off:off + 64, :], ps[off:off + 64, 3072:3584]))
            epi_state["vy"] = vy
            p.wait("gpsimd", "v", vy)
            dmy = p.dma("gpsimd",
                        cc_gy_i.ap()[hh * 64:(hh + 1) * 64, qb * 512:(qb + 1) * 512],
                        y_sb[off:off + 64, :])
        if nextB is not None:
            nextB()
        return end_s, epi

    epi_state = {"vy": 0}
    vqA = {0: emit_proA(0, 0)}
    numu_ms = {0: emit_proB(0, 0, vqA[0])}
    start_s = p.mark("s")
    prev_epi = None
    for it in range(len(iters)):
        slot = it % 2
        nxtA = nxtB = None
        if it + 1 < len(iters):
            def mkA(itn, sl):
                return lambda: vqA.__setitem__(itn, emit_proA(itn, sl))
            def mkB(itn, sl):
                return lambda: numu_ms.__setitem__(
                    itn, emit_proB(itn, sl, vqA[itn]))
            nxtA = mkA(it + 1, (it + 1) % 2)
            nxtB = mkB(it + 1, (it + 1) % 2)
        start_s, prev_epi = emit_body(it, slot, numu_ms[it], start_s,
                                      nxtA, nxtB, prev_epi)
    prev_epi()

    vm = p.op("vector", lambda eng: eng.reciprocal(zc[:], zc[:]))
    p.wait("tensor", "v", vm)
    for hh in range(8):
        p.op("tensor", lambda eng, hh=hh:
             eng.matmul(ps[0:32, 2048:2176], zc[:, hh * 32:(hh + 1) * 32],
                        identf[:], is_transpose=True,
                        skip_group_check=True), sem="t")
        tm = p.mark("t")
        p.wait("vector", "t", tm)
        vm = p.op("vector", lambda eng, hh=hh: eng.tensor_copy(
            rzt[:], ps[0:32, 2048:2176]))
        p.wait("gpsimd", "v", vm)
        dmz = p.dma("gpsimd", cc_gz_i.ap()[hh * 32:(hh + 1) * 32, :], rzt[:])
        p.wait("tensor", "qg", dmz)
        p.wait("vector", "qg", dmz)
    p.wait("gpsimd", "qg", p.mark("qg"))
    p.wait("gpsimd", "qs", p.mark("qs"))
    gz_cm = gather(cc_gz_i, cc_gz_o)
    gy_cm = gather(cc_gy_i, cc_gy_o)
    p.barrier()
    gather(wfcT_i, wfcT_o, GRP8)
    wpj_cm = gather(wpjT_i, wpjT_o, GRP8)
    P5.close()

    P6 = _ctx.ExitStack()
    NT = T // 128
    x3tm = [p.sbuf(f"p6x3tm{i}", [128, D], BF16, stack=P6) for i in range(NT)]
    xq = [p.sbuf(f"p6xq{i}", [128, T], BF16, stack=P6) for i in range(DC)]
    hqT = [p.sbuf(f"p6hqT{i}", [128, T], BF16, stack=P6) for i in range(4)]
    wfcqT = [p.sbuf(f"p6wfcqT{i}", [128, 1024], BF16, stack=P6) for i in range(DC)]
    wpqT = [p.sbuf(f"p6wpqT{i}", [128, D], BF16, stack=P6) for i in range(8)]
    f32buf = p.sbuf("p6f32", [128, 1024], F32, stack=P6)
    f32b2 = p.sbuf("p6f32b", [128, 1024], F32, stack=P6)
    rza_sb = p.sbuf("p6rza", [16, T], BF16, stack=P6)
    rzb_sb = p.sbuf("p6rzb", [16, T], BF16, stack=P6)
    ind_sb = p.sbuf("p6ind", [16, D], BF16, stack=P6)
    bfc_sb = p.sbuf("p6bfc", [1, HID], BF16, stack=P6)
    bproj_pp = p.sbuf("p6bproj", [128, DC], F32, stack=P6)
    am = p.sbuf("p6am", [128, 32], F32, stack=P6)
    inv = p.sbuf("p6inv", [128, 32], F32, stack=P6)
    tq = p.sbuf("p6tq", [128, 1024], F32, stack=P6)
    xhat = p.sbuf("p6xhat", [128, D], BF16, stack=P6)
    ln4 = p.sbuf("p6ln4", [128, 12], F32, stack=P6)
    yA = [f32buf[:].bitcast(BF16), f32b2[:].bitcast(BF16)]
    yB = [sq[:], hqT[0][:]]

    p.dma("sync", ind_sb[:], ind_h)
    rz_flat = cc_gz_o.ap().rearrange("(h q) w -> h (q w)", q=32)
    p.dma("sync", rza_sb[:], rz_flat[:, 0:T])
    p.dma("sync", rzb_sb[:], rz_flat[:, T:TT])
    lm = p.dma("sync", stage[0:1, 0:1024], bfc_h[0:1, 0:1024])
    p.wait("vector", "qs", p.mark("qs"))
    vm = p.op("vector", lambda eng: eng.tensor_copy(bfc_sb[0:1, 0:1024],
                                                    stage[0:1, 0:1024]))
    for piece in range(1, 4):
        p.wait("sync", "v", vm)
        lm = p.dma("sync", stage[0:1, 0:1024],
                   bfc_h[0:1, piece * 1024:(piece + 1) * 1024])
        p.wait("vector", "qs", lm)
        vm = p.op("vector", lambda eng, piece=piece: eng.tensor_copy(
            bfc_sb[0:1, piece * 1024:(piece + 1) * 1024], stage[0:1, 0:1024]))
    for r in range(DC):
        p.dma("sync", bproj_pp[:, r:r + 1], bproj_h[r:r + 1, :].rearrange("o d -> d o"))
    lm = p.mark("qs")
    p.wait("vector", "qs", lm)
    vm = p.op("vector", lambda eng: eng.tensor_sub(rzb_sb[:], rzb_sb[:], rza_sb[:]))
    p.wait("vector", "v", vm)
    vm = p.op("vector", lambda eng: eng.scalar_tensor_tensor(
        rza_sb[:], rzb_sb[:], flag[0:16, 0:1], rza_sb[:], ALU.mult, ALU.add))
    p.wait("vector", "v", p.mark("v"))

    p.wait("tensor", "qs", p.mark("qs"))
    p.wait("tensor", "v", p.mark("v"))
    p.wait("sync", "c2", gy_cm)
    for dc in range(DC):
        for tb in range(T // 512):
            fn = (lambda eng, dc=dc, tb=tb:
                  eng.matmul(ps[:, tb * 512:(tb + 1) * 512],
                             ind_sb[:, dc * 128:(dc + 1) * 128],
                             rza_sb[:, tb * 512:(tb + 1) * 512],
                             start=True, stop=True, skip_group_check=True))
            if tb == T // 512 - 1:
                p.op("tensor", fn, sem="t")
            else:
                p.opn("tensor", fn)
        tm = p.mark("t")
        lmy = p.dma("sync", yA[0], cc_gy_o.ap()[dc * 128:(dc + 1) * 128, 0:T])
        lmy = p.dma("sync", yB[0], cc_gy_o.ap()[dc * 128:(dc + 1) * 128, T:TT])
        p.wait("vector", "t", tm)
        p.wait("vector", "qs", lmy)
        vm = p.op("vector", lambda eng: eng.tensor_sub(yB[0], yA[0], yB[0]))
        p.wait("vector", "v", vm)
        vm = p.op("vector", lambda eng: eng.tensor_scalar(
            yB[0], yB[0], flag[:, 0:1], None, ALU.mult))
        p.wait("vector", "v", vm)
        vm = p.op("vector", lambda eng: eng.tensor_sub(yA[0], yA[0], yB[0]))
        p.wait("vector", "v", vm)
        vm = p.op("vector", lambda eng: eng.tensor_mul(yA[0], yA[0], ps[:, 0:T]))
        p.wait("vector", "v", vm)
        vm = p.op("vector", lambda eng, dc=dc: eng.tensor_add(
            stream[dc][:], stream[dc][:], yA[0]))
        p.wait("tensor", "v", vm)
        p.wait("sync", "v", vm)
    x3_done = p.mark("v")

    p.wait("tensor", "v", x3_done)
    x3tm_m = [0] * NT
    for tt in range(NT):
        for dc in range(DC):
            fn = (lambda eng, tt=tt, dc=dc:
                  eng.matmul(ptb[:, dc * 128:(dc + 1) * 128],
                             stream[dc][:, tt * 128:(tt + 1) * 128], ident[:],
                             is_transpose=True, skip_group_check=True))
            if dc == DC - 1:
                p.op("tensor", fn, sem="t")
            else:
                p.opn("tensor", fn)
        tm = p.mark("t")
        p.wait("vector", "t", tm)
        vm = p.op("vector", lambda eng, tt=tt: eng.tensor_copy(
            x3tm[tt][:], ptb[:, 0:1024]))
        p.wait("tensor", "v", vm)
        x3tm_m[tt] = vm

    for tt in range(NT):
        p.wait("scalar", "v", x3tm_m[tt])
        sm = p.op("scalar", lambda eng, tt=tt: eng.activation(
            f32buf[:], x3tm[tt][:], AF.Square))
        vm = p.op("vector", lambda eng, tt=tt: eng.reduce_sum(
            ln4[:, 0:1], x3tm[tt][:], axis=AX.X))
        p.wait("vector", "s", sm)
        vm = p.op("vector", lambda eng: eng.reduce_sum(ln4[:, 1:2], f32buf[:], axis=AX.X))
        p.wait("vector", "v", vm)
        vm = p.op("vector", lambda eng: eng.tensor_scalar(
            ln4[:, 2:3], ln4[:, 0:1], 1.0 / D, None, ALU.mult))
        p.wait("scalar", "v", vm)
        sm = p.op("scalar", lambda eng: eng.activation(ln4[:, 3:4], ln4[:, 2:3], AF.Square))
        p.wait("vector", "s", sm)
        vm = p.op("vector", lambda eng: eng.scalar_tensor_tensor(
            ln4[:, 3:4], ln4[:, 1:2], 1.0 / D, ln4[:, 3:4], ALU.mult, ALU.subtract))
        p.wait("scalar", "v", vm)
        sm = p.op("scalar", lambda eng: eng.activation(ln4[:, 4:5], ln4[:, 3:4],
                                                       AF.Sqrt, bias=eps1c[:], scale=1.0))
        p.wait("vector", "s", sm)
        vm = p.op("vector", lambda eng: eng.reciprocal(ln4[:, 4:5], ln4[:, 4:5]))
        p.wait("vector", "v", vm)
        vm = p.op("vector", lambda eng: eng.scalar_tensor_tensor(
            ln4[:, 5:6], ln4[:, 2:3], -1.0, ln4[:, 4:5], ALU.mult, ALU.mult))
        p.wait("vector", "v", vm)
        vm = p.op("vector", lambda eng, tt=tt: eng.scalar_tensor_tensor(
            f32buf[:], x3tm[tt][:], ln4[:, 4:5],
            ln4[:, 5:6].broadcast_to([128, D]), ALU.mult, ALU.add))
        p.wait("vector", "v", vm)
        p.wait("vector", "s", p.mark("s"))
        emit_quant(p, xhat, D, am, inv, tq, src=f32buf)
        p.wait("tensor", "v", p.mark("v"))
        for dc in range(DC):
            fn = (lambda eng, tt=tt, dc=dc:
                  eng.matmul(ptb[:, dc * 128:(dc + 1) * 128],
                             xhat[:, dc * 128:(dc + 1) * 128], ident[:],
                             is_transpose=True, skip_group_check=True))
            if dc == DC - 1:
                p.op("tensor", fn, sem="t")
            else:
                p.opn("tensor", fn)
        tm = p.mark("t")
        p.wait("vector", "t", tm)
        for dc in range(DC):
            vm = p.op("vector", lambda eng, tt=tt, dc=dc: eng.tensor_copy(
                xq[dc][:, tt * 128:(tt + 1) * 128],
                ptb[:, dc * 128:(dc + 1) * 128]))
        p.wait("tensor", "v", vm)
        p.wait("scalar", "v", p.mark("v"))

    hgrp = x3tm
    p.wait("sync", "c2", wpj_cm)
    for grp in range(4):
        p.wait("sync", "t", p.mark("t"))
        for kc in range(DC):
            for hb in range(2):
                p.dma("sync", wfcqT[kc][:, hb * 512:(hb + 1) * 512],
                      wfcT_o.ap()[(2 * grp + hb) * 1024 + kc * 128:
                                  (2 * grp + hb) * 1024 + (kc + 1) * 128, :])
        for j in range(DC):
            for es in range(DC):
                p.dma("sync", wpqT[j][:, es * 128:(es + 1) * 128],
                      wpjT_o.ap()[es * 4096 + grp * 1024 + j * 128:
                                  es * 4096 + grp * 1024 + (j + 1) * 128, :])
        wln = p.mark("qs")
        p.wait("tensor", "qs", wln)
        p.wait("tensor", "s", p.mark("s"))
        p.wait("scalar", "v", p.mark("v"))
        p.wait("vector", "t", p.mark("t"))
        sm_h = {}
        qv_h = {}
        for tt in range(NT):
            pb = (tt % 2) * 1024
            fb = f32buf if tt % 2 == 0 else f32b2
            if tt - 2 in sm_h:
                p.wait("tensor", "s", sm_h[tt - 2])
            for kc in range(DC):
                for nb in range(2):
                    p.opn("tensor", lambda eng, tt=tt, kc=kc, nb=nb, pb=pb:
                          eng.matmul(ps[:, pb + nb * 512:pb + (nb + 1) * 512],
                                     xq[kc][:, tt * 128:(tt + 1) * 128],
                                     wfcqT[kc][:, nb * 512:(nb + 1) * 512],
                                     start=(kc == 0), stop=False,
                                     skip_group_check=True))
            for nb in range(2):
                fn = (lambda eng, tt=tt, nb=nb, grp=grp, pb=pb:
                      eng.matmul(ps[:, pb + nb * 512:pb + (nb + 1) * 512],
                                 ones_rowb[:],
                                 bfc_sb[0:1, grp * 1024 + nb * 512:grp * 1024 + (nb + 1) * 512],
                                 start=False, stop=True,
                                 skip_group_check=True))
                if nb == 1:
                    p.op("tensor", fn, sem="t")
                else:
                    p.opn("tensor", fn)
            tm = p.mark("t")
            p.wait("scalar", "t", tm)
            if tt - 2 in qv_h:
                p.wait("scalar", "v", qv_h[tt - 2])
            sm = p.op("scalar", lambda eng, tt=tt, pb=pb, fb=fb: eng.activation(
                fb[:], ps[:, pb:pb + 1024], AF.Gelu))
            sm_h[tt] = sm
            p.wait("vector", "s", sm)
            qv_h[tt] = emit_quant(p, hgrp[tt], D, am, inv, tq, src=fb)
        hq_done = p.mark("v")
        for half in range(2):
            p.wait("tensor", "v", hq_done if half == 0 else p.mark("v"))
            for tt in range(NT):
                for nc4 in range(4):
                    p.op("tensor", lambda eng, tt=tt, nc4=nc4, half=half:
                         eng.matmul(ptb[:, nc4 * 128:(nc4 + 1) * 128],
                                    hgrp[tt][:, half * 512 + nc4 * 128:half * 512 + (nc4 + 1) * 128],
                                    ident[:], is_transpose=True,
                                    skip_group_check=True), sem="t")
                tm = p.mark("t")
                p.wait("vector", "t", tm)
                for nc4 in range(4):
                    vm = p.op("vector", lambda eng, tt=tt, nc4=nc4: eng.tensor_copy(
                        hqT[nc4][:, tt * 128:(tt + 1) * 128],
                        ptb[:, nc4 * 128:(nc4 + 1) * 128]))
                p.wait("tensor", "v", vm)
            for es in range(DC):
                for nc4 in range(4):
                    for tb in range(T // 512):
                        fn = (lambda eng, es=es, nc4=nc4, tb=tb, half=half:
                              eng.matmul(ps[:, 2048 + tb * 512:2048 + (tb + 1) * 512],
                                         wpqT[half * 4 + nc4][:, es * 128:(es + 1) * 128],
                                         hqT[nc4][:, tb * 512:(tb + 1) * 512],
                                         start=(nc4 == 0), stop=(nc4 == 3),
                                         skip_group_check=True))
                        if nc4 == 3 and tb == T // 512 - 1:
                            p.op("tensor", fn, sem="t")
                        else:
                            p.opn("tensor", fn)
                tm = p.mark("t")
                p.wait("vector", "t", tm)
                if grp == 0 and half == 0:
                    vm = p.op("vector", lambda eng, es=es: eng.scalar_tensor_tensor(
                        stream[es][:], ps[:, 2048:4096], bproj_pp[:, es:es + 1],
                        stream[es][:], ALU.add, ALU.add))
                else:
                    vm = p.op("vector", lambda eng, es=es: eng.tensor_add(
                        stream[es][:], stream[es][:], ps[:, 2048:4096]))
                p.wait("tensor", "v", vm)

    p.wait("tensor", "v", p.mark("v"))
    dmo = 0
    for tt in range(NT):
        for dc in range(DC):
            fn = (lambda eng, tt=tt, dc=dc:
                  eng.matmul(ptb[:, dc * 128:(dc + 1) * 128],
                             stream[dc][:, tt * 128:(tt + 1) * 128], ident[:],
                             is_transpose=True, skip_group_check=True))
            if dc == DC - 1:
                p.op("tensor", fn, sem="t")
            else:
                p.opn("tensor", fn)
        tm = p.mark("t")
        p.wait("vector", "t", tm)
        if dmo:
            p.wait("vector", "qg", dmo)
        vm = p.op("vector", lambda eng: eng.tensor_copy(tq[:, 0:D], ptb[:, 0:1024]))
        p.wait("tensor", "v", vm)
        p.wait("gpsimd", "v", vm)
        dmo = p.dma("gpsimd", out_h[tt * 128:(tt + 1) * 128, :], tq[:, 0:D])
    p.wait("gpsimd", "qg", p.mark("qg"))
    P6.close()
    return p.finalize()



_CACHE = {}

def _get(name, builder):
    if name not in _CACHE:
        _CACHE[name] = builder()
    return _CACHE[name]


def _pc(a):
    return np.ascontiguousarray(np.concatenate([a[:, 512:], a[:, :512]], axis=1))


def _pr(a):
    return np.ascontiguousarray(np.concatenate([a[512:], a[:512]], axis=0))


def kernel_multi(**z):
    cores = list(range(8))
    B = np.asarray

    def shard(c):
        return c // 2, slice((c % 2) * T, (c % 2) * T + T)

    nc = _get("LA", build_LA)
    bias3 = np.concatenate([z["L0_attn_bg"], z["L0_attn_bv"],
                            z["L0_attn_bd"]]).reshape(3 * DC, 128)
    in_maps = []
    for c in cores:
        b, sl = shard(c)
        in_maps.append(dict(
            xT=np.ascontiguousarray(B(z["L0_attn_x"])[b, sl, :].T),
            Wg=B(z["L0_attn_Wg"]), Wv=B(z["L0_attn_Wv"]), Wd=B(z["L0_attn_Wd"]),
            wmix_row=B(z["L0_attn_w_rms_mix"])[None, :], bias3=bias3))
    rA = run_bass_kernel_spmd(nc, in_maps, core_ids=cores).results

    nc = _get("LB", build_LB)
    in_maps = []
    for c in cores:
        b, sl = shard(c)
        if c % 2 == 0:
            carry = np.zeros((128, DC), np.float32)
        else:
            carry = np.asarray(rA[c - 1]["h"]).astype(np.float32)[:, -1].reshape(DC, 128).T.copy()
        in_maps.append(dict(
            xT=np.ascontiguousarray(B(z["L0_attn_x"])[b, sl, :].T),
            xhat=np.asarray(rA[c]["xhat"]), h=np.asarray(rA[c]["h"]),
            Acum=np.asarray(rA[c]["Acum"]), carry=carry,
            wffn_row=B(z["L0_attn_w_rms_ffn"])[None, :],
            W_gate=B(z["L0_attn_W_gate"]), W_up=B(z["L0_attn_W_up"]),
            W_out=B(z["L0_attn_W_out"])))
    rB = run_bass_kernel_spmd(nc, in_maps, core_ids=cores).results

    nc = _get("LC", build_LC)
    mu2 = np.concatenate([z["L0_ffn_mu_r"], z["L0_ffn_mu_k"]]).reshape(2 * DC, 128)
    in_maps = []
    for c in cores:
        b, sl = shard(c)
        if c % 2 == 0:
            pcol = np.zeros((128, DC), dtype=NB)
        else:
            pcol = np.ascontiguousarray(
                np.asarray(rB[c - 1]["x1"])[:, -1].reshape(DC, 128).T)
        in_maps.append(dict(
            x1=np.asarray(rB[c]["x1"]), pcol=pcol,
            W_rT=np.ascontiguousarray(B(z["L0_ffn_W_r"]).T),
            W_kT=np.ascontiguousarray(B(z["L0_ffn_W_k"]).T),
            W_vT=np.ascontiguousarray(B(z["L0_ffn_W_v"]).T),
            mu2=mu2))
    rC = run_bass_kernel_spmd(nc, in_maps, core_ids=cores).results

    nc = _get("LD", build_LD)
    in_maps = []
    for c in cores:
        in_maps.append(dict(
            x2=np.asarray(rC[c]["x2"]),
            W_dkvT=np.ascontiguousarray(B(z["L1_attn_W_dkv"]).T)))
    rD = run_bass_kernel_spmd(nc, in_maps, core_ids=cores).results

    nc = _get("LE", build_LE)
    mask4 = np.zeros((4, 128, 512), np.float32)
    for qt in range(4):
        for r in range(128):
            mask4[qt, r, qt * 128 + r + 1:] = -1e30
    ident = np.eye(128, dtype=NB)
    identf = np.eye(128, dtype=np.float32)
    in_maps = []
    for c in cores:
        b, half = c // 2, c % 2
        xh3 = np.concatenate([np.asarray(rD[2 * b]["xhat3"]),
                              np.asarray(rD[2 * b + 1]["xhat3"])], axis=1)
        ckv = np.concatenate([np.asarray(rD[2 * b]["ckv"]),
                              np.asarray(rD[2 * b + 1]["ckv"])], axis=1)
        WuqT = np.ascontiguousarray(B(z["L1_attn_W_uq"]).T)
        WoT = np.ascontiguousarray(B(z["L1_attn_W_o"]).T)
        in_maps.append(dict(
            xhat3=xh3, ckv=ckv,
            W_dq=B(z["L1_attn_W_dq"]),
            W_uqT_p=WuqT if half == 0 else _pc(WuqT),
            W_dq_p=B(z["L1_attn_W_dq"]) if half == 0 else _pc(B(z["L1_attn_W_dq"])),
            W_uk_p=B(z["L1_attn_W_uk"]) if half == 0 else _pr(B(z["L1_attn_W_uk"])),
            W_uv=B(z["L1_attn_W_uv"]),
            W_oT_p=WoT if half == 0 else _pc(WoT),
            ident=ident, identf=identf, mask4=mask4))
    rE = run_bass_kernel_spmd(nc, in_maps, core_ids=cores).results

    nc = _get("LF", build_LF)
    ind = np.zeros((16, D), dtype=NB)
    for e in range(D):
        ind[e // 64, e] = 1.0
    in_maps = []
    for c in cores:
        b, half = c // 2, c % 2
        sl = slice(half * T, half * T + T)
        ym = np.concatenate([np.asarray(rE[2 * b]["y"])[:, sl],
                             np.asarray(rE[2 * b + 1]["y"])[:, sl]], axis=0)
        rza = np.concatenate([np.asarray(rE[2 * b]["rz"]).reshape(8, 4096)[:, sl],
                              np.asarray(rE[2 * b + 1]["rz"]).reshape(8, 4096)[:, sl]],
                             axis=0)
        in_maps.append(dict(
            x2=np.asarray(rC[c]["x2"]), ym=np.ascontiguousarray(ym),
            rza=np.ascontiguousarray(rza), ind=ind,
            w_fc=B(z["L1_ffn_w_fc"]), bfc_row=B(z["L1_ffn_b_fc"])[None, :],
            w_proj=B(z["L1_ffn_w_proj"]),
            bproj=B(z["L1_ffn_b_proj"]).reshape(DC, 128),
            ident=ident))
    rF = run_bass_kernel_spmd(nc, in_maps, core_ids=cores).results

    out = np.empty((4, 2 * T, D), np.float32)
    for c in cores:
        b, sl = shard(c)
        out[b, sl, :] = np.asarray(rF[c]["out"])
    return out


def kernel(**z):
    cores = list(range(8))
    B = np.asarray
    nc = _get("FUSED", build_fused)
    mask4 = np.zeros((4, 128, 512), np.float32)
    for qt in range(4):
        for r in range(128):
            mask4[qt, r, qt * 128 + r + 1:] = -1e30
    ident = np.eye(128, dtype=NB)
    identf = np.eye(128, dtype=np.float32)
    ind = np.zeros((16, D), dtype=NB)
    for e in range(D):
        ind[e // 64, e] = 1.0
    bias3 = np.concatenate([z["L0_attn_bg"], z["L0_attn_bv"],
                            z["L0_attn_bd"]]).reshape(3 * DC, 128)
    mu2 = np.concatenate([z["L0_ffn_mu_r"], z["L0_ffn_mu_k"]]).reshape(2 * DC, 128)
    WuqT = np.ascontiguousarray(B(z["L1_attn_W_uq"]).T)
    WoT = np.ascontiguousarray(B(z["L1_attn_W_o"]).T)
    Wdq = B(z["L1_attn_W_dq"])
    Wuk = B(z["L1_attn_W_uk"])
    in_maps = []
    for c in cores:
        b, half = c // 2, c % 2
        sl = slice(half * T, half * T + T)
        in_maps.append(dict(
            xT=np.ascontiguousarray(B(z["L0_attn_x"])[b, sl, :].T),
            Wg=B(z["L0_attn_Wg"]), Wv=B(z["L0_attn_Wv"]), Wd=B(z["L0_attn_Wd"]),
            wmix_row=B(z["L0_attn_w_rms_mix"])[None, :], bias3=bias3,
            wffn_row=B(z["L0_attn_w_rms_ffn"])[None, :],
            W_gate=B(z["L0_attn_W_gate"]), W_up=B(z["L0_attn_W_up"]),
            W_out=B(z["L0_attn_W_out"]),
            W_rT=np.ascontiguousarray(B(z["L0_ffn_W_r"]).T),
            W_kT=np.ascontiguousarray(B(z["L0_ffn_W_k"]).T),
            W_vT=np.ascontiguousarray(B(z["L0_ffn_W_v"]).T),
            mu2=mu2,
            W_dkvT=np.ascontiguousarray(B(z["L1_attn_W_dkv"]).T),
            W_dq=Wdq,
            W_uqT_p=WuqT if half == 0 else _pc(WuqT),
            W_dq_p=Wdq if half == 0 else _pc(Wdq),
            W_uk_p=Wuk if half == 0 else _pr(Wuk),
            W_uv=B(z["L1_attn_W_uv"]),
            W_oT_p=WoT if half == 0 else _pc(WoT),
            ident=ident, identf=identf, mask4=mask4,
            wfc_sh=np.ascontiguousarray(B(z["L1_ffn_w_fc"])[c * 512:(c + 1) * 512, :]),
            bfc_row=B(z["L1_ffn_b_fc"])[None, :],
            wpj_sh=np.ascontiguousarray(B(z["L1_ffn_w_proj"])[c * 128:(c + 1) * 128, :]),
            bproj=B(z["L1_ffn_b_proj"]).reshape(DC, 128),
            ind=ind,
            flag=np.full((128, 1), float(half), np.float32),
        ))
    res = run_bass_kernel_spmd(nc, in_maps, core_ids=cores).results
    out = np.empty((4, 2 * T, D), np.float32)
    for c in cores:
        b, sl = c // 2, slice((c % 2) * T, (c % 2) * T + T)
        out[b, sl, :] = np.asarray(res[c]["out"])
    return out


# version 18
# speedup vs baseline: 1.0487x; 1.0487x over previous
import contextlib
import numpy as np
import ml_dtypes
import concourse.bass as bass
import concourse.mybir as mybir
from concourse.bass_utils import run_bass_kernel_spmd

F32 = mybir.dt.float32
BF16 = mybir.dt.bfloat16
I32 = mybir.dt.int32
AF = mybir.ActivationFunctionType
ALU = mybir.AluOpType
AX = mybir.AxisListType

MAGIC = 12582912.0
NB = np.dtype(ml_dtypes.bfloat16)

D = 1024
T = 2048
DC = D // 128
FF = 3072
HID = 4096


class Prog:
    ENG = ("sync", "tensor", "vector", "scalar", "gpsimd")

    def __init__(self):
        self.nc = bass.Bass("TRN2", target_bir_lowering=False, debug=False)
        self.stack = contextlib.ExitStack()
        self.ops = {e: [] for e in self.ENG}
        self.semh = {}
        self.semc = {}

    def sem(self, name):
        if name not in self.semh:
            self.semh[name] = self.stack.enter_context(self.nc.semaphore(name))
            self.semc[name] = 0
        return self.semh[name]

    def sbuf(self, name, shape, dtype, stack=None):
        return (stack or self.stack).enter_context(
            self.nc.sbuf_tensor(name, list(shape), dtype))

    def psum(self, name, shape, dtype=F32):
        return self.stack.enter_context(self.nc.psum_tensor(name, list(shape), dtype))

    def dram(self, name, shape, dtype, kind):
        return self.nc.dram_tensor(name, list(shape), dtype, kind=kind)

    def raw(self, engine, fn):
        self.ops[engine].append(fn)

    def wait(self, engine, sem, val):
        if val <= 0:
            return
        h = self.sem(sem)
        self.ops[engine].append(lambda eng, h=h, v=val: eng.wait_ge(h, v))

    def op(self, engine, fn, sem=None, k=None):
        if sem is None:
            sem = {"sync": "qs", "gpsimd": "qg"}.get(engine, engine[0])
        if k is None:
            k = 16 if sem.startswith("q") or sem == "cc" else 1
        self.sem(sem)
        self.semc[sem] += k
        h = self.semh[sem]
        self.ops[engine].append(lambda eng, h=h, k=k, fn=fn: fn(eng).then_inc(h, k))
        return self.semc[sem]

    def opn(self, engine, fn):
        self.ops[engine].append(lambda eng, fn=fn: fn(eng))

    def dma(self, queue, out, in_, sem=None):
        return self.op(queue, lambda eng, o=out, i=in_: eng.dma_start(out=o, in_=i),
                       sem=sem)

    def mark(self, sem):
        return self.semc.get(sem, 0)

    def barrier(self, engines=None):
        engines = engines or self.ENG
        marks = [(s, c) for s, c in self.semc.items() if c > 0]
        for e in engines:
            for s, c in marks:
                self.wait(e, s, c)

    def finalize(self):
        with self.nc.Block() as block:
            for e in self.ENG:
                ops = self.ops[e]
                if not ops:
                    continue
                def body(eng, ops=ops):
                    for f in ops:
                        f(eng)
                getattr(block, e)(body)
        self.stack.close()
        return self.nc



def emit_ln_stats(p, ps, xb, sq, ones_col):
    for kc in range(DC):
        sm = p.op("scalar", lambda eng, kc=kc: eng.activation(sq[:], xb[kc][:], AF.Square))
        p.wait("tensor", "s", sm)
        for tb in range(T // 512):
            p.opn("tensor", lambda eng, kc=kc, tb=tb:
                  eng.matmul(ps[0:1, tb * 512:(tb + 1) * 512], ones_col[:],
                             xb[kc][:, tb * 512:(tb + 1) * 512],
                             start=(kc == 0), stop=(kc == DC - 1)))
            if tb < T // 512 - 1:
                p.opn("tensor", lambda eng, kc=kc, tb=tb:
                      eng.matmul(ps[0:1, T + tb * 512:T + (tb + 1) * 512],
                                 ones_col[:], sq[:, tb * 512:(tb + 1) * 512],
                                 start=(kc == 0), stop=(kc == DC - 1)))
            else:
                p.op("tensor", lambda eng, kc=kc, tb=tb:
                     eng.matmul(ps[0:1, T + tb * 512:T + (tb + 1) * 512],
                                ones_col[:], sq[:, tb * 512:(tb + 1) * 512],
                                start=(kc == 0), stop=(kc == DC - 1)), sem="t")
        p.wait("scalar", "t", p.mark("t"))
    return p.mark("t")


def emit_ln_rows(p, ps, R, eps1_ap, eps2_ap, combine_rms):
    mean, var, rstd, rho, tmp = (R[k] for k in ("mean", "var", "rstd", "rho", "tmp"))
    p.op("vector", lambda eng: eng.tensor_scalar(mean[:], ps[0:1, 0:T],
                                                 1.0 / D, None, ALU.mult))
    vm = p.op("vector", lambda eng: eng.tensor_scalar(tmp[:], ps[0:1, T:2 * T],
                                                      1.0 / D, None, ALU.mult))
    p.wait("scalar", "v", vm)
    sm = p.op("scalar", lambda eng: eng.activation(var[:], mean[:], AF.Square))
    p.wait("vector", "s", sm)
    vm = p.op("vector", lambda eng: eng.tensor_sub(var[:], tmp[:], var[:]))
    p.wait("scalar", "v", vm)
    sm = p.op("scalar", lambda eng: eng.activation(tmp[:], var[:], AF.Sqrt,
                                                   bias=eps1_ap, scale=1.0))
    p.wait("vector", "s", sm)
    vm = p.op("vector", lambda eng: eng.reciprocal(rstd[:], tmp[:]))
    if combine_rms:
        p.wait("scalar", "v", vm)
        sm = p.op("scalar", lambda eng: eng.activation(tmp[:], rstd[:], AF.Square))
        p.wait("vector", "s", sm)
        vm = p.op("vector", lambda eng: eng.tensor_mul(tmp[:], tmp[:], var[:]))
        p.wait("scalar", "v", vm)
        sm = p.op("scalar", lambda eng: eng.activation(tmp[:], tmp[:], AF.Sqrt,
                                                       bias=eps2_ap, scale=1.0))
        p.wait("vector", "s", sm)
        vm = p.op("vector", lambda eng: eng.reciprocal(tmp[:], tmp[:]))
        p.wait("vector", "v", vm)
        vm = p.op("vector", lambda eng: eng.tensor_mul(rho[:], rstd[:], tmp[:]))
        p.wait("vector", "v", vm)
        vm = p.op("vector", lambda eng: eng.scalar_tensor_tensor(
            var[:], mean[:], -1.0, rho[:], ALU.mult, ALU.mult))
    vm = p.op("vector", lambda eng: eng.scalar_tensor_tensor(
        tmp[:], mean[:], -1.0, rstd[:], ALU.mult, ALU.mult))
    return vm


def emit_ln_rows_plain(p, ps, mean, rstd, tmp, eps1_ap):
    p.op("vector", lambda eng: eng.tensor_scalar(mean[:], ps[0:1, 0:T],
                                                 1.0 / D, None, ALU.mult))
    vm = p.op("vector", lambda eng: eng.tensor_scalar(tmp[:], ps[0:1, T:2 * T],
                                                      1.0 / D, None, ALU.mult))
    p.wait("scalar", "v", vm)
    sm = p.op("scalar", lambda eng: eng.activation(rstd[:], mean[:], AF.Square))
    p.wait("vector", "s", sm)
    vm = p.op("vector", lambda eng: eng.tensor_sub(tmp[:], tmp[:], rstd[:]))
    p.wait("scalar", "v", vm)
    sm = p.op("scalar", lambda eng: eng.activation(tmp[:], tmp[:], AF.Sqrt,
                                                   bias=eps1_ap, scale=1.0))
    p.wait("vector", "s", sm)
    def _recip(eng):
        with p.nc.allow_low_precision(reason="bf16 LN rows"):
            return eng.reciprocal(rstd[:], tmp[:])
    vm = p.op("vector", _recip)
    p.wait("vector", "v", vm)
    vm = p.op("vector", lambda eng: eng.scalar_tensor_tensor(
        tmp[:], mean[:], -1.0, rstd[:], ALU.mult, ALU.mult))
    return vm


def emit_ln_rows4(p, ps, mrow, var, rstd, rho, eps1_ap, eps2_ap):
    vm = p.op("vector", lambda eng: eng.tensor_scalar(mrow[:], ps[0:1, 0:T],
                                                      1.0 / D, None, ALU.mult))
    p.wait("scalar", "v", vm)
    sm = p.op("scalar", lambda eng: eng.activation(var[:], mrow[:], AF.Square))
    p.wait("vector", "s", sm)
    vm = p.op("vector", lambda eng: eng.scalar_tensor_tensor(
        var[:], ps[0:1, T:2 * T], 1.0 / D, var[:], ALU.mult, ALU.subtract))
    p.wait("scalar", "v", vm)
    sm = p.op("scalar", lambda eng: eng.activation(rho[:], var[:], AF.Sqrt,
                                                   bias=eps1_ap, scale=1.0))
    p.wait("vector", "s", sm)
    vm = p.op("vector", lambda eng: eng.reciprocal(rstd[:], rho[:]))
    p.wait("scalar", "v", vm)
    sm = p.op("scalar", lambda eng: eng.activation(rho[:], rstd[:], AF.Square))
    p.wait("vector", "s", sm)
    vm = p.op("vector", lambda eng: eng.tensor_mul(rho[:], rho[:], var[:]))
    p.wait("scalar", "v", vm)
    sm = p.op("scalar", lambda eng: eng.activation(rho[:], rho[:], AF.Sqrt,
                                                   bias=eps2_ap, scale=1.0))
    p.wait("vector", "s", sm)
    vm = p.op("vector", lambda eng: eng.reciprocal(rho[:], rho[:]))
    p.wait("vector", "v", vm)
    vm = p.op("vector", lambda eng: eng.tensor_mul(rho[:], rstd[:], rho[:]))
    p.wait("vector", "v", vm)
    vm = p.op("vector", lambda eng: eng.scalar_tensor_tensor(
        var[:], mrow[:], -1.0, rho[:], ALU.mult, ALU.mult))
    vm = p.op("vector", lambda eng: eng.scalar_tensor_tensor(
        mrow[:], mrow[:], -1.0, rstd[:], ALU.mult, ALU.mult))
    return vm


def emit_ln_rows_plain3(p, ps, scr, rstd, shp, eps1_ap):
    vm = p.op("vector", lambda eng: eng.tensor_scalar(shp[:], ps[0:1, 0:T],
                                                      1.0 / D, None, ALU.mult))
    p.wait("scalar", "v", vm)
    sm = p.op("scalar", lambda eng: eng.activation(scr[:], shp[:], AF.Square))
    p.wait("vector", "s", sm)
    vm = p.op("vector", lambda eng: eng.scalar_tensor_tensor(
        scr[:], ps[0:1, T:2 * T], 1.0 / D, scr[:], ALU.mult, ALU.subtract))
    p.wait("scalar", "v", vm)
    sm = p.op("scalar", lambda eng: eng.activation(scr[:], scr[:], AF.Sqrt,
                                                   bias=eps1_ap, scale=1.0))
    p.wait("vector", "s", sm)
    vm = p.op("vector", lambda eng: eng.reciprocal(rstd[:], scr[:]))
    p.wait("vector", "v", vm)
    vm = p.op("vector", lambda eng: eng.scalar_tensor_tensor(
        shp[:], shp[:], -1.0, rstd[:], ALU.mult, ALU.mult))
    return vm


def emit_row_hilo(p, row, dst):
    return p.op("vector", lambda eng: eng.tensor_copy(dst[0:1, 0:T], row[0:1, 0:T]))


def emit_bcast_apply(p, ps, w_row_slice, scale_row, shift_row, xin, out_tile,
                     pre_wait=(), rows_b=None, ones_b=None):
    if rows_b is not None:
        sc, sh = rows_b
        for tb in range(T // 512):
            for src, off in ((sc, 0), (sh, T)):
                fn = (lambda eng, tb=tb, src=src, off=off:
                      eng.matmul(ps[:, off + tb * 512:off + (tb + 1) * 512],
                                 ones_b, src[0:1, tb * 512:(tb + 1) * 512],
                                 start=True, stop=True, skip_group_check=True))
                if tb == T // 512 - 1 and off == T:
                    p.op("tensor", fn, sem="t")
                else:
                    p.opn("tensor", fn)
    else:
        for tb in range(T // 512):
            p.opn("tensor", lambda eng, tb=tb:
                  eng.matmul(ps[:, tb * 512:(tb + 1) * 512], w_row_slice,
                             scale_row[0:1, tb * 512:(tb + 1) * 512],
                             start=True, stop=True, skip_group_check=True))
            if tb < T // 512 - 1:
                p.opn("tensor", lambda eng, tb=tb:
                      eng.matmul(ps[:, T + tb * 512:T + (tb + 1) * 512], w_row_slice,
                                 shift_row[0:1, tb * 512:(tb + 1) * 512],
                                 start=True, stop=True, skip_group_check=True))
            else:
                p.op("tensor", lambda eng, tb=tb:
                     eng.matmul(ps[:, T + tb * 512:T + (tb + 1) * 512], w_row_slice,
                                shift_row[0:1, tb * 512:(tb + 1) * 512],
                                start=True, stop=True, skip_group_check=True), sem="t")
    tm = p.mark("t")
    p.wait("vector", "t", tm)
    for sem, val in pre_wait:
        p.wait("vector", sem, val)
    p.op("vector", lambda eng: eng.tensor_mul(out_tile[:], xin[:], ps[:, 0:T]))
    vm = p.op("vector", lambda eng: eng.tensor_add(out_tile[:], out_tile[:],
                                                   ps[:, T:2 * T]))
    p.wait("tensor", "v", vm)
    return vm


def emit_load_w_bf16(p, hbm_ap, kdim, ndim, stage, out_tiles, stage_guard=0):
    vm = stage_guard
    for kc in range(kdim // 128):
        if vm:
            p.wait("sync", "v", vm)
        lm = p.dma("sync", stage[:, 0:ndim], hbm_ap[kc * 128:(kc + 1) * 128, :])
        p.wait("vector", "qs", lm)
        vm = p.op("vector", lambda eng, kc=kc: eng.tensor_copy(
            out_tiles[kc][:], stage[:, 0:ndim]))
    return vm


def emit_load_xbf(p, xT, stage, xb):
    vm = 0
    for kc in range(DC):
        if vm:
            p.wait("sync", "v", vm)
        lm = p.dma("sync", stage[:, 0:T], xT[kc * 128:(kc + 1) * 128, :])
        p.wait("vector", "qs", lm)
        vm = p.op("vector", lambda eng, kc=kc: eng.tensor_copy(xb[kc][:],
                                                               stage[:, 0:T]))
    return vm



def build_LA(dbg=False):
    p = Prog()
    xT = p.dram("xT", [D, T], F32, "ExternalInput").ap()
    wg_h = p.dram("Wg", [D, D], F32, "ExternalInput").ap()
    wv_h = p.dram("Wv", [D, D], F32, "ExternalInput").ap()
    wd_h = p.dram("Wd", [D, D], F32, "ExternalInput").ap()
    wmix_h = p.dram("wmix_row", [1, D], F32, "ExternalInput").ap()
    bias3_h = p.dram("bias3", [3 * DC, 128], F32, "ExternalInput").ap()
    xhat_o = p.dram("xhat", [D, T], BF16, "ExternalOutput").ap()
    h_o = p.dram("h", [D, T], BF16, "ExternalOutput").ap()
    A_o = p.dram("Acum", [D, T], BF16, "ExternalOutput").ap()

    xb = [p.sbuf(f"xb{i}", [128, T], BF16) for i in range(DC)]
    xh = p.sbuf("xh", [128, T], BF16)
    stage = p.sbuf("stage", [128, T], F32)
    wbf = {w: [p.sbuf(f"w{w}{i}", [128, D], BF16) for i in range(DC)]
           for w in "gvd"}
    sq = p.sbuf("sq", [128, T], BF16)
    ones_col = p.sbuf("ones_col", [128, 1], BF16)
    ones_row = p.sbuf("ones_row", [1, 128], F32)
    w_row = p.sbuf("w_row", [1, D], F32)
    bias_pp = p.sbuf("bias_pp", [128, 3 * DC], F32)
    R = {k: p.sbuf("row_" + k, [1, T], F32)
         for k in ("mean", "var", "rstd", "rho", "tmp")}
    eps1 = p.sbuf("eps1", [1, 1], F32)
    eps2 = p.sbuf("eps2", [1, 1], F32)
    sg = p.sbuf("sg", [128, T], BF16)
    tv = p.sbuf("tv", [128, T], BF16)
    a_f = p.sbuf("a_f", [128, T], F32)
    xs_f = p.sbuf("xs_f", [128, T], F32)
    hA = p.sbuf("hA", [128, 2 * T], BF16)
    ps = p.psum("ps", [128, 4096], F32)

    p.op("vector", lambda eng: eng.memset(ones_col[:], 1.0))
    p.op("vector", lambda eng: eng.memset(ones_row[:], 1.0))
    p.op("vector", lambda eng: eng.memset(eps1[:], 1e-5))
    p.op("vector", lambda eng: eng.memset(eps2[:], 1e-6))
    p.dma("sync", w_row[:], wmix_h)
    for r in range(3 * DC):
        p.dma("sync", bias_pp[:, r:r + 1],
              bias3_h[r:r + 1, :].rearrange("o d -> d o"))
    ldv = p.mark("qs")

    xb_done = emit_load_xbf(p, xT, stage, xb)
    if dbg:
        dbg_o = p.dram("dbg", [6, T], F32, "ExternalOutput").ap()
        dbg_xb = p.dram("dbg_xb", [D, T], BF16, "ExternalOutput").ap()
        p.wait("gpsimd", "v", xb_done)
        for kc in range(DC):
            p.dma("gpsimd", dbg_xb[kc * 128:(kc + 1) * 128, :], xb[kc][:])

    p.wait("scalar", "v", xb_done)
    p.wait("scalar", "qs", ldv)
    tm = emit_ln_stats(p, ps, xb, sq, ones_col)
    p.wait("vector", "t", tm)
    if dbg:
        p.wait("vector", "t", tm)
        sc = p.sbuf("dbg_sq", [1, 2 * T], F32)
        dvm = p.op("vector", lambda eng: eng.tensor_copy(sc[:], ps[0:1, 0:2 * T]))
        p.wait("gpsimd", "v", dvm)
        p.dma("gpsimd", dbg_o[0:1, :], sc[0:1, 0:T])
        p.dma("gpsimd", dbg_o[1:2, :], sc[0:1, T:2 * T])
    rowm = emit_ln_rows(p, ps, R, eps1[:], eps2[:], combine_rms=True)
    p.wait("tensor", "v", rowm)
    if dbg:
        p.wait("gpsimd", "v", rowm)
        p.dma("gpsimd", dbg_o[2:3, :], R["rstd"][:])
        p.dma("gpsimd", dbg_o[3:4, :], R["tmp"][:])
        p.dma("gpsimd", dbg_o[4:5, :], R["rho"][:])
        dmm = p.dma("gpsimd", dbg_o[5:6, :], R["var"][:])
        p.wait("tensor", "qg", dmm)

    dm_x = 0
    for dc in range(DC):
        vm = emit_bcast_apply(p, ps, ones_row[0:1, 0:128], R["rstd"], R["tmp"],
                              xb[dc], xh, pre_wait=(("qg", dm_x),))
        p.wait("gpsimd", "v", vm)
        dm_x = p.dma("gpsimd", xhat_o[dc * 128:(dc + 1) * 128, :], xh[:])
        emit_bcast_apply(p, ps, w_row[0:1, dc * 128:(dc + 1) * 128],
                         R["rho"], R["var"], xb[dc], xb[dc])
    xn_done = p.mark("v")

    wm = xb_done
    for w, hbm in zip("gvd", (wg_h, wv_h, wd_h)):
        wm = emit_load_w_bf16(p, hbm, D, D, stage, wbf[w], stage_guard=wm)
    p.wait("tensor", "v", wm)

    def proj(w, dc, b0):
        for kc in range(DC):
            for tb in range(T // 512):
                p.op("tensor", lambda eng, kc=kc, tb=tb, dc=dc, b0=b0, w=w:
                     eng.matmul(ps[:, (b0 + tb) * 512:(b0 + tb + 1) * 512],
                                wbf[w][kc][:, dc * 128:(dc + 1) * 128],
                                xb[kc][:, tb * 512:(tb + 1) * 512],
                                start=(kc == 0), stop=(kc == DC - 1)), sem="t")
        return p.mark("t")

    for dc in range(DC):
        tm = proj("g", dc, 0)
        p.wait("scalar", "t", tm)
        smg = p.op("scalar", lambda eng, dc=dc: eng.activation(
            sg[:], ps[:, 0:T], AF.Sigmoid, bias=bias_pp[:, dc:dc + 1], scale=1.0))
        tm = proj("v", dc, 4)
        p.wait("scalar", "t", tm)
        smv = p.op("scalar", lambda eng, dc=dc: eng.activation(
            tv[:], ps[:, T:2 * T], AF.Tanh,
            bias=bias_pp[:, DC + dc:DC + dc + 1], scale=1.0))
        p.wait("tensor", "s", smg)
        tm = proj("d", dc, 0)
        p.wait("scalar", "t", tm)
        smd = p.op("scalar", lambda eng, dc=dc: eng.activation(
            stage[:, 0:T], ps[:, 0:T], AF.Sigmoid,
            bias=bias_pp[:, 2 * DC + dc:2 * DC + dc + 1], scale=1.0))
        p.wait("tensor", "s", smd)
        p.wait("tensor", "s", smv)
        p.wait("vector", "s", smd)
        p.op("vector", lambda eng: eng.tensor_scalar(
            a_f[:], stage[:, 0:T], 0.998, 0.001, ALU.mult, ALU.add))
        p.op("vector", lambda eng: eng.tensor_mul(xs_f[:], sg[:], tv[:]))
        p.op("vector", lambda eng: eng.tensor_tensor_scan(
            hA[:, 0:T], a_f[:], xs_f[:], 0.0, ALU.mult, ALU.add))
        vm = p.op("vector", lambda eng: eng.tensor_tensor_scan(
            hA[:, T:2 * T], a_f[:], a_f[:], 1.0, ALU.mult, ALU.bypass))
        p.wait("gpsimd", "v", vm)
        p.dma("gpsimd", h_o[dc * 128:(dc + 1) * 128, :], hA[:, 0:T])
        dm = p.dma("gpsimd", A_o[dc * 128:(dc + 1) * 128, :], hA[:, T:2 * T])
        p.wait("vector", "qg", dm)
        p.wait("scalar", "qg", dm)
    p.wait("gpsimd", "qg", p.mark("qg"))
    return p.finalize()



def build_LB():
    GRP = 6
    NGRP = FF // 128 // GRP
    p = Prog()
    xT = p.dram("xT", [D, T], F32, "ExternalInput").ap()
    xhat_h = p.dram("xhat", [D, T], BF16, "ExternalInput").ap()
    h_h = p.dram("h", [D, T], BF16, "ExternalInput").ap()
    A_h = p.dram("Acum", [D, T], BF16, "ExternalInput").ap()
    carry_h = p.dram("carry", [128, DC], F32, "ExternalInput").ap()
    wffn_h = p.dram("wffn_row", [1, D], F32, "ExternalInput").ap()
    wgate_h = p.dram("W_gate", [D, FF], F32, "ExternalInput").ap()
    wup_h = p.dram("W_up", [D, FF], F32, "ExternalInput").ap()
    wout_h = p.dram("W_out", [FF, D], F32, "ExternalInput").ap()
    x1_o = p.dram("x1", [D, T], BF16, "ExternalOutput").ap()

    xb = [p.sbuf(f"xb{i}", [128, T], BF16) for i in range(DC)]
    xm = [p.sbuf(f"xm{i}", [128, T], BF16) for i in range(DC)]
    yn = [p.sbuf(f"yn{i}", [128, T], BF16) for i in range(DC)]
    da = [p.sbuf(f"da{i}", [128, T], BF16) for i in range(DC)]
    mt = [p.sbuf(f"mt{i}", [128, T], BF16) for i in range(GRP)]
    wo_sl = [p.sbuf(f"wo{i}", [128, D], BF16) for i in range(GRP)]
    wg_sl = p.sbuf("wg_sl", [128, D], BF16)
    wu_sl = p.sbuf("wu_sl", [128, D], BF16)
    stage = p.sbuf("stage", [128, T], F32)
    tmpb = p.sbuf("tmpb", [128, T], BF16)
    sq = p.sbuf("sq", [128, T], BF16)
    ones_col = p.sbuf("ones_col", [128, 1], BF16)
    w_row = p.sbuf("w_row", [1, D], F32)
    carry = p.sbuf("carry_sb", [128, DC], F32)
    R = {k: p.sbuf("row_" + k, [1, T], F32) for k in ("q", "rr")}
    eps2 = p.sbuf("eps2", [1, 1], F32)
    ps = p.psum("ps", [128, 4096], F32)

    p.op("vector", lambda eng: eng.memset(ones_col[:], 1.0))
    p.op("vector", lambda eng: eng.memset(eps2[:], 1e-6))
    p.dma("sync", w_row[:], wffn_h)
    p.dma("sync", carry[:], carry_h)
    ldv = p.mark("qs")

    for dc in range(DC):
        l1 = p.dma("sync", xm[dc][:], h_h[dc * 128:(dc + 1) * 128, :])
        l2 = p.dma("sync", tmpb[:], A_h[dc * 128:(dc + 1) * 128, :])
        p.wait("vector", "qs", max(l1, l2, ldv))
        vm = p.op("vector", lambda eng, dc=dc: eng.scalar_tensor_tensor(
            xm[dc][:], tmpb[:], carry[:, dc:dc + 1], xm[dc][:], ALU.mult, ALU.add))
        p.wait("sync", "v", vm)
        l3 = p.dma("sync", tmpb[:], xhat_h[dc * 128:(dc + 1) * 128, :])
        p.wait("vector", "qs", l3)
        vm = p.op("vector", lambda eng, dc=dc: eng.tensor_add(
            xm[dc][:], xm[dc][:], tmpb[:]))
        p.wait("sync", "v", vm)
    xm_done = p.mark("v")

    xb_done = emit_load_xbf(p, xT, stage, xb)

    p.wait("scalar", "v", xm_done)
    for kc in range(DC):
        sm = p.op("scalar", lambda eng, kc=kc: eng.activation(sq[:], xm[kc][:], AF.Square))
        p.wait("tensor", "s", sm)
        for tb in range(T // 512):
            p.op("tensor", lambda eng, kc=kc, tb=tb:
                 eng.matmul(ps[0:1, tb * 512:(tb + 1) * 512], ones_col[:],
                            sq[:, tb * 512:(tb + 1) * 512],
                            start=(kc == 0), stop=(kc == DC - 1)), sem="t")
        p.wait("scalar", "t", p.mark("t"))
    tm = p.mark("t")
    p.wait("vector", "t", tm)
    vm = p.op("vector", lambda eng: eng.tensor_scalar(
        R["q"][:], ps[0:1, 0:T], 1.0 / D, None, ALU.mult))
    p.wait("scalar", "v", vm)
    sm = p.op("scalar", lambda eng: eng.activation(R["q"][:], R["q"][:], AF.Sqrt,
                                                   bias=eps2[:], scale=1.0))
    p.wait("vector", "s", sm)
    rowm = p.op("vector", lambda eng: eng.reciprocal(R["rr"][:], R["q"][:]))
    p.wait("tensor", "v", rowm)

    for dc in range(DC):
        for tb in range(T // 512):
            p.op("tensor", lambda eng, dc=dc, tb=tb:
                 eng.matmul(ps[:, tb * 512:(tb + 1) * 512],
                            w_row[0:1, dc * 128:(dc + 1) * 128],
                            R["rr"][0:1, tb * 512:(tb + 1) * 512],
                            start=True, stop=True, skip_group_check=True), sem="t")
        tm = p.mark("t")
        p.wait("vector", "t", tm)
        vm = p.op("vector", lambda eng, dc=dc: eng.tensor_mul(
            yn[dc][:], xm[dc][:], ps[:, 0:T]))
        p.wait("tensor", "v", vm)

    p.wait("sync", "v", p.mark("v"))
    for grp in range(NGRP):
        for gi in range(GRP):
            fc = grp * GRP + gi
            lm = p.dma("sync", stage[:, 0:D].rearrange("p (a n) -> p a n", n=128),
                       wgate_h[:, fc * 128:(fc + 1) * 128]
                       .rearrange("(a p) n -> p a n", p=128))
            p.wait("vector", "qs", lm)
            vg = p.op("vector", lambda eng: eng.tensor_copy(wg_sl[:], stage[:, 0:D]))
            p.wait("sync", "v", vg)
            lm = p.dma("sync", stage[:, 0:D].rearrange("p (a n) -> p a n", n=128),
                       wup_h[:, fc * 128:(fc + 1) * 128]
                       .rearrange("(a p) n -> p a n", p=128))
            p.wait("vector", "qs", lm)
            vu = p.op("vector", lambda eng: eng.tensor_copy(wu_sl[:], stage[:, 0:D]))
            p.wait("sync", "v", vu)
            lm = p.dma("sync", stage[:, 0:D], wout_h[fc * 128:(fc + 1) * 128, :])
            p.wait("vector", "qs", lm)
            vo = p.op("vector", lambda eng, gi=gi: eng.tensor_copy(
                wo_sl[gi][:], stage[:, 0:D]))
            p.wait("sync", "v", vo)
            p.wait("tensor", "v", vg)
            for kc in range(DC):
                for tb in range(T // 512):
                    p.op("tensor", lambda eng, kc=kc, tb=tb:
                         eng.matmul(ps[:, tb * 512:(tb + 1) * 512],
                                    wg_sl[:, kc * 128:(kc + 1) * 128],
                                    yn[kc][:, tb * 512:(tb + 1) * 512],
                                    start=(kc == 0), stop=(kc == DC - 1)), sem="t")
            tm = p.mark("t")
            p.wait("scalar", "t", tm)
            sm = p.op("scalar", lambda eng: eng.activation(tmpb[:], ps[:, 0:T], AF.Silu))
            p.wait("tensor", "s", sm)
            p.wait("tensor", "v", vu)
            for kc in range(DC):
                for tb in range(T // 512):
                    p.op("tensor", lambda eng, kc=kc, tb=tb:
                         eng.matmul(ps[:, T + tb * 512:T + (tb + 1) * 512],
                                    wu_sl[:, kc * 128:(kc + 1) * 128],
                                    yn[kc][:, tb * 512:(tb + 1) * 512],
                                    start=(kc == 0), stop=(kc == DC - 1)), sem="t")
            tm = p.mark("t")
            p.wait("vector", "t", tm)
            p.wait("vector", "s", sm)
            vm = p.op("vector", lambda eng, gi=gi: eng.tensor_mul(
                mt[gi][:], tmpb[:], ps[:, T:2 * T]))
            p.wait("tensor", "v", vm)
            p.wait("scalar", "v", vm)
        mdone = p.mark("v")
        p.wait("tensor", "v", mdone)
        for dc in range(DC):
            for gi in range(GRP):
                for tb in range(T // 512):
                    fn = (lambda eng, gi=gi, tb=tb, dc=dc:
                          eng.matmul(ps[:, tb * 512:(tb + 1) * 512],
                                     wo_sl[gi][:, dc * 128:(dc + 1) * 128],
                                     mt[gi][:, tb * 512:(tb + 1) * 512],
                                     start=(gi == 0), stop=(gi == GRP - 1)))
                    if gi == GRP - 1 and tb == T // 512 - 1:
                        p.op("tensor", fn, sem="t")
                    else:
                        p.opn("tensor", fn)
            tm = p.mark("t")
            p.wait("vector", "t", tm)
            if grp == 0:
                vm = p.op("vector", lambda eng, dc=dc: eng.tensor_copy(
                    da[dc][:], ps[:, 0:T]))
            else:
                vm = p.op("vector", lambda eng, dc=dc: eng.tensor_add(
                    da[dc][:], da[dc][:], ps[:, 0:T]))
            p.wait("tensor", "v", vm)
    p.wait("vector", "v", p.mark("v"))
    dm = 0
    for dc in range(DC):
        p.op("vector", lambda eng, dc=dc: eng.tensor_add(
            da[dc][:], da[dc][:], xm[dc][:]))
        vm = p.op("vector", lambda eng, dc=dc: eng.tensor_add(
            da[dc][:], da[dc][:], xb[dc][:]))
        p.wait("gpsimd", "v", vm)
        dm = p.dma("gpsimd", x1_o[dc * 128:(dc + 1) * 128, :], da[dc][:])
    p.wait("gpsimd", "qg", p.mark("qg"))
    return p.finalize()



def build_LC(dbg=False):
    p = Prog()
    x1_h = p.dram("x1", [D, T], BF16, "ExternalInput").ap()
    pcol_h = p.dram("pcol", [128, DC], BF16, "ExternalInput").ap()
    wrT_h = p.dram("W_rT", [D, D], F32, "ExternalInput").ap()
    wkT_h = p.dram("W_kT", [D, D], F32, "ExternalInput").ap()
    wvT_h = p.dram("W_vT", [D, D], F32, "ExternalInput").ap()
    mu2_h = p.dram("mu2", [2 * DC, 128], F32, "ExternalInput").ap()
    x2_o = p.dram("x2", [D, T], BF16, "ExternalOutput").ap()

    xb = [p.sbuf(f"xb{i}", [128, T], BF16) for i in range(DC)]
    xr = [p.sbuf(f"xr{i}", [128, T], BF16) for i in range(DC)]
    xk = [p.sbuf(f"xk{i}", [128, T], BF16) for i in range(DC)]
    rt = [p.sbuf(f"rt{i}", [128, T], BF16) for i in range(DC)]
    ka = [p.sbuf(f"ka{i}", [128, T], BF16) for i in range(DC)]
    xh2 = p.sbuf("xh2", [128, T], BF16)
    wbf = [p.sbuf(f"wbf{i}", [128, D], BF16) for i in range(DC)]
    stage = p.sbuf("stage", [128, D], F32)
    sq = None
    ones_col = p.sbuf("ones_col", [128, 1], BF16)
    ones_row = p.sbuf("ones_row", [1, 128], BF16)
    ones_rowf = p.sbuf("ones_rowf", [1, 128], F32)
    mu_pp = p.sbuf("mu_pp", [128, 4 * DC], F32)
    pcol = p.sbuf("pcol_sb", [128, DC], BF16)
    pchat = p.sbuf("pchat", [128, DC], BF16)
    prow = p.sbuf("prow", [1, 2 * DC], F32)
    pval = p.sbuf("pval", [1, 4], F32)
    ppb = p.sbuf("ppb", [128, 2], F32)
    R = {k: p.sbuf("row_" + k, [1, T], BF16) for k in ("mean", "rstd", "tmp")}
    eps1 = p.sbuf("eps1", [1, 1], F32)
    ps = p.psum("ps", [128, 4096], F32)

    p.op("vector", lambda eng: eng.memset(ones_col[:], 1.0))
    p.op("vector", lambda eng: eng.memset(ones_row[:], 1.0))
    p.op("vector", lambda eng: eng.memset(ones_rowf[:], 1.0))
    p.op("vector", lambda eng: eng.memset(eps1[:], 1e-5))
    for r in range(DC):
        p.dma("sync", mu_pp[:, r:r + 1], mu2_h[r:r + 1, :].rearrange("o d -> d o"))
        p.dma("sync", mu_pp[:, 2 * DC + r:2 * DC + r + 1],
              mu2_h[DC + r:DC + r + 1, :].rearrange("o d -> d o"))
    lm = p.dma("sync", pcol[:], pcol_h)
    p.wait("vector", "qs", lm)
    p.op("vector", lambda eng: eng.tensor_scalar(
        mu_pp[:, DC:2 * DC], mu_pp[:, 0:DC], -1.0, 1.0, ALU.mult, ALU.add))
    p.op("vector", lambda eng: eng.tensor_scalar(
        mu_pp[:, 3 * DC:4 * DC], mu_pp[:, 2 * DC:3 * DC], -1.0, 1.0, ALU.mult, ALU.add))

    for dc in range(DC):
        p.dma("sync", xb[dc][:], x1_h[dc * 128:(dc + 1) * 128, :])
    x_ld = p.mark("qs")
    p.wait("scalar", "qs", x_ld)
    p.wait("vector", "qs", x_ld)

    tm = emit_ln_stats(p, ps, xb, xh2, ones_col)
    p.wait("vector", "t", tm)
    rowm = emit_ln_rows_plain(p, ps, R["mean"], R["rstd"], R["tmp"], eps1[:])

    sm = p.op("scalar", lambda eng: eng.activation(xh2[:, 0:DC], pcol[:], AF.Square))
    p.wait("tensor", "s", sm)
    p.wait("tensor", "qs", lm)
    p.wait("tensor", "v", rowm)
    p.op("tensor", lambda eng: eng.matmul(ps[0:1, 0:DC], ones_col[:],
                                          pcol[:], start=True, stop=True,
                                          skip_group_check=True), sem="t")
    tm2 = p.op("tensor", lambda eng: eng.matmul(ps[0:1, DC:2 * DC],
                                                ones_col[:], xh2[:, 0:DC],
                                                start=True, stop=True,
                                                skip_group_check=True), sem="t")
    p.wait("vector", "t", tm2)
    vm = p.op("vector", lambda eng: eng.tensor_copy(prow[:], ps[0:1, 0:2 * DC]))
    p.wait("vector", "v", vm)
    vm = p.op("vector", lambda eng: eng.reduce_sum(pval[0:1, 0:1], prow[0:1, 0:DC], axis=AX.X))
    vm = p.op("vector", lambda eng: eng.reduce_sum(pval[0:1, 1:2], prow[0:1, DC:2 * DC], axis=AX.X))
    p.wait("vector", "v", vm)
    vm = p.op("vector", lambda eng: eng.tensor_scalar(
        pval[0:1, 2:3], pval[0:1, 0:1], 1.0 / D, None, ALU.mult))
    p.wait("scalar", "v", vm)
    sm = p.op("scalar", lambda eng: eng.activation(pval[0:1, 3:4], pval[0:1, 2:3], AF.Square))
    p.wait("vector", "s", sm)
    vm = p.op("vector", lambda eng: eng.scalar_tensor_tensor(
        pval[0:1, 3:4], pval[0:1, 1:2], 1.0 / D, pval[0:1, 3:4], ALU.mult, ALU.subtract))
    p.wait("scalar", "v", vm)
    sm = p.op("scalar", lambda eng: eng.activation(pval[0:1, 3:4], pval[0:1, 3:4],
                                                   AF.Sqrt, bias=eps1[:], scale=1.0))
    p.wait("vector", "s", sm)
    vm = p.op("vector", lambda eng: eng.reciprocal(pval[0:1, 3:4], pval[0:1, 3:4]))
    p.wait("vector", "v", vm)
    vm2 = p.op("vector", lambda eng: eng.scalar_tensor_tensor(
        pval[0:1, 2:3], pval[0:1, 2:3], -1.0, pval[0:1, 3:4], ALU.mult, ALU.mult))
    p.wait("tensor", "v", vm2)
    p.op("tensor", lambda eng: eng.matmul(ps[:, 0:1],
                                          ones_rowf[0:1, :], pval[0:1, 3:4],
                                          start=True, stop=True,
                                          skip_group_check=True), sem="t")
    tm2 = p.op("tensor", lambda eng: eng.matmul(ps[:, 1:2],
                                                ones_rowf[0:1, :], pval[0:1, 2:3],
                                                start=True, stop=True,
                                                skip_group_check=True), sem="t")
    p.wait("vector", "t", tm2)
    vm = p.op("vector", lambda eng: eng.tensor_copy(ppb[:], ps[:, 0:2]))
    p.wait("vector", "v", vm)
    vm = p.op("vector", lambda eng: eng.scalar_tensor_tensor(
        pchat[:], pcol[:], ppb[:, 0:1], ppb[:, 1:2].broadcast_to([128, DC]),
        ALU.mult, ALU.add))

    if dbg:
        dxh = p.dram("dbg_xh", [D, T], BF16, "ExternalOutput").ap()
        dxr = p.dram("dbg_xr", [D, T], BF16, "ExternalOutput").ap()
        dxk = p.dram("dbg_xk", [D, T], BF16, "ExternalOutput").ap()
        drt = p.dram("dbg_rt", [D, T], BF16, "ExternalOutput").ap()
        dka = p.dram("dbg_ka", [D, T], BF16, "ExternalOutput").ap()
    p.wait("tensor", "v", p.mark("v"))
    for dc in range(DC):
        avm = emit_bcast_apply(p, ps, ones_row[0:1, 0:128], R["rstd"], R["tmp"],
                               xb[dc], xh2)
        if dbg:
            p.wait("gpsimd", "v", avm)
            dmx = p.dma("gpsimd", dxh[dc * 128:(dc + 1) * 128, :], xh2[:])
            p.wait("vector", "qg", dmx)
        vm = p.op("vector", lambda eng, dc=dc: eng.tensor_scalar(
            xr[dc][:, 1:T], xh2[:, 0:T - 1], mu_pp[:, DC + dc:DC + dc + 1],
            None, ALU.mult))
        vm = p.op("vector", lambda eng, dc=dc: eng.tensor_scalar(
            xr[dc][:, 0:1], pchat[:, dc:dc + 1], mu_pp[:, DC + dc:DC + dc + 1],
            None, ALU.mult))
        vm = p.op("vector", lambda eng, dc=dc: eng.tensor_scalar(
            xk[dc][:, 1:T], xh2[:, 0:T - 1], mu_pp[:, 3 * DC + dc:3 * DC + dc + 1],
            None, ALU.mult))
        vm = p.op("vector", lambda eng, dc=dc: eng.tensor_scalar(
            xk[dc][:, 0:1], pchat[:, dc:dc + 1], mu_pp[:, 3 * DC + dc:3 * DC + dc + 1],
            None, ALU.mult))
        vm = p.op("vector", lambda eng, dc=dc: eng.scalar_tensor_tensor(
            xr[dc][:], xh2[:], mu_pp[:, dc:dc + 1], xr[dc][:], ALU.mult, ALU.add))
        vm = p.op("vector", lambda eng, dc=dc: eng.scalar_tensor_tensor(
            xk[dc][:], xh2[:], mu_pp[:, 2 * DC + dc:2 * DC + dc + 1], xk[dc][:],
            ALU.mult, ALU.add))
        p.wait("scalar", "v", vm)
        if dbg:
            p.wait("gpsimd", "v", vm)
            dmx = p.dma("gpsimd", dxr[dc * 128:(dc + 1) * 128, :], xr[dc][:])
            dmx = p.dma("gpsimd", dxk[dc * 128:(dc + 1) * 128, :], xk[dc][:])
    mix_done = p.mark("v")

    def proj_from(src, wtiles, dc, b0):
        for kc in range(DC):
            for tb in range(T // 512):
                p.op("tensor", lambda eng, kc=kc, tb=tb, dc=dc, b0=b0:
                     eng.matmul(ps[:, (b0 + tb) * 512:(b0 + tb + 1) * 512],
                                wtiles[kc][:, dc * 128:(dc + 1) * 128],
                                src[kc][:, tb * 512:(tb + 1) * 512],
                                start=(kc == 0), stop=(kc == DC - 1)), sem="t")
        return p.mark("t")

    wm = emit_load_w_bf16(p, wrT_h, D, D, stage, wbf, stage_guard=0)
    p.wait("tensor", "v", wm)
    for dc in range(DC):
        b0 = 0 if dc % 2 == 0 else 4
        tm = proj_from(xr, wbf, dc, b0)
        p.wait("scalar", "t", tm)
        sm = p.op("scalar", lambda eng, dc=dc, b0=b0: eng.activation(
            rt[dc][:], ps[:, b0 * 512:b0 * 512 + T], AF.Sigmoid))
        p.wait("tensor", "s", sm)
        if dbg:
            p.wait("gpsimd", "s", sm)
            p.dma("gpsimd", drt[dc * 128:(dc + 1) * 128, :], rt[dc][:])
    p.wait("vector", "t", p.mark("t"))
    wm = emit_load_w_bf16(p, wkT_h, D, D, stage, wbf, stage_guard=wm)
    p.wait("tensor", "v", wm)
    for dc in range(DC):
        b0 = 0 if dc % 2 == 0 else 4
        tm = proj_from(xk, wbf, dc, b0)
        p.wait("scalar", "t", tm)
        sm = p.op("scalar", lambda eng, dc=dc, b0=b0: eng.activation(
            xh2[:], ps[:, b0 * 512:b0 * 512 + T], AF.Relu))
        p.wait("scalar", "s", sm)
        sm = p.op("scalar", lambda eng, dc=dc: eng.activation(
            ka[dc][:], xh2[:], AF.Square))
        p.wait("tensor", "s", sm)
        if dbg:
            p.wait("gpsimd", "s", sm)
            p.dma("gpsimd", dka[dc * 128:(dc + 1) * 128, :], ka[dc][:])
    p.wait("vector", "t", p.mark("t"))
    wm = emit_load_w_bf16(p, wvT_h, D, D, stage, wbf, stage_guard=wm)
    p.wait("tensor", "v", wm)
    dm = 0
    for dc in range(DC):
        b0 = 0 if dc % 2 == 0 else 4
        tm = proj_from(ka, wbf, dc, b0)
        p.wait("vector", "t", tm)
        if dm:
            p.wait("vector", "qg", dm)
        vm = p.op("vector", lambda eng, dc=dc, b0=b0: eng.tensor_mul(
            xh2[:], rt[dc][:], ps[:, b0 * 512:b0 * 512 + T]))
        vm = p.op("vector", lambda eng, dc=dc: eng.tensor_add(
            xh2[:], xh2[:], xb[dc][:]))
        p.wait("tensor", "v", vm)
        p.wait("gpsimd", "v", vm)
        dm = p.dma("gpsimd", x2_o[dc * 128:(dc + 1) * 128, :], xh2[:])
    p.wait("gpsimd", "qg", p.mark("qg"))
    return p.finalize()



def build_LD():
    p = Prog()
    x2_h = p.dram("x2", [D, T], BF16, "ExternalInput").ap()
    wdkvT_h = p.dram("W_dkvT", [D, 128], F32, "ExternalInput").ap()
    xh3_o = p.dram("xhat3", [D, T], BF16, "ExternalOutput").ap()
    ckv_o = p.dram("ckv", [128, T], BF16, "ExternalOutput").ap()

    xb = [p.sbuf(f"xb{i}", [128, T], BF16) for i in range(DC)]
    x3 = [p.sbuf(f"x3{i}", [128, T], BF16) for i in range(DC)]
    wdkv = [p.sbuf(f"wdkv{i}", [128, 128], BF16) for i in range(DC)]
    stage = p.sbuf("stage", [128, 128], F32)
    sq = p.sbuf("sq", [128, T], BF16)
    ckv_sb = p.sbuf("ckv_sb", [128, T], BF16)
    ones_col = p.sbuf("ones_col", [128, 1], BF16)
    ones_row = p.sbuf("ones_row", [1, 128], F32)
    R = {k: p.sbuf("row_" + k, [1, T], F32) for k in ("mean", "rstd", "tmp")}
    eps1 = p.sbuf("eps1", [1, 1], F32)
    ps = p.psum("ps", [128, 4096], F32)

    p.op("vector", lambda eng: eng.memset(ones_col[:], 1.0))
    p.op("vector", lambda eng: eng.memset(ones_row[:], 1.0))
    p.op("vector", lambda eng: eng.memset(eps1[:], 1e-5))
    for dc in range(DC):
        p.dma("sync", xb[dc][:], x2_h[dc * 128:(dc + 1) * 128, :])
    ldm = p.mark("qs")
    wm = 0
    for kc in range(DC):
        if wm:
            p.wait("sync", "v", wm)
        lm = p.dma("sync", stage[:], wdkvT_h[kc * 128:(kc + 1) * 128, :])
        p.wait("vector", "qs", lm)
        wm = p.op("vector", lambda eng, kc=kc: eng.tensor_copy(wdkv[kc][:], stage[:]))
    p.wait("scalar", "qs", ldm)
    p.wait("scalar", "v", p.mark("v"))
    tm = emit_ln_stats(p, ps, xb, sq, ones_col)
    p.wait("vector", "t", tm)
    rowm = emit_ln_rows_plain(p, ps, R["mean"], R["rstd"], R["tmp"], eps1[:])
    p.wait("tensor", "v", rowm)
    dmx = 0
    for dc in range(DC):
        vm = emit_bcast_apply(p, ps, ones_row[0:1, 0:128], R["rstd"], R["tmp"],
                              xb[dc], x3[dc])
        p.wait("gpsimd", "v", vm)
        dmx = p.dma("gpsimd", xh3_o[dc * 128:(dc + 1) * 128, :], x3[dc][:])
    p.wait("tensor", "v", wm)
    for tb in range(T // 512):
        for kc in range(DC):
            p.op("tensor", lambda eng, kc=kc, tb=tb:
                 eng.matmul(ps[:, tb * 512:(tb + 1) * 512], wdkv[kc][:],
                            x3[kc][:, tb * 512:(tb + 1) * 512],
                            start=(kc == 0), stop=(kc == DC - 1)), sem="t")
    tm = p.mark("t")
    p.wait("vector", "t", tm)
    vm = p.op("vector", lambda eng: eng.tensor_copy(ckv_sb[:], ps[:, 0:T]))
    p.wait("gpsimd", "v", vm)
    p.dma("gpsimd", ckv_o, ckv_sb[:])
    p.wait("gpsimd", "qg", p.mark("qg"))
    return p.finalize()



TT = 4096
NKT = TT // 128
NQB = TT // 512

def build_LE(dbg=False):
    p = Prog()
    x3_h = p.dram("xhat3", [D, TT], BF16, "ExternalInput").ap()
    ckv_h = p.dram("ckv", [128, TT], BF16, "ExternalInput").ap()
    wdq_h = p.dram("W_dq", [128, D], F32, "ExternalInput").ap()
    wuqT_h = p.dram("W_uqT_p", [128, D], F32, "ExternalInput").ap()
    wdqp_h = p.dram("W_dq_p", [128, D], F32, "ExternalInput").ap()
    wukp_h = p.dram("W_uk_p", [D, 128], F32, "ExternalInput").ap()
    wuv_h = p.dram("W_uv", [D, 128], F32, "ExternalInput").ap()
    woTp_h = p.dram("W_oT_p", [D, D], F32, "ExternalInput").ap()
    ident_h = p.dram("ident", [128, 128], BF16, "ExternalInput").ap()
    identf_h = p.dram("identf", [128, 128], F32, "ExternalInput").ap()
    mask4_h = p.dram("mask4", [4, 128, 512], F32, "ExternalInput").ap()
    y_o = p.dram("y", [512, TT], BF16, "ExternalOutput").ap()
    rz_o = p.dram("rz", [8, 32, 128], BF16, "ExternalOutput").ap()

    x3f = [p.sbuf(f"x3f{i}", [128, TT], BF16) for i in range(DC)]
    ckv_fm = p.sbuf("ckv_fm", [128, TT], BF16)
    ckv_tm = [p.sbuf(f"ckvtm{i}", [128, 128], BF16) for i in range(NKT)]
    mown = [p.sbuf(f"mown{i}", [128, D], BF16) for i in range(DC)]
    vown = p.sbuf("vown", [128, 512], BF16)
    at1 = [p.sbuf(f"at1_{i}", [128, D], BF16) for i in range(4)]
    at2 = [p.sbuf(f"at2_{i}", [128, 512], BF16) for i in range(DC)]
    kloc = [p.sbuf(f"kloc{i}", [128, 128], BF16) for i in range(4)]
    wtmp = p.sbuf("wtmp", [128, D], BF16)
    stage = p.sbuf("stage", [128, D], F32)
    ident = p.sbuf("ident_sb", [128, 128], BF16)
    identf = p.sbuf("identf_sb", [128, 128], F32)
    mask4 = p.sbuf("mask4_sb", [128, 4 * 512], F32)
    pexp = p.sbuf("pexp", [128, 2048], BF16)
    pT = p.sbuf("pT", [128, 16 * 512], BF16)
    qlb = p.sbuf("qlb", [128, 512], BF16)
    sqq = p.sbuf("sqq", [128, 512], BF16)
    dql = p.sbuf("dql", [128, 512], BF16)
    stats = p.sbuf("stats", [128, 8], F32)
    numu = p.sbuf("numu", [128, 8], F32)
    r2pp = p.sbuf("r2pp", [128, 1], F32)
    zc = p.sbuf("zc", [128, 8 * 32], F32)
    zpart = p.sbuf("zpart", [128, 1], F32)
    ctx_sb = p.sbuf("ctx_sb", [128, 512], BF16)
    y_sb = p.sbuf("y_sb", [128, 512], BF16)
    rzt = p.sbuf("rzt", [32, 128], BF16)
    ones_col = p.sbuf("ones_col", [128, 1], BF16)
    sqrow = p.sbuf("sqrow", [128, 512], BF16)
    r2row = p.sbuf("r2row", [1, TT // 512], F32)
    ones_rowf = p.sbuf("ones_rowf", [1, 128], F32)
    ps = p.psum("ps", [128, 4096], F32)
    ptb = ps[:, 3584:4096].bitcast(BF16)

    p.op("vector", lambda eng: eng.memset(ones_col[:], 1.0))
    p.op("vector", lambda eng: eng.memset(ones_rowf[:], 1.0))
    p.op("vector", lambda eng: eng.memset(zc[:], 0.0))
    p.dma("sync", ident[:], ident_h)
    p.dma("sync", identf[:], identf_h)
    for i in range(4):
        p.dma("sync", mask4[:, i * 512:(i + 1) * 512], mask4_h[i])
    for dc in range(DC):
        p.dma("sync", x3f[dc][:], x3_h[dc * 128:(dc + 1) * 128, :])
    lm = p.dma("sync", ckv_fm[:], ckv_h)
    in_ld = p.mark("qs")

    def loadw(hbm, ncols, dst, guard):
        if guard:
            p.wait("sync", "v", guard)
        lm = p.dma("sync", stage[:, 0:ncols], hbm)
        p.wait("vector", "qs", lm)
        return p.op("vector", lambda eng: eng.tensor_copy(dst[:, 0:ncols],
                                                          stage[:, 0:ncols]))
    wuqm = loadw(wuqT_h, D, wtmp, 0)
    wdqm = 0
    wdq_sb = p.sbuf("wdq_sb", [128, D], BF16)
    wdqm = loadw(wdq_h, D, wdq_sb, wuqm)
    p.wait("tensor", "v", wdqm)
    for jc in range(4):
        for nb in range(2):
            p.op("tensor", lambda eng, jc=jc, nb=nb:
                 eng.matmul(ps[:, nb * 512:(nb + 1) * 512],
                            wtmp[:, jc * 128:(jc + 1) * 128],
                            wdq_sb[:, nb * 512:(nb + 1) * 512],
                            start=True, stop=True, skip_group_check=True), sem="t")
        tm = p.mark("t")
        p.wait("vector", "t", tm)
        vm = p.op("vector", lambda eng, jc=jc: eng.tensor_copy(at1[jc][:], ps[:, 0:1024]))
        p.wait("tensor", "v", vm)
    p.wait("vector", "t", p.mark("t"))
    wdqm = loadw(wdqp_h, D, wdq_sb, wdqm)
    p.wait("tensor", "v", wdqm)
    for jc in range(DC):
        p.op("tensor", lambda eng, jc=jc:
             eng.matmul(ps[:, 0:512], wtmp[:, jc * 128:(jc + 1) * 128],
                        wdq_sb[:, 0:512],
                        start=True, stop=True, skip_group_check=True), sem="t")
        tm = p.mark("t")
        p.wait("vector", "t", tm)
        vm = p.op("vector", lambda eng, jc=jc: eng.tensor_copy(at2[jc][:], ps[:, 0:512]))
        p.wait("tensor", "v", vm)
    wukm = wdqm
    wuk_sb = [p.sbuf(f"wuk{i}", [128, 128], BF16) for i in range(DC)]
    for jc in range(DC):
        wukm = loadw(wukp_h[jc * 128:(jc + 1) * 128, :], 128, wuk_sb[jc], wukm)
    p.wait("tensor", "v", wukm)
    for ds in range(4):
        for jc in range(DC):
            p.op("tensor", lambda eng, ds=ds, jc=jc:
                 eng.matmul(ps[:, 1024 + ds * 128:1024 + (ds + 1) * 128],
                            at2[jc][:, ds * 128:(ds + 1) * 128], wuk_sb[jc][:],
                            start=(jc == 0), stop=(jc == DC - 1)), sem="t")
    tm = p.mark("t")
    p.wait("vector", "t", tm)
    for ds in range(4):
        vm = p.op("vector", lambda eng, ds=ds: eng.tensor_copy(
            kloc[ds][:], ps[:, 1024 + ds * 128:1024 + (ds + 1) * 128]))
    p.wait("tensor", "v", p.mark("v"))
    for hh in range(8):
        tile, off = hh // 2, 64 * (hh % 2)
        for d1c in range(DC):
            p.op("tensor", lambda eng, hh=hh, tile=tile, off=off, d1c=d1c:
                 eng.matmul(ps[:, 2048 + (d1c % 4) * 128:2048 + (d1c % 4) * 128 + 128],
                            at1[tile][off:off + 64, d1c * 128:(d1c + 1) * 128],
                            kloc[tile][off:off + 64, :],
                            start=True, stop=True, skip_group_check=True), sem="t")
            tm = p.mark("t")
            p.wait("vector", "t", tm)
            vm = p.op("vector", lambda eng, hh=hh, d1c=d1c: eng.tensor_copy(
                mown[d1c][:, hh * 128:(hh + 1) * 128],
                ps[:, 2048 + (d1c % 4) * 128:2048 + (d1c % 4) * 128 + 128]))
            p.wait("tensor", "v", vm)
    p.wait("vector", "t", p.mark("t"))
    wuvm = wukm
    wuv_sb = p.sbuf("wuv_sb", [128, 128], BF16)
    for cc in range(DC):
        wuvm = loadw(wuv_h[cc * 128:(cc + 1) * 128, :], 128, wuv_sb, wuvm)
        guard = loadw(woTp_h[cc * 128:(cc + 1) * 128, 0:512], 512, wtmp, wuvm)
        p.wait("tensor", "v", guard)
        p.op("tensor", lambda eng, cc=cc:
             eng.matmul(ps[:, 3072:3584], wuv_sb[:], wtmp[:, 0:512],
                        start=(cc == 0), stop=(cc == DC - 1)), sem="t")
        wuvm = guard
        p.wait("vector", "t", p.mark("t"))
    tm = p.mark("t")
    p.wait("vector", "t", tm)
    vm = p.op("vector", lambda eng: eng.tensor_copy(vown[:], ps[:, 3072:3584]))
    p.wait("tensor", "v", vm)
    if dbg:
        d_at1 = p.dram("d_at1", [4, 128, D], BF16, "ExternalOutput").ap()
        d_at2 = p.dram("d_at2", [DC, 128, 512], BF16, "ExternalOutput").ap()
        d_kloc = p.dram("d_kloc", [4, 128, 128], BF16, "ExternalOutput").ap()
        d_mown = p.dram("d_mown", [D, D], BF16, "ExternalOutput").ap()
        d_vown = p.dram("d_vown", [128, 512], BF16, "ExternalOutput").ap()
        d_ql = p.dram("d_ql", [128, 512], BF16, "ExternalOutput").ap()
        d_stats = p.dram("d_stats", [128, 8], F32, "ExternalOutput").ap()
        d_numu = p.dram("d_numu", [128, 8], F32, "ExternalOutput").ap()
        d_pexp = p.dram("d_pexp", [128, 512], BF16, "ExternalOutput").ap()
        d_ctx = p.dram("d_ctx", [128, 512], BF16, "ExternalOutput").ap()
        p.wait("gpsimd", "v", p.mark("v"))
        for i in range(4):
            p.dma("gpsimd", d_at1[i], at1[i][:])
            p.dma("gpsimd", d_kloc[i], kloc[i][:])
        for i in range(DC):
            p.dma("gpsimd", d_at2[i], at2[i][:])
        for dc in range(DC):
            p.dma("gpsimd", d_mown[dc * 128:(dc + 1) * 128, :], mown[dc][:])
        p.dma("gpsimd", d_vown, vown[:])

    p.wait("tensor", "qs", in_ld)
    for kt in range(NKT):
        p.op("tensor", lambda eng, kt=kt:
             eng.matmul(ptb[:, (kt % 8) * 128:(kt % 8) * 128 + 128],
                        ckv_fm[:, kt * 128:(kt + 1) * 128], ident[:],
                        is_transpose=True, skip_group_check=True), sem="t")
        tm = p.mark("t")
        p.wait("vector", "t", tm)
        vm = p.op("vector", lambda eng, kt=kt: eng.tensor_copy(
            ckv_tm[kt][:], ptb[:, (kt % 8) * 128:(kt % 8) * 128 + 128]))
        p.wait("tensor", "v", vm)

    p.wait("scalar", "qs", in_ld)
    r2m = p.op("vector", lambda eng: eng.memset(r2row[:], 0.0))
    for tb in range(TT // 512):
        sm = p.op("scalar", lambda eng, tb=tb: eng.activation(
            sqrow[:], ckv_fm[:, tb * 512:(tb + 1) * 512], AF.Square))
        p.wait("tensor", "s", sm)
        tmx = p.op("tensor", lambda eng:
                   eng.matmul(ps[0:1, 2048:2560], ones_col[:], sqrow[:],
                              start=True, stop=True, skip_group_check=True),
                   sem="t")
        p.wait("vector", "t", tmx)
        p.wait("vector", "v", r2m)
        r2m = p.op("vector", lambda eng, tb=tb: eng.reduce_max(
            r2row[0:1, tb:tb + 1], ps[0:1, 2048:2560], axis=AX.X))
        p.wait("tensor", "v", r2m)
        p.wait("scalar", "t", p.mark("t"))
    p.wait("vector", "v", r2m)
    r2m = p.op("vector", lambda eng: eng.reduce_max(
        r2row[0:1, 0:1], r2row[0:1, 0:TT // 512], axis=AX.X))
    p.wait("tensor", "v", r2m)
    p.op("tensor", lambda eng: eng.matmul(
        ps[:, 2048:2049], ones_rowf[0:1, :], r2row[0:1, 0:1],
        start=True, stop=True, skip_group_check=True), sem="t")
    tm = p.mark("t")
    p.wait("vector", "t", tm)
    r2m = p.op("vector", lambda eng: eng.tensor_copy(r2pp[:], ps[:, 2048:2049]))
    p.wait("tensor", "v", r2m)

    dmy = 0
    for hh in range(8):
        for qb in range(NQB):
            KB = qb + 1
            for dc in range(DC):
                p.op("tensor", lambda eng, dc=dc, hh=hh, qb=qb:
                     eng.matmul(ps[:, 2048:2560],
                                mown[dc][:, hh * 128:(hh + 1) * 128],
                                x3f[dc][:, qb * 512:(qb + 1) * 512],
                                start=(dc == 0), stop=(dc == DC - 1)), sem="t")
            tm = p.mark("t")
            p.wait("vector", "t", tm)
            vq = p.op("vector", lambda eng: eng.tensor_copy(qlb[:], ps[:, 2048:2560]))
            p.wait("tensor", "v", vq)
            if dbg and hh == 0 and qb == 0:
                p.wait("gpsimd", "v", vq)
                p.dma("gpsimd", d_ql, qlb[:])
            p.wait("scalar", "v", vq)
            sm = p.op("scalar", lambda eng: eng.activation(sqq[:], qlb[:], AF.Square))
            vd = p.op("vector", lambda eng, qb=qb: eng.tensor_mul(
                dql[:], qlb[:], ckv_fm[:, qb * 512:(qb + 1) * 512]))
            p.wait("tensor", "s", sm)
            p.wait("tensor", "v", vd)
            for qt in range(4):
                p.op("tensor", lambda eng, qt=qt:
                     eng.matmul(ps[:, 2048 + qt:2048 + qt + 1],
                                sqq[:, qt * 128:(qt + 1) * 128], ones_col[:],
                                start=True, stop=True, skip_group_check=True), sem="t")
                p.op("tensor", lambda eng, qt=qt:
                     eng.matmul(ps[:, 2052 + qt:2052 + qt + 1],
                                dql[:, qt * 128:(qt + 1) * 128], ones_col[:],
                                start=True, stop=True, skip_group_check=True), sem="t")
            tm = p.mark("t")
            p.wait("vector", "t", tm)
            vm = p.op("vector", lambda eng: eng.tensor_copy(stats[:], ps[:, 2048:2056]))
            p.wait("tensor", "v", vm)
            p.wait("vector", "v", vm)
            vm = p.op("vector", lambda eng: eng.tensor_scalar(
                numu[:, 4:8], stats[:, 0:4], r2pp[:, 0:1], None, ALU.mult))
            p.wait("scalar", "v", vm)
            sm = p.op("scalar", lambda eng: eng.activation(
                numu[:, 4:8], numu[:, 4:8], AF.Sqrt))
            p.wait("vector", "s", sm)
            vm = p.op("vector", lambda eng: eng.tensor_scalar(
                numu[:, 4:8], numu[:, 4:8], -0.125, 40.0, ALU.mult, ALU.add))
            p.wait("vector", "v", vm)
            vm = p.op("vector", lambda eng: eng.scalar_tensor_tensor(
                numu[:, 0:4], stats[:, 4:8], -0.125, numu[:, 4:8],
                ALU.mult, ALU.min))
            numu_m = vm
            if dbg and hh == 0 and qb == 0:
                p.wait("gpsimd", "v", numu_m)
                p.dma("gpsimd", d_stats, stats[:])
                p.dma("gpsimd", d_numu, numu[:])

            nchunk = (KB + 3) // 4
            first_pv = True
            for ch in range(nchunk):
                kb0 = ch * 4
                nkb = min(4, KB - kb0)
                ncols = nkb * 512
                for qt in range(4):
                    p.wait("tensor", "s", p.mark("s"))
                    for kb in range(nkb):
                        p.op("tensor", lambda eng, qt=qt, kb=kb, kb0=kb0, qb=qb:
                             eng.matmul(ps[:, kb * 512:(kb + 1) * 512],
                                        qlb[:, qt * 128:(qt + 1) * 128],
                                        ckv_fm[:, (kb0 + kb) * 512:(kb0 + kb + 1) * 512],
                                        start=True, stop=True,
                                        skip_group_check=True), sem="t")
                    tm = p.mark("t")
                    if kb0 + nkb == KB:
                        p.wait("vector", "t", tm)
                        vm = p.op("vector", lambda eng, qt=qt, nkb=nkb:
                                  eng.tensor_add(ps[:, (nkb - 1) * 512:nkb * 512],
                                                 ps[:, (nkb - 1) * 512:nkb * 512],
                                                 mask4[:, qt * 512:(qt + 1) * 512]))
                        p.wait("scalar", "v", vm)
                    else:
                        p.wait("scalar", "t", tm)
                    p.wait("scalar", "v", numu_m)
                    sm = p.op("scalar", lambda eng, qt=qt, ncols=ncols:
                              eng.activation(pexp[:, 0:ncols], ps[:, 0:ncols],
                                             AF.Exp, bias=numu[:, qt:qt + 1],
                                             scale=0.125, accum_out=zpart[:]))
                    p.wait("tensor", "s", sm)
                    p.wait("vector", "s", sm)
                    qtg = qb * 4 + qt
                    if ch == 0:
                        vm = p.op("vector", lambda eng, hh=hh, qtg=qtg:
                                  eng.tensor_copy(zc[:, hh * 32 + qtg:hh * 32 + qtg + 1],
                                                  zpart[:]))
                    else:
                        vm = p.op("vector", lambda eng, hh=hh, qtg=qtg:
                                  eng.tensor_add(zc[:, hh * 32 + qtg:hh * 32 + qtg + 1],
                                                 zc[:, hh * 32 + qtg:hh * 32 + qtg + 1],
                                                 zpart[:]))
                    p.wait("scalar", "v", vm)
                    if dbg and hh == 0 and qb == 0 and ch == 0 and qt == 0:
                        p.wait("gpsimd", "s", sm)
                        p.dma("gpsimd", d_pexp, pexp[:, 0:512])
                    for half in range(2):
                        kt0 = half * 8
                        nkt = min(8, ncols // 128 - kt0)
                        if nkt <= 0:
                            continue
                        for kt in range(nkt):
                            p.op("tensor", lambda eng, kt=kt, kt0=kt0, qt=qt:
                                 eng.matmul(ptb[:, kt * 128:(kt + 1) * 128],
                                            pexp[:, (kt0 + kt) * 128:(kt0 + kt + 1) * 128],
                                            ident[:], is_transpose=True,
                                            skip_group_check=True), sem="t")
                        tm = p.mark("t")
                        p.wait("vector", "t", tm)
                        vm = p.op("vector", lambda eng, kt0=kt0, nkt=nkt, qt=qt:
                                  eng.tensor_copy(
                                      pT[:].rearrange("p (kt q) -> p kt q", q=512)
                                      [:, kt0:kt0 + nkt, qt * 128:(qt + 1) * 128],
                                      ptb[:, 0:nkt * 128].rearrange(
                                          "p (kt q) -> p kt q", q=128)))
                        p.wait("tensor", "v", vm)
                    ptm = p.mark("v")
                p.wait("tensor", "v", ptm)
                nkt_c = ncols // 128
                last_ch = (ch == nchunk - 1)
                for kt in range(nkt_c):
                    p.op("tensor", lambda eng, kt=kt, kb0=kb0, first=first_pv,
                         stop=(last_ch and kt == nkt_c - 1):
                         eng.matmul(ps[:, 2560:3072], ckv_tm[kb0 * 4 + kt][:],
                                    pT[:, kt * 512:(kt + 1) * 512],
                                    start=(first and kt == 0),
                                    stop=stop, skip_group_check=True), sem="t")
                first_pv = False
                tm = p.mark("t")
                p.wait("vector", "t", tm)
            tm = p.mark("t")
            p.wait("vector", "t", tm)
            vc = p.op("vector", lambda eng: eng.tensor_copy(ctx_sb[:], ps[:, 2560:3072]))
            p.wait("tensor", "v", vc)
            if dbg and hh == 0 and qb == 0:
                p.wait("gpsimd", "v", vc)
                dmc = p.dma("gpsimd", d_ctx, ctx_sb[:])
                p.wait("vector", "qg", dmc)
            off = 64 * (hh % 2)
            p.op("tensor", lambda eng, hh=hh, off=off:
                 eng.matmul(ps[off:off + 64, 3072:3584],
                            vown[:, hh * 64:(hh + 1) * 64], ctx_sb[:],
                            start=True, stop=True, skip_group_check=True), sem="t")
            tm = p.mark("t")
            p.wait("vector", "t", tm)
            if dmy:
                p.wait("vector", "qg", dmy)
            vy = p.op("vector", lambda eng, off=off: eng.tensor_copy(
                y_sb[off:off + 64, :], ps[off:off + 64, 3072:3584]))
            p.wait("tensor", "v", vy)
            p.wait("gpsimd", "v", vy)
            dmy = p.dma("gpsimd", y_o[hh * 64:(hh + 1) * 64, qb * 512:(qb + 1) * 512],
                        y_sb[off:off + 64, :])
    vm = p.op("vector", lambda eng: eng.reciprocal(zc[:], zc[:]))
    p.wait("tensor", "v", vm)
    for hh in range(8):
        p.op("tensor", lambda eng, hh=hh:
             eng.matmul(ps[0:32, 2048:2176], zc[:, hh * 32:(hh + 1) * 32],
                        identf[:], is_transpose=True,
                        skip_group_check=True), sem="t")
        tm = p.mark("t")
        p.wait("vector", "t", tm)
        vm = p.op("vector", lambda eng, hh=hh: eng.tensor_copy(
            rzt[:], ps[0:32, 2048:2176]))
        p.wait("gpsimd", "v", vm)
        dmz = p.dma("gpsimd", rz_o[hh], rzt[:])
        p.wait("tensor", "qg", dmz)
    p.wait("gpsimd", "qg", p.mark("qg"))
    return p.finalize()



def emit_quant(p, tile, W, am, inv, tq, eng="vector", src=None):
    nb = W // 32
    if src is None:
        src = tile
    vm = p.op("vector", lambda eng_: eng_.reduce_max(
        am[:, 0:nb], src[:, 0:W].rearrange("p (n k) -> p n k", k=32),
        axis=AX.X, apply_absolute_value=True))
    p.wait("vector", "v", vm)
    vm = p.op("vector", lambda eng_: eng_.tensor_scalar_max(
        am[:, 0:nb], am[:, 0:nb], 2.0 ** -30))
    p.wait("vector", "v", vm)
    vm = p.op("vector", lambda eng_: eng_.tensor_single_scalar(
        am[:, 0:nb].bitcast(I32), am[:, 0:nb].bitcast(I32),
        0x7F800000, ALU.bitwise_and))
    p.wait("vector", "v", vm)
    vm = p.op("vector", lambda eng_: eng_.reciprocal(inv[:, 0:nb], am[:, 0:nb]))
    p.wait("vector", "v", vm)
    vm = p.op("vector", lambda eng_: eng_.tensor_scalar_mul(
        inv[:, 0:nb], inv[:, 0:nb], 8.0))
    p.wait(eng, "v", vm)
    e0 = eng[0]
    vm = p.op(eng, lambda eng_: eng_.tensor_tensor(
        tq[:, 0:W].rearrange("p (n k) -> p n k", k=32),
        src[:, 0:W].rearrange("p (n k) -> p n k", k=32),
        inv[:, 0:nb].broadcast_to([128, nb, 32]), ALU.mult))
    p.wait(eng, e0, vm)
    vm = p.op(eng, lambda eng_: eng_.tensor_scalar(
        tq[:, 0:W], tq[:, 0:W], MAGIC, MAGIC, ALU.add, ALU.subtract))
    p.wait(eng, e0, vm)
    vm = p.op(eng, lambda eng_: eng_.scalar_tensor_tensor(
        tile[:, 0:W].rearrange("p (n k) -> p n k", k=32),
        tq[:, 0:W].rearrange("p (n k) -> p n k", k=32), 0.125,
        am[:, 0:nb].broadcast_to([128, nb, 32]), ALU.mult, ALU.mult))
    p.wait("vector", e0, vm)
    return vm



def build_LF(dbg=False):
    NT = T // 128
    p = Prog()
    x2_h = p.dram("x2", [D, T], BF16, "ExternalInput").ap()
    ym_h = p.dram("ym", [D, T], BF16, "ExternalInput").ap()
    rza_h = p.dram("rza", [16, T], BF16, "ExternalInput").ap()
    ind_h = p.dram("ind", [16, D], BF16, "ExternalInput").ap()
    wfc_h = p.dram("w_fc", [HID, D], F32, "ExternalInput").ap()
    bfc_h = p.dram("bfc_row", [1, HID], F32, "ExternalInput").ap()
    wproj_h = p.dram("w_proj", [D, HID], F32, "ExternalInput").ap()
    bproj_h = p.dram("bproj", [DC, 128], F32, "ExternalInput").ap()
    ident_h = p.dram("ident", [128, 128], BF16, "ExternalInput").ap()
    out_h = p.dram("out", [T, D], F32, "ExternalOutput").ap()

    x3 = [p.sbuf(f"x3_{i}", [128, T], BF16) for i in range(DC)]
    x3tm = [p.sbuf(f"x3tm{i}", [128, D], BF16) for i in range(NT)]
    xq = [p.sbuf(f"xq{i}", [128, T], BF16) for i in range(DC)]
    acc = [p.sbuf(f"acc{i}", [128, T], BF16) for i in range(DC)]
    hqT = [p.sbuf(f"hqT{i}", [128, T], BF16) for i in range(4)]
    wfcqT = [p.sbuf(f"wfcqT{i}", [128, 1024], BF16) for i in range(DC)]
    wpqT = [p.sbuf(f"wpqT{i}", [128, D], BF16) for i in range(8)]
    wq_c = p.sbuf("wq_c", [128, 1024], BF16)
    f32buf = p.sbuf("f32buf", [128, 1024], F32)
    stage = f32buf
    rza_sb = p.sbuf("rza_sb", [16, T], BF16)
    ind_sb = p.sbuf("ind_sb", [16, D], BF16)
    bfc_sb = p.sbuf("bfc_sb", [1, HID], BF16)
    bproj_pp = p.sbuf("bproj_pp", [128, DC], F32)
    ones1 = p.sbuf("ones1", [1, 128], BF16)
    ident = p.sbuf("ident_sb", [128, 128], BF16)
    am = p.sbuf("am_q", [128, 32], F32)
    inv = p.sbuf("inv_q", [128, 32], F32)
    tq = p.sbuf("tq_q", [128, 1024], F32)
    xhat = p.sbuf("xhat_t", [128, D], BF16)
    xhat32 = f32buf
    h32 = f32buf
    sq = f32buf
    ln4 = p.sbuf("ln4", [128, 6], F32)
    eps1 = p.sbuf("eps1", [128, 1], F32)
    ps = p.psum("ps", [128, 4096], F32)
    ptb = ps[:, 3584:4096].bitcast(BF16)
    ystage = tq[:].bitcast(BF16)

    p.op("vector", lambda eng: eng.memset(ones1[:], 1.0))
    p.op("vector", lambda eng: eng.memset(eps1[:], 1e-5))
    p.dma("sync", ident[:], ident_h)
    p.dma("sync", rza_sb[:], rza_h)
    p.dma("sync", ind_sb[:], ind_h)
    lm = p.dma("sync", stage[0:1, 0:1024], bfc_h[0:1, 0:1024])
    p.wait("vector", "qs", p.mark("qs"))
    vm = p.op("vector", lambda eng: eng.tensor_copy(bfc_sb[0:1, 0:1024],
                                                    stage[0:1, 0:1024]))
    for piece in range(1, 4):
        p.wait("sync", "v", vm)
        lm = p.dma("sync", stage[0:1, 0:1024], bfc_h[0:1, piece * 1024:(piece + 1) * 1024])
        p.wait("vector", "qs", lm)
        vm = p.op("vector", lambda eng, piece=piece: eng.tensor_copy(
            bfc_sb[0:1, piece * 1024:(piece + 1) * 1024], stage[0:1, 0:1024]))
    for r in range(DC):
        p.dma("sync", bproj_pp[:, r:r + 1], bproj_h[r:r + 1, :].rearrange("o d -> d o"))
    for dc in range(DC):
        p.dma("sync", x3[dc][:], x2_h[dc * 128:(dc + 1) * 128, :])
    ld0 = p.mark("qs")

    p.wait("tensor", "qs", ld0)
    dmy = 0
    for dc in range(DC):
        for tb in range(T // 512):
            p.op("tensor", lambda eng, dc=dc, tb=tb:
                 eng.matmul(ps[:, tb * 512:(tb + 1) * 512],
                            ind_sb[:, dc * 128:(dc + 1) * 128],
                            rza_sb[:, tb * 512:(tb + 1) * 512],
                            start=True, stop=True, skip_group_check=True), sem="t")
        tm = p.mark("t")
        lmy = p.dma("sync", ystage, ym_h[dc * 128:(dc + 1) * 128, :])
        p.wait("vector", "t", tm)
        p.wait("vector", "qs", lmy)
        vm = p.op("vector", lambda eng: eng.tensor_mul(ystage, ystage, ps[:, 0:T]))
        p.wait("vector", "v", vm)
        vm = p.op("vector", lambda eng, dc=dc: eng.tensor_add(
            x3[dc][:], x3[dc][:], ystage))
        p.wait("tensor", "v", vm)
        p.wait("sync", "v", vm)
    x3_done = p.mark("v")

    p.wait("tensor", "v", x3_done)
    x3tm_m = [0] * NT
    for tt in range(NT):
        for dc in range(DC):
            p.op("tensor", lambda eng, tt=tt, dc=dc:
                 eng.matmul(ptb[:, (dc % 4) * 128:(dc % 4) * 128 + 128],
                            x3[dc][:, tt * 128:(tt + 1) * 128], ident[:],
                            is_transpose=True, skip_group_check=True), sem="t")
            tm = p.mark("t")
            p.wait("vector", "t", tm)
            vm = p.op("vector", lambda eng, tt=tt, dc=dc: eng.tensor_copy(
                x3tm[tt][:, dc * 128:(dc + 1) * 128],
                ptb[:, (dc % 4) * 128:(dc % 4) * 128 + 128]))
            p.wait("tensor", "v", vm)
            x3tm_m[tt] = vm

    for tt in range(NT):
        p.wait("scalar", "v", x3tm_m[tt])
        sm = p.op("scalar", lambda eng, tt=tt: eng.activation(sq[:], x3tm[tt][:], AF.Square))
        vm = p.op("vector", lambda eng, tt=tt: eng.reduce_sum(
            ln4[:, 0:1], x3tm[tt][:], axis=AX.X))
        p.wait("vector", "s", sm)
        vm = p.op("vector", lambda eng: eng.reduce_sum(ln4[:, 1:2], sq[:], axis=AX.X))
        p.wait("vector", "v", vm)
        vm = p.op("vector", lambda eng: eng.tensor_scalar(
            ln4[:, 2:3], ln4[:, 0:1], 1.0 / D, None, ALU.mult))
        p.wait("scalar", "v", vm)
        sm = p.op("scalar", lambda eng: eng.activation(ln4[:, 3:4], ln4[:, 2:3], AF.Square))
        p.wait("vector", "s", sm)
        vm = p.op("vector", lambda eng: eng.scalar_tensor_tensor(
            ln4[:, 3:4], ln4[:, 1:2], 1.0 / D, ln4[:, 3:4], ALU.mult, ALU.subtract))
        p.wait("scalar", "v", vm)
        sm = p.op("scalar", lambda eng: eng.activation(ln4[:, 4:5], ln4[:, 3:4],
                                                       AF.Sqrt, bias=eps1[:], scale=1.0))
        p.wait("vector", "s", sm)
        vm = p.op("vector", lambda eng: eng.reciprocal(ln4[:, 4:5], ln4[:, 4:5]))
        p.wait("vector", "v", vm)
        vm = p.op("vector", lambda eng: eng.scalar_tensor_tensor(
            ln4[:, 5:6], ln4[:, 2:3], -1.0, ln4[:, 4:5], ALU.mult, ALU.mult))
        p.wait("vector", "v", vm)
        vm = p.op("vector", lambda eng, tt=tt: eng.scalar_tensor_tensor(
            xhat32[:], x3tm[tt][:], ln4[:, 4:5],
            ln4[:, 5:6].broadcast_to([128, D]), ALU.mult, ALU.add))
        p.wait("vector", "v", vm)
        emit_quant(p, xhat, D, am, inv, tq, src=xhat32)
        p.wait("tensor", "v", p.mark("v"))
        for dc in range(DC):
            p.op("tensor", lambda eng, tt=tt, dc=dc:
                 eng.matmul(ptb[:, (dc % 4) * 128:(dc % 4) * 128 + 128],
                            xhat[:, dc * 128:(dc + 1) * 128], ident[:],
                            is_transpose=True, skip_group_check=True), sem="t")
            tm = p.mark("t")
            p.wait("vector", "t", tm)
            vm = p.op("vector", lambda eng, tt=tt, dc=dc: eng.tensor_copy(
                xq[dc][:, tt * 128:(tt + 1) * 128],
                ptb[:, (dc % 4) * 128:(dc % 4) * 128 + 128]))
            p.wait("tensor", "v", vm)
        p.wait("scalar", "v", p.mark("v"))

    if dbg:
        d_xq = p.dram("d_xq", [D, T], BF16, "ExternalOutput").ap()
        d_hraw = p.dram("d_hraw", [128, 1024], BF16, "ExternalOutput").ap()
        d_hq = p.dram("d_hq", [128, 1024], BF16, "ExternalOutput").ap()
        d_acc0 = p.dram("d_acc0", [128, T], BF16, "ExternalOutput").ap()
        p.wait("gpsimd", "v", p.mark("v"))
        for dc in range(DC):
            p.dma("gpsimd", d_xq[dc * 128:(dc + 1) * 128, :], xq[dc][:])
    wlm = 0
    for grp in range(4):
        p.wait("sync", "s", p.mark("s"))
        p.wait("sync", "v", p.mark("v"))
        for ci in range(8):
            r0 = grp * 1024 + ci * 128
            if wlm:
                p.wait("sync", "v", wlm)
            lmw = p.dma("sync", stage[:, 0:D], wfc_h[r0:r0 + 128, :])
            p.wait("vector", "qs", lmw)
            p.wait("vector", "t", p.mark("t"))
            wlm = emit_quant(p, wq_c, D, am, inv, tq, src=stage)
            p.wait("tensor", "v", wlm)
            for kc in range(DC):
                p.op("tensor", lambda eng, kc=kc:
                     eng.matmul(ptb[:, (kc % 4) * 128:(kc % 4) * 128 + 128],
                                wq_c[:, kc * 128:(kc + 1) * 128], ident[:],
                                is_transpose=True, skip_group_check=True), sem="t")
                tm = p.mark("t")
                p.wait("vector", "t", tm)
                vm = p.op("vector", lambda eng, kc=kc, ci=ci: eng.tensor_copy(
                    wfcqT[kc][:, ci * 128:(ci + 1) * 128],
                    ptb[:, (kc % 4) * 128:(kc % 4) * 128 + 128]))
                p.wait("tensor", "v", vm)
            wlm = p.mark("v")
        for ec in range(DC):
            if wlm:
                p.wait("sync", "v", wlm)
            lmw = p.dma("sync", stage[:, 0:1024],
                        wproj_h[ec * 128:(ec + 1) * 128, grp * 1024:(grp + 1) * 1024])
            p.wait("vector", "qs", lmw)
            p.wait("vector", "t", p.mark("t"))
            wlm = emit_quant(p, wq_c, 1024, am, inv, tq, src=stage)
            p.wait("tensor", "v", wlm)
            for nc8 in range(DC):
                p.op("tensor", lambda eng, nc8=nc8:
                     eng.matmul(ptb[:, (nc8 % 4) * 128:(nc8 % 4) * 128 + 128],
                                wq_c[:, nc8 * 128:(nc8 + 1) * 128], ident[:],
                                is_transpose=True, skip_group_check=True), sem="t")
                tm = p.mark("t")
                p.wait("vector", "t", tm)
                vm = p.op("vector", lambda eng, nc8=nc8, ec=ec: eng.tensor_copy(
                    wpqT[nc8][:, ec * 128:(ec + 1) * 128],
                    ptb[:, (nc8 % 4) * 128:(nc8 % 4) * 128 + 128]))
                p.wait("tensor", "v", vm)
            wlm = p.mark("v")

        for tt in range(NT):
            for kc in range(DC):
                for nb in range(2):
                    p.op("tensor", lambda eng, tt=tt, kc=kc, nb=nb:
                         eng.matmul(ps[:, nb * 512:(nb + 1) * 512],
                                    xq[kc][:, tt * 128:(tt + 1) * 128],
                                    wfcqT[kc][:, nb * 512:(nb + 1) * 512],
                                    start=(kc == 0), stop=False,
                                    skip_group_check=True), sem="t")
            for nb in range(2):
                p.op("tensor", lambda eng, tt=tt, nb=nb, grp=grp:
                     eng.matmul(ps[:, nb * 512:(nb + 1) * 512],
                                ones1[:],
                                bfc_sb[0:1, grp * 1024 + nb * 512:grp * 1024 + (nb + 1) * 512],
                                start=False, stop=True,
                                skip_group_check=True), sem="t")
            tm = p.mark("t")
            p.wait("scalar", "t", tm)
            sm = p.op("scalar", lambda eng, tt=tt: eng.activation(
                h32[:], ps[:, 0:1024], AF.Gelu))
            p.wait("tensor", "s", sm)
            if dbg and grp == 0 and tt == 0:
                p.wait("gpsimd", "s", sm)
                dmh = p.dma("gpsimd", d_hraw, h32[:])
                p.wait("vector", "qg", dmh)
            p.wait("vector", "s", sm)
            qm = emit_quant(p, x3tm[tt], D, am, inv, tq, src=h32)
            p.wait("scalar", "v", qm)
            if dbg and grp == 0 and tt == 0:
                p.wait("gpsimd", "v", qm)
                dmh = p.dma("gpsimd", d_hq, x3tm[0][:])
                p.wait("tensor", "qg", dmh)
        hq_done = p.mark("v")

        for half in range(2):
            p.wait("tensor", "v", hq_done if half == 0 else p.mark("v"))
            for tt in range(NT):
                for nc4 in range(4):
                    p.op("tensor", lambda eng, tt=tt, nc4=nc4, half=half:
                         eng.matmul(ptb[:, nc4 * 128:(nc4 + 1) * 128],
                                    x3tm[tt][:, half * 512 + nc4 * 128:half * 512 + (nc4 + 1) * 128],
                                    ident[:], is_transpose=True,
                                    skip_group_check=True), sem="t")
                tm = p.mark("t")
                p.wait("vector", "t", tm)
                for nc4 in range(4):
                    vm = p.op("vector", lambda eng, tt=tt, nc4=nc4: eng.tensor_copy(
                        hqT[nc4][:, tt * 128:(tt + 1) * 128],
                        ptb[:, nc4 * 128:(nc4 + 1) * 128]))
                p.wait("tensor", "v", vm)
            for es in range(DC):
                for nc4 in range(4):
                    for tb in range(T // 512):
                        fn = (lambda eng, es=es, nc4=nc4, tb=tb, half=half:
                              eng.matmul(ps[:, 2048 + tb * 512:2048 + (tb + 1) * 512],
                                         wpqT[half * 4 + nc4][:, es * 128:(es + 1) * 128],
                                         hqT[nc4][:, tb * 512:(tb + 1) * 512],
                                         start=(nc4 == 0), stop=(nc4 == 3),
                                         skip_group_check=True))
                        if nc4 == 3 and tb == T // 512 - 1:
                            p.op("tensor", fn, sem="t")
                        else:
                            p.opn("tensor", fn)
                tm = p.mark("t")
                first = (grp == 0 and half == 0)
                if first:
                    p.wait("scalar", "t", tm)
                    sm = p.op("scalar", lambda eng, es=es: eng.activation(
                        acc[es][:], ps[:, 2048:4096], AF.Identity,
                        bias=bproj_pp[:, es:es + 1], scale=1.0))
                    p.wait("tensor", "s", sm)
                else:
                    p.wait("vector", "t", tm)
                    vm = p.op("vector", lambda eng, es=es: eng.tensor_add(
                        acc[es][:], acc[es][:], ps[:, 2048:4096]))
                    p.wait("tensor", "v", vm)

    if dbg:
        p.wait("gpsimd", "v", p.mark("v"))
        dmh = p.dma("gpsimd", d_acc0, acc[0][:])
        p.wait("vector", "qg", dmh)
    p.wait("tensor", "v", p.mark("v"))
    fstage = tq
    dmo = 0
    for dc in range(DC):
        vm = p.op("vector", lambda eng, dc=dc: eng.tensor_add(
            acc[dc][:], acc[dc][:], x3[dc][:]))
        p.wait("tensor", "v", vm)
    for tt in range(NT):
        for dc in range(DC):
            p.op("tensor", lambda eng, tt=tt, dc=dc:
                 eng.matmul(ptb[:, (dc % 4) * 128:(dc % 4) * 128 + 128],
                            acc[dc][:, tt * 128:(tt + 1) * 128], ident[:],
                            is_transpose=True, skip_group_check=True), sem="t")
            tm = p.mark("t")
            p.wait("vector", "t", tm)
            if dmo:
                p.wait("vector", "qg", dmo)
            vm = p.op("vector", lambda eng, tt=tt, dc=dc: eng.tensor_copy(
                fstage[:, dc * 128:(dc + 1) * 128],
                ptb[:, (dc % 4) * 128:(dc % 4) * 128 + 128]))
            p.wait("tensor", "v", vm)
        vm = p.mark("v")
        p.wait("gpsimd", "v", vm)
        dmo = p.dma("gpsimd", out_h[tt * 128:(tt + 1) * 128, :], fstage[:, 0:D])
    p.wait("gpsimd", "qg", p.mark("qg"))
    return p.finalize()





PAIRS = [[0, 1], [2, 3], [4, 5], [6, 7]]
GRP8 = [list(range(8))]


def build_fused():
    import contextlib as _ctx
    p = Prog()
    xT = p.dram("xT", [D, T], F32, "ExternalInput").ap()
    wg_h = p.dram("Wg", [D, D], F32, "ExternalInput").ap()
    wv_h = p.dram("Wv", [D, D], F32, "ExternalInput").ap()
    wd_h = p.dram("Wd", [D, D], F32, "ExternalInput").ap()
    wmix_h = p.dram("wmix_row", [1, D], F32, "ExternalInput").ap()
    bias3_h = p.dram("bias3", [3 * DC, 128], F32, "ExternalInput").ap()
    wffn_h = p.dram("wffn_row", [1, D], F32, "ExternalInput").ap()
    wgate_h = p.dram("W_gate", [D, FF], F32, "ExternalInput").ap()
    wup_h = p.dram("W_up", [D, FF], F32, "ExternalInput").ap()
    wout_h = p.dram("W_out", [FF, D], F32, "ExternalInput").ap()
    wrT_h = p.dram("W_rT", [D, D], F32, "ExternalInput").ap()
    wkT_h = p.dram("W_kT", [D, D], F32, "ExternalInput").ap()
    wvT_h = p.dram("W_vT", [D, D], F32, "ExternalInput").ap()
    mu2_h = p.dram("mu2", [2 * DC, 128], F32, "ExternalInput").ap()
    wdkvT_h = p.dram("W_dkvT", [D, 128], F32, "ExternalInput").ap()
    wdq_h = p.dram("W_dq", [128, D], F32, "ExternalInput").ap()
    wuqT_h = p.dram("W_uqT_p", [128, D], F32, "ExternalInput").ap()
    wdqp_h = p.dram("W_dq_p", [128, D], F32, "ExternalInput").ap()
    wukp_h = p.dram("W_uk_p", [D, 128], F32, "ExternalInput").ap()
    wuv_h = p.dram("W_uv", [D, 128], F32, "ExternalInput").ap()
    woTp_h = p.dram("W_oT_p", [D, D], F32, "ExternalInput").ap()
    ident_h = p.dram("ident", [128, 128], BF16, "ExternalInput").ap()
    identf_h = p.dram("identf", [128, 128], F32, "ExternalInput").ap()
    mask4_h = p.dram("mask4", [4, 128, 512], F32, "ExternalInput").ap()
    wfcsh_h = p.dram("wfc_sh", [512, D], F32, "ExternalInput").ap()
    bfc_h = p.dram("bfc_row", [1, HID], F32, "ExternalInput").ap()
    wpjsh_h = p.dram("wpj_sh", [128, HID], F32, "ExternalInput").ap()
    bproj_h = p.dram("bproj", [DC, 128], F32, "ExternalInput").ap()
    ind_h = p.dram("ind", [16, D], BF16, "ExternalInput").ap()
    flag_h = p.dram("flag", [128, 1], F32, "ExternalInput").ap()
    out_h = p.dram("out", [T, D], F32, "ExternalOutput").ap()

    a_dram = p.nc.dram_tensor("a_dram", [D, T], BF16)
    cc_carry_i = p.nc.dram_tensor("cc_carry_i", [128, DC], F32)
    cc_carry_o = p.nc.dram_tensor("cc_carry_o", [256, DC], F32)
    cc_pcol_i = p.nc.dram_tensor("cc_pcol_i", [128, DC], BF16)
    cc_pcol_o = p.nc.dram_tensor("cc_pcol_o", [256, DC], BF16)
    cc_gx_i = p.nc.dram_tensor("cc_gx_i", [D, T], BF16)
    cc_gx_o = p.nc.dram_tensor("cc_gx_o", [2 * D, T], BF16)
    cc_gc_i = p.nc.dram_tensor("cc_gc_i", [128, T], BF16)
    cc_gc_o = p.nc.dram_tensor("cc_gc_o", [256, T], BF16)
    cc_gy_i = p.nc.dram_tensor("cc_gy_i", [512, TT], BF16)
    cc_gy_o = p.nc.dram_tensor("cc_gy_o", [1024, TT], BF16)
    cc_gz_i = p.nc.dram_tensor("cc_gz_i", [256, 128], BF16)
    cc_gz_o = p.nc.dram_tensor("cc_gz_o", [512, 128], BF16)
    wfcT_i = p.nc.dram_tensor("wfcT_i", [1024, 512], BF16)
    wfcT_o = p.nc.dram_tensor("wfcT_o", [8192, 512], BF16)
    wpjT_i = p.nc.dram_tensor("wpjT_i", [4096, 128], BF16)
    wpjT_o = p.nc.dram_tensor("wpjT_o", [32768, 128], BF16)

    def gather(din, dout, groups=PAIRS):
        cm = p.op("gpsimd", lambda eng: eng.collective_compute(
            "AllGather", ALU.bypass, replica_groups=groups,
            ins=[din.ap().opt()], outs=[dout.ap().opt()]), sem="c2", k=1)
        p.wait("gpsimd", "c2", cm)
        return cm

    PR = _ctx.ExitStack()
    PR1 = _ctx.ExitStack()
    P12 = _ctx.ExitStack()
    stream = [p.sbuf(f"strm{i}", [128, T], BF16) for i in range(DC)]
    ones_col = p.sbuf("ones_col", [128, 1], BF16)
    ones_row = p.sbuf("ones_row", [1, 128], F32)
    ones_rowb = p.sbuf("ones_rowb", [1, 128], BF16)
    eps1 = p.sbuf("eps1", [1, 1], F32)
    eps2 = p.sbuf("eps2", [1, 1], F32)
    eps1c = p.sbuf("eps1c", [128, 1], F32)
    flag = p.sbuf("flag_sb", [128, 1], F32)
    ident = p.sbuf("ident_sb", [128, 128], BF16)
    stage = p.sbuf("stage_g", [128, 1024], F32)
    sq = p.sbuf("sq_g", [128, T], BF16)
    ps = p.psum("ps", [128, 4096], F32)
    ptb = ps[:, 3584:4096].bitcast(BF16)

    p.op("vector", lambda eng: eng.memset(ones_col[:], 1.0))
    p.op("vector", lambda eng: eng.memset(ones_row[:], 1.0))
    p.op("vector", lambda eng: eng.memset(ones_rowb[:], 1.0))
    p.op("vector", lambda eng: eng.memset(eps1[:], 1e-5))
    p.op("vector", lambda eng: eng.memset(eps2[:], 1e-6))
    p.op("vector", lambda eng: eng.memset(eps1c[:], 1e-5))
    p.dma("sync", ident[:], ident_h)
    p.dma("sync", flag[:], flag_h)

    P1 = _ctx.ExitStack()
    r_rstd = p.sbuf("r_rstd", [1, T], F32, stack=PR)
    r_shp = p.sbuf("r_shp", [1, T], F32, stack=PR)
    h = [p.sbuf(f"h{i}", [128, T], BF16, stack=P12) for i in range(DC)]
    r_var = p.sbuf("rw_var", [1, T], F32, stack=PR1)
    r_rho = p.sbuf("rw_rho", [1, T], F32, stack=PR1)
    xn = [p.sbuf(f"p1xn{i}", [128, T], BF16, stack=P1) for i in range(DC)]
    wbf = {w: [p.sbuf(f"p1w{w}{i}", [128, D], BF16, stack=P1) for i in range(DC)]
           for w in "gvd"}
    w_row = p.sbuf("p1wrow", [1, D], F32, stack=P1)
    bias_pp = p.sbuf("p1bias", [128, 3 * DC], F32, stack=P1)
    tv = sq
    a_f = p.sbuf("p1af", [128, T], F32, stack=P1)
    xs_f = p.sbuf("p1xs", [128, T], F32, stack=P1)
    asta = stage[:].bitcast(BF16)
    sg = asta

    p.dma("sync", w_row[:], wmix_h)
    for r in range(3 * DC):
        p.dma("sync", bias_pp[:, r:r + 1],
              bias3_h[r:r + 1, :].rearrange("o d -> d o"))
    ldv = p.mark("qs")
    vm = 0
    for kc in range(DC):
        for hf in range(2):
            if vm:
                p.wait("sync", "v", vm)
            lm = p.dma("sync", stage[:, 0:1024],
                       xT[kc * 128:(kc + 1) * 128, hf * 1024:(hf + 1) * 1024])
            p.wait("vector", "qs", lm)
            vm = p.op("vector", lambda eng, kc=kc, hf=hf: eng.tensor_copy(
                stream[kc][:, hf * 1024:(hf + 1) * 1024], stage[:, 0:1024]))
    xb_done = vm
    p.wait("scalar", "v", xb_done)
    p.wait("scalar", "qs", ldv)
    tm = emit_ln_stats(p, ps, stream, sq, ones_col)
    p.wait("vector", "t", tm)
    rowm = emit_ln_rows4(p, ps, r_shp, r_var, r_rstd, r_rho, eps1[:], eps2[:])
    p.wait("tensor", "v", rowm)
    for dc in range(DC):
        emit_bcast_apply(p, ps, w_row[0:1, dc * 128:(dc + 1) * 128],
                         r_rho, r_var, stream[dc], xn[dc])
    xn_done = p.mark("v")
    wm = xb_done
    for w, hbm in zip("gvd", (wg_h, wv_h, wd_h)):
        p.wait("vector", "v", xn_done)
        wm = emit_load_w_bf16(p, hbm, D, D, stage, wbf[w], stage_guard=wm)
    p.wait("tensor", "v", wm)

    def proj_p1(w, dc, b0):
        for kc in range(DC):
            for tb in range(T // 512):
                fn = (lambda eng, kc=kc, tb=tb, dc=dc, b0=b0, w=w:
                      eng.matmul(ps[:, (b0 + tb) * 512:(b0 + tb + 1) * 512],
                                 wbf[w][kc][:, dc * 128:(dc + 1) * 128],
                                 xn[kc][:, tb * 512:(tb + 1) * 512],
                                 start=(kc == 0), stop=(kc == DC - 1)))
                if kc == DC - 1 and tb == T // 512 - 1:
                    p.op("tensor", fn, sem="t")
                else:
                    p.opn("tensor", fn)
        return p.mark("t")

    for dc in range(DC):
        tm = proj_p1("g", dc, 0)
        p.wait("scalar", "t", tm)
        smg = p.op("scalar", lambda eng, dc=dc: eng.activation(
            sg[:], ps[:, 0:T], AF.Sigmoid, bias=bias_pp[:, dc:dc + 1], scale=1.0))
        tm = proj_p1("v", dc, 4)
        p.wait("scalar", "t", tm)
        smv = p.op("scalar", lambda eng, dc=dc: eng.activation(
            tv[:], ps[:, T:2 * T], AF.Tanh,
            bias=bias_pp[:, DC + dc:DC + dc + 1], scale=1.0))
        p.wait("tensor", "s", smg)
        tm = proj_p1("d", dc, 0)
        p.wait("scalar", "t", tm)
        smd = p.op("scalar", lambda eng, dc=dc: eng.activation(
            a_f[:], ps[:, 0:T], AF.Sigmoid,
            bias=bias_pp[:, 2 * DC + dc:2 * DC + dc + 1], scale=1.0))
        p.wait("tensor", "s", smd)
        p.wait("tensor", "s", smv)
        p.wait("vector", "s", smd)
        p.op("vector", lambda eng: eng.tensor_scalar(
            a_f[:], a_f[:], 0.998, 0.001, ALU.mult, ALU.add))
        p.op("vector", lambda eng: eng.tensor_mul(xs_f[:], sg[:], tv[:]))
        p.op("vector", lambda eng, dc=dc: eng.tensor_tensor_scan(
            h[dc][:], a_f[:], xs_f[:], 0.0, ALU.mult, ALU.add))
        vm = p.op("vector", lambda eng: eng.tensor_tensor_scan(
            asta, a_f[:], a_f[:], 1.0, ALU.mult, ALU.bypass))
        p.wait("gpsimd", "v", vm)
        dm = p.dma("gpsimd", a_dram.ap()[dc * 128:(dc + 1) * 128, :], asta)
        p.wait("vector", "qg", dm)
        p.wait("scalar", "qg", dm)
    p.barrier()
    P1.close()
    PR1.close()

    P2 = _ctx.ExitStack()
    GRP = 6
    NGRP = FF // 128 // GRP
    yn = [p.sbuf(f"p2yn{i}", [128, T], BF16, stack=P2) for i in range(DC)]
    mt = [p.sbuf(f"p2mt{i}", [128, T], BF16, stack=P2) for i in range(GRP)]
    wo_sl = [p.sbuf(f"p2wo{i}", [128, D], BF16, stack=P2) for i in range(GRP)]
    wg_sl = [p.sbuf(f"p2wg{i}", [128, D], BF16, stack=P2) for i in range(2)]
    wu_sl = [p.sbuf(f"p2wu{i}", [128, D], BF16, stack=P2) for i in range(2)]
    w_row2 = p.sbuf("p2wrow", [1, D], F32, stack=P2)
    carry = p.sbuf("p2carry", [128, DC], F32, stack=P2)
    tmpb = p.sbuf("p2tmpb", [128, T], BF16, stack=P2)

    p.dma("sync", w_row2[:], wffn_h)
    for dc in range(DC):
        p.op("vector", lambda eng, dc=dc: eng.tensor_copy(
            carry[:, dc:dc + 1], h[dc][:, T - 1:T]))
    vm = p.mark("v")
    p.wait("gpsimd", "v", vm)
    dm = p.dma("gpsimd", cc_carry_i.ap(), carry[:])
    p.wait("gpsimd", "qg", p.mark("qg"))
    gather(cc_carry_i, cc_carry_o)
    dm = p.dma("gpsimd", carry[:], cc_carry_o.ap()[0:128, :])
    p.wait("vector", "qg", p.mark("qg"))
    vm = p.op("vector", lambda eng: eng.tensor_scalar(
        carry[:], carry[:], flag[:, 0:1], None, ALU.mult))
    p.wait("vector", "v", vm)
    rb_s = p.sbuf("rb2s", [1, T], BF16, stack=P2)
    rb_t = p.sbuf("rb2t", [1, T], BF16, stack=P2)
    p.wait("vector", "t", p.mark("t"))
    emit_row_hilo(p, r_rstd, rb_s)
    hl = emit_row_hilo(p, r_shp, rb_t)
    p.wait("tensor", "v", hl)
    for dc in range(DC):
        lm = p.dma("sync", tmpb[:], a_dram.ap()[dc * 128:(dc + 1) * 128, :])
        p.wait("vector", "qs", lm)
        vm = p.op("vector", lambda eng, dc=dc: eng.scalar_tensor_tensor(
            h[dc][:], tmpb[:], carry[:, dc:dc + 1], h[dc][:], ALU.mult, ALU.add))
        p.wait("sync", "v", vm)
        emit_bcast_apply(p, ps, ones_row[0:1, 0:128], r_rstd, r_shp,
                         stream[dc], tmpb,
                         rows_b=(rb_s, rb_t), ones_b=ones_rowb[:])
        vm = p.op("vector", lambda eng, dc=dc: eng.tensor_add(
            h[dc][:], h[dc][:], tmpb[:]))
        p.wait("sync", "v", vm)
    xm = h
    xm_done = p.mark("v")

    p.wait("scalar", "v", xm_done)
    for kc in range(DC):
        sm = p.op("scalar", lambda eng, kc=kc: eng.activation(sq[:], xm[kc][:], AF.Square))
        p.wait("tensor", "s", sm)
        for tb in range(T // 512):
            p.op("tensor", lambda eng, kc=kc, tb=tb:
                 eng.matmul(ps[0:1, tb * 512:(tb + 1) * 512], ones_col[:],
                            sq[:, tb * 512:(tb + 1) * 512],
                            start=(kc == 0), stop=(kc == DC - 1)), sem="t")
        p.wait("scalar", "t", p.mark("t"))
    tm = p.mark("t")
    p.wait("vector", "t", tm)
    vm = p.op("vector", lambda eng: eng.tensor_scalar(
        r_shp[:], ps[0:1, 0:T], 1.0 / D, None, ALU.mult))
    p.wait("scalar", "v", vm)
    sm = p.op("scalar", lambda eng: eng.activation(r_shp[:], r_shp[:], AF.Sqrt,
                                                   bias=eps2[:], scale=1.0))
    p.wait("vector", "s", sm)
    rowm = p.op("vector", lambda eng: eng.reciprocal(r_rstd[:], r_shp[:]))
    p.wait("tensor", "v", rowm)
    for dc in range(DC):
        for tb in range(T // 512):
            fn = (lambda eng, dc=dc, tb=tb:
                  eng.matmul(ps[:, tb * 512:(tb + 1) * 512],
                             w_row2[0:1, dc * 128:(dc + 1) * 128],
                             r_rstd[0:1, tb * 512:(tb + 1) * 512],
                             start=True, stop=True, skip_group_check=True))
            if tb == T // 512 - 1:
                p.op("tensor", fn, sem="t")
            else:
                p.opn("tensor", fn)
        tm = p.mark("t")
        p.wait("vector", "t", tm)
        vm = p.op("vector", lambda eng, dc=dc: eng.tensor_mul(
            yn[dc][:], xm[dc][:], ps[:, 0:T]))
        p.wait("tensor", "v", vm)

    p.wait("sync", "v", p.mark("v"))
    _lc = [0]

    def ffn_load(grp2, gi2):
        fc = grp2 * GRP + gi2
        b = gi2 % 2
        wt = p.mark("t")
        if _lc[0]:
            p.wait("sync", "v", _lc[0])
        lm = p.dma("sync", stage[:, 0:D].rearrange("p (a n) -> p a n", n=128),
                   wgate_h[:, fc * 128:(fc + 1) * 128]
                   .rearrange("(a p) n -> p a n", p=128))
        p.wait("vector", "qs", lm)
        p.wait("vector", "t", wt)
        vg = p.op("vector", lambda eng, b=b: eng.tensor_copy(wg_sl[b][:],
                                                             stage[:, 0:D]))
        p.wait("sync", "v", vg)
        lm = p.dma("sync", stage[:, 0:D].rearrange("p (a n) -> p a n", n=128),
                   wup_h[:, fc * 128:(fc + 1) * 128]
                   .rearrange("(a p) n -> p a n", p=128))
        p.wait("vector", "qs", lm)
        vu = p.op("vector", lambda eng, b=b: eng.tensor_copy(wu_sl[b][:],
                                                             stage[:, 0:D]))
        p.wait("sync", "v", vu)
        lm = p.dma("sync", stage[:, 0:D], wout_h[fc * 128:(fc + 1) * 128, :])
        p.wait("vector", "qs", lm)
        vo = p.op("vector", lambda eng, gi2=gi2: eng.tensor_copy(
            wo_sl[gi2][:], stage[:, 0:D]))
        _lc[0] = vo
        return vg, vu, vo

    lds = {(0, 0): ffn_load(0, 0)}
    for grp in range(NGRP):
        for gi in range(GRP):
            b = gi % 2
            vg, vu, vo = lds[(grp, gi)]
            p.wait("tensor", "v", vg)
            for kc in range(DC):
                for tb in range(T // 512):
                    fn = (lambda eng, kc=kc, tb=tb, b=b:
                          eng.matmul(ps[:, tb * 512:(tb + 1) * 512],
                                     wg_sl[b][:, kc * 128:(kc + 1) * 128],
                                     yn[kc][:, tb * 512:(tb + 1) * 512],
                                     start=(kc == 0), stop=(kc == DC - 1)))
                    if kc == DC - 1 and tb == T // 512 - 1:
                        p.op("tensor", fn, sem="t")
                    else:
                        p.opn("tensor", fn)
            tm = p.mark("t")
            if gi + 1 < GRP:
                lds[(grp, gi + 1)] = ffn_load(grp, gi + 1)
            p.wait("scalar", "t", tm)
            sm = p.op("scalar", lambda eng: eng.activation(tmpb[:], ps[:, 0:T], AF.Silu))
            p.wait("tensor", "s", sm)
            p.wait("tensor", "v", vu)
            for kc in range(DC):
                for tb in range(T // 512):
                    fn = (lambda eng, kc=kc, tb=tb, b=b:
                          eng.matmul(ps[:, T + tb * 512:T + (tb + 1) * 512],
                                     wu_sl[b][:, kc * 128:(kc + 1) * 128],
                                     yn[kc][:, tb * 512:(tb + 1) * 512],
                                     start=(kc == 0), stop=(kc == DC - 1)))
                    if kc == DC - 1 and tb == T // 512 - 1:
                        p.op("tensor", fn, sem="t")
                    else:
                        p.opn("tensor", fn)
            tm = p.mark("t")
            p.wait("vector", "t", tm)
            p.wait("vector", "s", sm)
            vm = p.op("vector", lambda eng, gi=gi: eng.tensor_mul(
                mt[gi][:], tmpb[:], ps[:, T:2 * T]))
            p.wait("tensor", "v", vm)
            p.wait("scalar", "v", vm)
        mdone = p.mark("v")
        p.wait("tensor", "v", mdone)
        for dc in range(DC):
            for gi in range(GRP):
                for tb in range(T // 512):
                    fn = (lambda eng, gi=gi, tb=tb, dc=dc:
                          eng.matmul(ps[:, tb * 512:(tb + 1) * 512],
                                     wo_sl[gi][:, dc * 128:(dc + 1) * 128],
                                     mt[gi][:, tb * 512:(tb + 1) * 512],
                                     start=(gi == 0), stop=(gi == GRP - 1)))
                    if gi == GRP - 1 and tb == T // 512 - 1:
                        p.op("tensor", fn, sem="t")
                    else:
                        p.opn("tensor", fn)
            tm = p.mark("t")
            p.wait("vector", "t", tm)
            vm = p.op("vector", lambda eng, dc=dc: eng.tensor_add(
                xm[dc][:], xm[dc][:], ps[:, 0:T]))
            p.wait("tensor", "v", vm)
        if grp + 1 < NGRP:
            lds[(grp + 1, 0)] = ffn_load(grp + 1, 0)
    p.wait("vector", "v", p.mark("v"))
    for dc in range(DC):
        vm = p.op("vector", lambda eng, dc=dc: eng.tensor_add(
            stream[dc][:], stream[dc][:], xm[dc][:]))
    p.barrier()
    P2.close()
    P12.close()

    P3 = _ctx.ExitStack()
    xr = [p.sbuf(f"p3xr{i}", [128, T], BF16, stack=P3) for i in range(DC)]
    xk = [p.sbuf(f"p3xk{i}", [128, T], BF16, stack=P3) for i in range(DC)]
    rt = [p.sbuf(f"p3rt{i}", [128, T], BF16, stack=P3) for i in range(DC)]
    ka = xr
    xh2 = sq
    wbf3 = [p.sbuf(f"p3w{i}", [128, D], BF16, stack=P3) for i in range(DC)]
    mu_pp = p.sbuf("p3mu", [128, 4 * DC], F32, stack=P3)
    pcol = p.sbuf("p3pcol", [128, DC], BF16, stack=P3)
    pchat = p.sbuf("p3pchat", [128, DC], BF16, stack=P3)
    prow = p.sbuf("p3prow", [1, 2 * DC], F32, stack=P3)
    scr3 = p.sbuf("p3scr", [1, T], F32, stack=P3)
    pval = p.sbuf("p3pval", [1, 4], F32, stack=P3)
    ppb = p.sbuf("p3ppb", [128, 2], F32, stack=P3)

    for r in range(DC):
        p.dma("sync", mu_pp[:, r:r + 1], mu2_h[r:r + 1, :].rearrange("o d -> d o"))
        p.dma("sync", mu_pp[:, 2 * DC + r:2 * DC + r + 1],
              mu2_h[DC + r:DC + r + 1, :].rearrange("o d -> d o"))
    mul_ld = p.mark("qs")
    p.wait("vector", "qs", mul_ld)
    p.op("vector", lambda eng: eng.tensor_scalar(
        mu_pp[:, DC:2 * DC], mu_pp[:, 0:DC], -1.0, 1.0, ALU.mult, ALU.add))
    p.op("vector", lambda eng: eng.tensor_scalar(
        mu_pp[:, 3 * DC:4 * DC], mu_pp[:, 2 * DC:3 * DC], -1.0, 1.0, ALU.mult, ALU.add))
    for dc in range(DC):
        p.op("vector", lambda eng, dc=dc: eng.tensor_copy(
            pcol[:, dc:dc + 1], stream[dc][:, T - 1:T]))
    vm = p.mark("v")
    p.wait("gpsimd", "v", vm)
    p.dma("gpsimd", cc_pcol_i.ap(), pcol[:])
    p.wait("gpsimd", "qg", p.mark("qg"))
    gather(cc_pcol_i, cc_pcol_o)
    p.dma("gpsimd", pcol[:], cc_pcol_o.ap()[0:128, :])
    p.wait("vector", "qg", p.mark("qg"))
    vm = p.op("vector", lambda eng: eng.tensor_scalar(
        pcol[:], pcol[:], flag[:, 0:1], None, ALU.mult))
    p.wait("scalar", "v", vm)
    p.wait("tensor", "v", vm)

    tm = emit_ln_stats(p, ps, stream, sq, ones_col)
    p.wait("vector", "t", tm)
    rowm = emit_ln_rows_plain3(p, ps, scr3, r_rstd, r_shp, eps1[:])
    sm = p.op("scalar", lambda eng: eng.activation(xh2[:, 0:DC], pcol[:], AF.Square))
    p.wait("tensor", "s", sm)
    p.wait("tensor", "v", rowm)
    p.op("tensor", lambda eng: eng.matmul(ps[0:1, 0:DC], ones_col[:],
                                          pcol[:], start=True, stop=True,
                                          skip_group_check=True), sem="t")
    tm2 = p.op("tensor", lambda eng: eng.matmul(ps[0:1, DC:2 * DC],
                                                ones_col[:], xh2[:, 0:DC],
                                                start=True, stop=True,
                                                skip_group_check=True), sem="t")
    p.wait("vector", "t", tm2)
    vm = p.op("vector", lambda eng: eng.tensor_copy(prow[:], ps[0:1, 0:2 * DC]))
    p.wait("vector", "v", vm)
    vm = p.op("vector", lambda eng: eng.reduce_sum(pval[0:1, 0:1], prow[0:1, 0:DC], axis=AX.X))
    vm = p.op("vector", lambda eng: eng.reduce_sum(pval[0:1, 1:2], prow[0:1, DC:2 * DC], axis=AX.X))
    p.wait("vector", "v", vm)
    vm = p.op("vector", lambda eng: eng.tensor_scalar(
        pval[0:1, 2:3], pval[0:1, 0:1], 1.0 / D, None, ALU.mult))
    p.wait("scalar", "v", vm)
    sm = p.op("scalar", lambda eng: eng.activation(pval[0:1, 3:4], pval[0:1, 2:3], AF.Square))
    p.wait("vector", "s", sm)
    vm = p.op("vector", lambda eng: eng.scalar_tensor_tensor(
        pval[0:1, 3:4], pval[0:1, 1:2], 1.0 / D, pval[0:1, 3:4], ALU.mult, ALU.subtract))
    p.wait("scalar", "v", vm)
    sm = p.op("scalar", lambda eng: eng.activation(pval[0:1, 3:4], pval[0:1, 3:4],
                                                   AF.Sqrt, bias=eps1[:], scale=1.0))
    p.wait("vector", "s", sm)
    vm = p.op("vector", lambda eng: eng.reciprocal(pval[0:1, 3:4], pval[0:1, 3:4]))
    p.wait("vector", "v", vm)
    vm2 = p.op("vector", lambda eng: eng.scalar_tensor_tensor(
        pval[0:1, 2:3], pval[0:1, 2:3], -1.0, pval[0:1, 3:4], ALU.mult, ALU.mult))
    p.wait("tensor", "v", vm2)
    p.op("tensor", lambda eng: eng.matmul(ps[:, 0:1],
                                          ones_row[0:1, :], pval[0:1, 3:4],
                                          start=True, stop=True,
                                          skip_group_check=True), sem="t")
    tm2 = p.op("tensor", lambda eng: eng.matmul(ps[:, 1:2],
                                                ones_row[0:1, :], pval[0:1, 2:3],
                                                start=True, stop=True,
                                                skip_group_check=True), sem="t")
    p.wait("vector", "t", tm2)
    vm = p.op("vector", lambda eng: eng.tensor_copy(ppb[:], ps[:, 0:2]))
    p.wait("vector", "v", vm)
    vm = p.op("vector", lambda eng: eng.scalar_tensor_tensor(
        pchat[:], pcol[:], ppb[:, 0:1], ppb[:, 1:2].broadcast_to([128, DC]),
        ALU.mult, ALU.add))
    rb_s = p.sbuf("rb3s", [1, T], BF16, stack=P3)
    rb_t = p.sbuf("rb3t", [1, T], BF16, stack=P3)
    p.wait("vector", "t", p.mark("t"))
    emit_row_hilo(p, r_rstd, rb_s)
    emit_row_hilo(p, r_shp, rb_t)
    p.wait("tensor", "v", p.mark("v"))
    for dc in range(DC):
        emit_bcast_apply(p, ps, ones_row[0:1, 0:128], r_rstd, r_shp,
                         stream[dc], xh2,
                         rows_b=(rb_s, rb_t), ones_b=ones_rowb[:])
        vm = p.op("vector", lambda eng, dc=dc: eng.tensor_scalar(
            xr[dc][:, 1:T], xh2[:, 0:T - 1], mu_pp[:, DC + dc:DC + dc + 1],
            None, ALU.mult))
        vm = p.op("vector", lambda eng, dc=dc: eng.tensor_scalar(
            xr[dc][:, 0:1], pchat[:, dc:dc + 1], mu_pp[:, DC + dc:DC + dc + 1],
            None, ALU.mult))
        vm = p.op("vector", lambda eng, dc=dc: eng.tensor_scalar(
            xk[dc][:, 1:T], xh2[:, 0:T - 1], mu_pp[:, 3 * DC + dc:3 * DC + dc + 1],
            None, ALU.mult))
        vm = p.op("vector", lambda eng, dc=dc: eng.tensor_scalar(
            xk[dc][:, 0:1], pchat[:, dc:dc + 1], mu_pp[:, 3 * DC + dc:3 * DC + dc + 1],
            None, ALU.mult))
        vm = p.op("vector", lambda eng, dc=dc: eng.scalar_tensor_tensor(
            xr[dc][:], xh2[:], mu_pp[:, dc:dc + 1], xr[dc][:], ALU.mult, ALU.add))
        vm = p.op("vector", lambda eng, dc=dc: eng.scalar_tensor_tensor(
            xk[dc][:], xh2[:], mu_pp[:, 2 * DC + dc:2 * DC + dc + 1], xk[dc][:],
            ALU.mult, ALU.add))

    def proj_p3(src, dc, b0):
        for kc in range(DC):
            for tb in range(T // 512):
                fn = (lambda eng, kc=kc, tb=tb, dc=dc, b0=b0:
                      eng.matmul(ps[:, (b0 + tb) * 512:(b0 + tb + 1) * 512],
                                 wbf3[kc][:, dc * 128:(dc + 1) * 128],
                                 src[kc][:, tb * 512:(tb + 1) * 512],
                                 start=(kc == 0), stop=(kc == DC - 1)))
                if kc == DC - 1 and tb == T // 512 - 1:
                    p.op("tensor", fn, sem="t")
                else:
                    p.opn("tensor", fn)
        return p.mark("t")

    wm = emit_load_w_bf16(p, wrT_h, D, D, stage, wbf3, stage_guard=p.mark("v"))
    p.wait("tensor", "v", wm)
    for dc in range(DC):
        b0 = 0 if dc % 2 == 0 else 4
        tm = proj_p3(xr, dc, b0)
        p.wait("scalar", "t", tm)
        sm = p.op("scalar", lambda eng, dc=dc, b0=b0: eng.activation(
            rt[dc][:], ps[:, b0 * 512:b0 * 512 + T], AF.Sigmoid))
        p.wait("tensor", "s", sm)
    p.wait("vector", "t", p.mark("t"))
    wm = emit_load_w_bf16(p, wkT_h, D, D, stage, wbf3, stage_guard=wm)
    p.wait("tensor", "v", wm)
    for dc in range(DC):
        b0 = 0 if dc % 2 == 0 else 4
        tm = proj_p3(xk, dc, b0)
        p.wait("scalar", "t", tm)
        sm = p.op("scalar", lambda eng, dc=dc, b0=b0: eng.activation(
            xh2[:], ps[:, b0 * 512:b0 * 512 + T], AF.Relu))
        p.wait("scalar", "s", sm)
        sm = p.op("scalar", lambda eng, dc=dc: eng.activation(
            ka[dc][:], xh2[:], AF.Square))
        p.wait("tensor", "s", sm)
    p.wait("vector", "t", p.mark("t"))
    wm = emit_load_w_bf16(p, wvT_h, D, D, stage, wbf3, stage_guard=wm)
    p.wait("tensor", "v", wm)
    for dc in range(DC):
        b0 = 0 if dc % 2 == 0 else 4
        tm = proj_p3(ka, dc, b0)
        p.wait("vector", "t", tm)
        vm = p.op("vector", lambda eng, dc=dc, b0=b0: eng.tensor_mul(
            xh2[:], rt[dc][:], ps[:, b0 * 512:b0 * 512 + T]))
        p.wait("vector", "v", vm)
        vm = p.op("vector", lambda eng, dc=dc: eng.tensor_add(
            stream[dc][:], stream[dc][:], xh2[:]))
        p.wait("tensor", "v", vm)
    p.barrier()
    P3.close()

    P4 = _ctx.ExitStack()
    x3l = [p.sbuf(f"p4x3{i}", [128, T], BF16, stack=P4) for i in range(DC)]
    wdkv = [p.sbuf(f"p4wdkv{i}", [128, 128], BF16, stack=P4) for i in range(DC)]
    ckvl = p.sbuf("p4ckv", [128, T], BF16, stack=P4)
    scr4 = p.sbuf("p4scr", [1, T], F32, stack=P4)
    wm = 0
    for kc in range(DC):
        if wm:
            p.wait("sync", "v", wm)
        lm = p.dma("sync", stage[:, 0:128], wdkvT_h[kc * 128:(kc + 1) * 128, :])
        p.wait("vector", "qs", lm)
        wm = p.op("vector", lambda eng, kc=kc: eng.tensor_copy(wdkv[kc][:], stage[:, 0:128]))
    tm = emit_ln_stats(p, ps, stream, sq, ones_col)
    p.wait("vector", "t", tm)
    rowm = emit_ln_rows_plain3(p, ps, scr4, r_rstd, r_shp, eps1[:])
    rb_s = p.sbuf("rb4s", [1, T], BF16, stack=P4)
    rb_t = p.sbuf("rb4t", [1, T], BF16, stack=P4)
    p.wait("vector", "t", p.mark("t"))
    emit_row_hilo(p, r_rstd, rb_s)
    hl = emit_row_hilo(p, r_shp, rb_t)
    p.wait("tensor", "v", hl)
    dmx = 0
    for dc in range(DC):
        vm = emit_bcast_apply(p, ps, ones_row[0:1, 0:128], r_rstd, r_shp,
                              stream[dc], x3l[dc],
                              rows_b=(rb_s, rb_t), ones_b=ones_rowb[:])
        p.wait("gpsimd", "v", vm)
        dmx = p.dma("gpsimd", cc_gx_i.ap()[dc * 128:(dc + 1) * 128, :], x3l[dc][:])
    p.wait("tensor", "v", wm)
    for tb in range(T // 512):
        for kc in range(DC):
            fn = (lambda eng, kc=kc, tb=tb:
                  eng.matmul(ps[:, tb * 512:(tb + 1) * 512], wdkv[kc][:],
                             x3l[kc][:, tb * 512:(tb + 1) * 512],
                             start=(kc == 0), stop=(kc == DC - 1)))
            if tb == T // 512 - 1 and kc == DC - 1:
                p.op("tensor", fn, sem="t")
            else:
                p.opn("tensor", fn)
    tm = p.mark("t")
    p.wait("vector", "t", tm)
    vm = p.op("vector", lambda eng: eng.tensor_copy(ckvl[:], ps[:, 0:T]))
    p.wait("gpsimd", "v", vm)
    p.dma("gpsimd", cc_gc_i.ap(), ckvl[:])
    p.wait("gpsimd", "qg", p.mark("qg"))

    wsh = [p.sbuf(f"wq4s{i}", [128, 1024], BF16, stack=P4) for i in range(4)]
    wtq = p.sbuf("wq4tq", [128, 1024], F32, stack=P4)
    wam = p.sbuf("wq4am", [128, 32], F32, stack=P4)
    winv = p.sbuf("wq4in", [128, 32], F32, stack=P4)
    tevac = p.sbuf("wq4ev", [128, 1024], BF16, stack=P4)
    qm = wm
    for wrt in range(4):
        p.wait("sync", "v", qm)
        lw = p.dma("sync", stage[:, 0:1024], wfcsh_h[wrt * 128:(wrt + 1) * 128, :])
        p.wait("vector", "qs", lw)
        qm = emit_quant(p, wsh[wrt], 1024, wam, winv, wtq, src=stage)
    p.wait("tensor", "v", qm)
    dmw = 0
    for wk2 in range(4):
        for wsb in range(2):
            wkc = wk2 * 2 + wsb
            for wrt in range(4):
                fn = (lambda eng, wkc=wkc, wrt=wrt, wsb=wsb:
                      eng.matmul(ptb[:, (wsb * 4 + wrt) * 128:(wsb * 4 + wrt + 1) * 128],
                                 wsh[wrt][:, wkc * 128:(wkc + 1) * 128], ident[:],
                                 is_transpose=True, skip_group_check=True))
                if wsb == 1 and wrt == 3:
                    p.op("tensor", fn, sem="t")
                else:
                    p.opn("tensor", fn)
        tm = p.mark("t")
        p.wait("vector", "t", tm)
        if dmw:
            p.wait("vector", "qs", dmw)
        ev = p.op("vector", lambda eng: eng.tensor_copy(tevac[:], ptb[:, 0:1024]))
        p.wait("tensor", "v", ev)
        p.wait("sync", "v", ev)
        for wsb in range(2):
            wkc = wk2 * 2 + wsb
            dmw = p.dma("sync", wfcT_i.ap()[wkc * 128:(wkc + 1) * 128, :],
                        tevac[:, wsb * 512:wsb * 512 + 512])
    for wpc in range(4):
        p.wait("sync", "v", qm)
        lw = p.dma("sync", stage[:, 0:1024], wpjsh_h[:, wpc * 1024:(wpc + 1) * 1024])
        p.wait("vector", "qs", lw)
        p.wait("vector", "t", p.mark("t"))
        qm = emit_quant(p, wsh[wpc], 1024, wam, winv, wtq, src=stage)
    p.wait("tensor", "v", qm)
    for wpc in range(4):
        for wkc in range(8):
            fn = (lambda eng, wpc=wpc, wkc=wkc:
                  eng.matmul(ptb[:, wkc * 128:(wkc + 1) * 128],
                             wsh[wpc][:, wkc * 128:(wkc + 1) * 128], ident[:],
                             is_transpose=True, skip_group_check=True))
            if wkc == 7:
                p.op("tensor", fn, sem="t")
            else:
                p.opn("tensor", fn)
        tm = p.mark("t")
        p.wait("vector", "t", tm)
        p.wait("vector", "qs", dmw)
        ev = p.op("vector", lambda eng: eng.tensor_copy(tevac[:], ptb[:, 0:1024]))
        p.wait("tensor", "v", ev)
        p.wait("sync", "v", ev)
        for wkc in range(8):
            dmw = p.dma("sync",
                        wpjT_i.ap()[(wpc * 8 + wkc) * 128:(wpc * 8 + wkc + 1) * 128, :],
                        tevac[:, wkc * 128:(wkc + 1) * 128])

    gather(cc_gx_i, cc_gx_o)
    gather(cc_gc_i, cc_gc_o)
    p.barrier()
    P4.close()
    PR.close()

    P5 = _ctx.ExitStack()
    x3f = [p.sbuf(f"p5x3f{i}", [128, TT], BF16, stack=P5) for i in range(DC)]
    ckv_fm = p.sbuf("p5ckvfm", [128, TT], BF16, stack=P5)
    ckv_tm = [p.sbuf(f"p5ckvtm{i}", [128, 128], BF16, stack=P5) for i in range(NKT)]
    mown = [p.sbuf(f"p5mown{i}", [128, D], BF16, stack=P5) for i in range(DC)]
    vown = p.sbuf("p5vown", [128, 512], BF16, stack=P5)
    at1 = [p.sbuf(f"p5at1_{i}", [128, D], BF16, stack=P5) for i in range(4)]
    at2 = [p.sbuf(f"p5at2_{i}", [128, 512], BF16, stack=P5) for i in range(DC)]
    kloc = [p.sbuf(f"p5kloc{i}", [128, 128], BF16, stack=P5) for i in range(4)]
    wtmp = p.sbuf("p5wtmp", [128, D], BF16, stack=P5)
    identf = p.sbuf("p5identf", [128, 128], F32, stack=P5)
    mask4 = p.sbuf("p5mask4", [128, 4 * 512], F32, stack=P5)
    pexp = p.sbuf("p5pexp", [128, 2048], BF16, stack=P5)
    pT = p.sbuf("p5pT", [128, 16 * 512], BF16, stack=P5)
    qlb = [p.sbuf(f"p5qlb{i}", [128, 512], BF16, stack=P5) for i in range(2)]
    sqq = p.sbuf("p5sqq", [128, 512], BF16, stack=P5)
    dql = p.sbuf("p5dql", [128, 512], BF16, stack=P5)
    stats = p.sbuf("p5stats", [128, 16], F32, stack=P5)
    numu = p.sbuf("p5numu", [128, 16], F32, stack=P5)
    r2pp = p.sbuf("p5r2pp", [128, 1], F32, stack=P5)
    zc = p.sbuf("p5zc", [128, 8 * 32], F32, stack=P5)
    zpart = p.sbuf("p5zpart", [128, 1], F32, stack=P5)
    ctx_sb = p.sbuf("p5ctx", [128, 512], BF16, stack=P5)
    y_sb = p.sbuf("p5ysb", [128, 512], BF16, stack=P5)
    rzt = p.sbuf("p5rzt", [32, 128], BF16, stack=P5)
    sqrow = p.sbuf("p5sqrow", [128, 512], BF16, stack=P5)
    r2row = p.sbuf("p5r2row", [1, TT // 512], F32, stack=P5)
    wuk_sb = [p.sbuf(f"p5wuk{i}", [128, 128], BF16, stack=P5) for i in range(DC)]
    wdq_sb = p.sbuf("p5wdq", [128, D], BF16, stack=P5)
    wuv_sb = p.sbuf("p5wuv", [128, 128], BF16, stack=P5)

    p.dma("sync", identf[:], identf_h)
    for i in range(4):
        p.dma("sync", mask4[:, i * 512:(i + 1) * 512], mask4_h[i])
    for dc in range(DC):
        p.dma("sync", x3f[dc][:, 0:T], cc_gx_o.ap()[dc * 128:(dc + 1) * 128, :])
        p.dma("sync", x3f[dc][:, T:TT], cc_gx_o.ap()[D + dc * 128:D + (dc + 1) * 128, :])
    p.dma("sync", ckv_fm[:, 0:T], cc_gc_o.ap()[0:128, :])
    lm = p.dma("sync", ckv_fm[:, T:TT], cc_gc_o.ap()[128:256, :])
    in_ld = p.mark("qs")

    def loadw5(hbm, ncols, dst, guard):
        if guard:
            p.wait("sync", "v", guard)
        lm = p.dma("sync", stage[:, 0:ncols], hbm)
        p.wait("vector", "qs", lm)
        return p.op("vector", lambda eng: eng.tensor_copy(dst[:, 0:ncols],
                                                          stage[:, 0:ncols]))
    wuqm = loadw5(wuqT_h, D, wtmp, 0)
    wdqm = loadw5(wdq_h, D, wdq_sb, wuqm)
    p.wait("tensor", "v", wdqm)
    for jc in range(4):
        for nb in range(2):
            p.op("tensor", lambda eng, jc=jc, nb=nb:
                 eng.matmul(ps[:, nb * 512:(nb + 1) * 512],
                            wtmp[:, jc * 128:(jc + 1) * 128],
                            wdq_sb[:, nb * 512:(nb + 1) * 512],
                            start=True, stop=True, skip_group_check=True), sem="t")
        tm = p.mark("t")
        p.wait("vector", "t", tm)
        vm = p.op("vector", lambda eng, jc=jc: eng.tensor_copy(at1[jc][:], ps[:, 0:1024]))
        p.wait("tensor", "v", vm)
    p.wait("vector", "t", p.mark("t"))
    wdqm = loadw5(wdqp_h, D, wdq_sb, wdqm)
    p.wait("tensor", "v", wdqm)
    for jc in range(DC):
        p.op("tensor", lambda eng, jc=jc:
             eng.matmul(ps[:, 0:512], wtmp[:, jc * 128:(jc + 1) * 128],
                        wdq_sb[:, 0:512],
                        start=True, stop=True, skip_group_check=True), sem="t")
        tm = p.mark("t")
        p.wait("vector", "t", tm)
        vm = p.op("vector", lambda eng, jc=jc: eng.tensor_copy(at2[jc][:], ps[:, 0:512]))
        p.wait("tensor", "v", vm)
    wukm = wdqm
    for jc in range(DC):
        wukm = loadw5(wukp_h[jc * 128:(jc + 1) * 128, :], 128, wuk_sb[jc], wukm)
    p.wait("tensor", "v", wukm)
    for ds in range(4):
        for jc in range(DC):
            p.op("tensor", lambda eng, ds=ds, jc=jc:
                 eng.matmul(ps[:, 1024 + ds * 128:1024 + (ds + 1) * 128],
                            at2[jc][:, ds * 128:(ds + 1) * 128], wuk_sb[jc][:],
                            start=(jc == 0), stop=(jc == DC - 1)), sem="t")
    tm = p.mark("t")
    p.wait("vector", "t", tm)
    for ds in range(4):
        vm = p.op("vector", lambda eng, ds=ds: eng.tensor_copy(
            kloc[ds][:], ps[:, 1024 + ds * 128:1024 + (ds + 1) * 128]))
    p.wait("tensor", "v", p.mark("v"))
    for hh in range(8):
        tile, off = hh // 2, 64 * (hh % 2)
        for d1c in range(DC):
            p.op("tensor", lambda eng, hh=hh, tile=tile, off=off, d1c=d1c:
                 eng.matmul(ps[:, 2048 + (d1c % 4) * 128:2048 + (d1c % 4) * 128 + 128],
                            at1[tile][off:off + 64, d1c * 128:(d1c + 1) * 128],
                            kloc[tile][off:off + 64, :],
                            start=True, stop=True, skip_group_check=True), sem="t")
            tm = p.mark("t")
            p.wait("vector", "t", tm)
            vm = p.op("vector", lambda eng, hh=hh, d1c=d1c: eng.tensor_copy(
                mown[d1c][:, hh * 128:(hh + 1) * 128],
                ps[:, 2048 + (d1c % 4) * 128:2048 + (d1c % 4) * 128 + 128]))
            p.wait("tensor", "v", vm)
    p.wait("vector", "t", p.mark("t"))
    wuvm = wukm
    for cc in range(DC):
        wuvm = loadw5(wuv_h[cc * 128:(cc + 1) * 128, :], 128, wuv_sb, wuvm)
        guard = loadw5(woTp_h[cc * 128:(cc + 1) * 128, 0:512], 512, wtmp, wuvm)
        p.wait("tensor", "v", guard)
        p.op("tensor", lambda eng, cc=cc:
             eng.matmul(ps[:, 3072:3584], wuv_sb[:], wtmp[:, 0:512],
                        start=(cc == 0), stop=(cc == DC - 1)), sem="t")
        wuvm = guard
        p.wait("vector", "t", p.mark("t"))
    tm = p.mark("t")
    p.wait("vector", "t", tm)
    vm = p.op("vector", lambda eng: eng.tensor_copy(vown[:], ps[:, 3072:3584]))
    p.wait("tensor", "v", vm)

    p.wait("tensor", "qs", in_ld)
    for kt in range(NKT):
        p.op("tensor", lambda eng, kt=kt:
             eng.matmul(ptb[:, (kt % 8) * 128:(kt % 8) * 128 + 128],
                        ckv_fm[:, kt * 128:(kt + 1) * 128], ident[:],
                        is_transpose=True, skip_group_check=True), sem="t")
        tm = p.mark("t")
        p.wait("vector", "t", tm)
        vm = p.op("vector", lambda eng, kt=kt: eng.tensor_copy(
            ckv_tm[kt][:], ptb[:, (kt % 8) * 128:(kt % 8) * 128 + 128]))
        p.wait("tensor", "v", vm)
    p.wait("scalar", "qs", in_ld)
    r2m = p.op("vector", lambda eng: eng.memset(r2row[:], 0.0))
    for tb in range(TT // 512):
        sm = p.op("scalar", lambda eng, tb=tb: eng.activation(
            sqrow[:], ckv_fm[:, tb * 512:(tb + 1) * 512], AF.Square))
        p.wait("tensor", "s", sm)
        tmx = p.op("tensor", lambda eng:
                   eng.matmul(ps[0:1, 2048:2560], ones_col[:], sqrow[:],
                              start=True, stop=True, skip_group_check=True),
                   sem="t")
        p.wait("vector", "t", tmx)
        p.wait("vector", "v", r2m)
        r2m = p.op("vector", lambda eng, tb=tb: eng.reduce_max(
            r2row[0:1, tb:tb + 1], ps[0:1, 2048:2560], axis=AX.X))
        p.wait("tensor", "v", r2m)
        p.wait("scalar", "t", p.mark("t"))
    p.wait("vector", "v", r2m)
    r2m = p.op("vector", lambda eng: eng.reduce_max(
        r2row[0:1, 0:1], r2row[0:1, 0:TT // 512], axis=AX.X))
    p.wait("tensor", "v", r2m)
    p.op("tensor", lambda eng: eng.matmul(
        ps[:, 2048:2049], ones_row[0:1, :], r2row[0:1, 0:1],
        start=True, stop=True, skip_group_check=True), sem="t")
    tm = p.mark("t")
    p.wait("vector", "t", tm)
    r2m = p.op("vector", lambda eng: eng.tensor_copy(r2pp[:], ps[:, 2048:2049]))
    p.wait("tensor", "v", r2m)

    dmy = 0
    iters = [(hh, qb) for hh in range(8) for qb in range(NQB)]

    def emit_proA(it, slot):
        hh, qb = iters[it]
        for dc in range(DC):
            fn = (lambda eng, dc=dc, hh=hh, qb=qb:
                  eng.matmul(ps[:, 2048:2560],
                             mown[dc][:, hh * 128:(hh + 1) * 128],
                             x3f[dc][:, qb * 512:(qb + 1) * 512],
                             start=(dc == 0), stop=(dc == DC - 1),
                             skip_group_check=True))
            if dc == DC - 1:
                p.op("tensor", fn, sem="t")
            else:
                p.opn("tensor", fn)
        tm = p.mark("t")
        p.wait("vector", "t", tm)
        vq = p.op("vector", lambda eng, slot=slot: eng.tensor_copy(
            qlb[slot][:], ps[:, 2048:2560]))
        return vq

    def emit_proB(it, slot, vq):
        hh, qb = iters[it]
        S0 = slot * 8
        p.wait("scalar", "v", vq)
        sm = p.op("scalar", lambda eng, slot=slot: eng.activation(
            sqq[:], qlb[slot][:], AF.Square))
        vd = p.op("vector", lambda eng, qb=qb, slot=slot: eng.tensor_mul(
            dql[:], qlb[slot][:], ckv_fm[:, qb * 512:(qb + 1) * 512]))
        p.wait("tensor", "v", vq)
        p.wait("tensor", "s", sm)
        p.wait("tensor", "v", vd)
        for qt in range(4):
            p.opn("tensor", lambda eng, qt=qt:
                  eng.matmul(ps[:, 2048 + qt:2048 + qt + 1],
                             sqq[:, qt * 128:(qt + 1) * 128], ones_col[:],
                             start=True, stop=True, skip_group_check=True))
            fn = (lambda eng, qt=qt:
                  eng.matmul(ps[:, 2052 + qt:2052 + qt + 1],
                             dql[:, qt * 128:(qt + 1) * 128], ones_col[:],
                             start=True, stop=True, skip_group_check=True))
            if qt == 3:
                p.op("tensor", fn, sem="t")
            else:
                p.opn("tensor", fn)
        tm = p.mark("t")
        p.wait("vector", "t", tm)
        vm = p.op("vector", lambda eng, S0=S0: eng.tensor_copy(
            stats[:, S0:S0 + 8], ps[:, 2048:2056]))
        p.wait("tensor", "v", vm)
        p.wait("vector", "v", vm)
        vm = p.op("vector", lambda eng, S0=S0: eng.tensor_scalar(
            numu[:, S0 + 4:S0 + 8], stats[:, S0:S0 + 4], r2pp[:, 0:1], None,
            ALU.mult))
        p.wait("scalar", "v", vm)
        sm = p.op("scalar", lambda eng, S0=S0: eng.activation(
            numu[:, S0 + 4:S0 + 8], numu[:, S0 + 4:S0 + 8], AF.Ln))
        p.wait("scalar", "s", sm)
        sm = p.op("scalar", lambda eng, S0=S0: eng.activation(
            numu[:, S0 + 4:S0 + 8], numu[:, S0 + 4:S0 + 8], AF.Exp, scale=0.5))
        p.wait("vector", "s", sm)
        vm = p.op("vector", lambda eng, S0=S0: eng.tensor_scalar(
            numu[:, S0 + 4:S0 + 8], numu[:, S0 + 4:S0 + 8], -0.125, 40.0,
            ALU.mult, ALU.add))
        p.wait("vector", "v", vm)
        vm = p.op("vector", lambda eng, S0=S0: eng.scalar_tensor_tensor(
            numu[:, S0:S0 + 4], stats[:, S0 + 4:S0 + 8], -0.125,
            numu[:, S0 + 4:S0 + 8], ALU.mult, ALU.min))
        return vm

    def emit_body(it, slot, numu_m, start_s, nextA, nextB, prev_epi):
        nonlocal dmy
        hh, qb = iters[it]
        S0 = slot * 8
        KB = qb + 1
        nch2 = (KB + 1) // 2
        stages = [(c2, qt) for c2 in range(nch2) for qt in range(4)]
        ns = len(stages)
        exp_m = {}
        trans_m = {}
        evac_m = {}
        qk_marks = {}
        first_pv = [True]
        p.wait("tensor", "s", start_s)

        def emit_qk(si):
            c2, qt = stages[si]
            buf = si % 2
            kb0 = c2 * 2
            nkb = min(2, KB - kb0)
            diag = (kb0 + nkb == KB)
            if si - 2 >= 0:
                p.wait("tensor", "s", exp_m[si - 2])
            if diag:
                if si - 2 >= 0:
                    p.wait("vector", "s", exp_m[si - 2])
                else:
                    p.wait("vector", "s", start_s)
                mv = p.op("vector", lambda eng, qt=qt, nkb=nkb, buf=buf:
                          eng.tensor_copy(
                              ps[:, (buf * 2 + nkb - 1) * 512:(buf * 2 + nkb) * 512],
                              mask4[:, qt * 512:(qt + 1) * 512]))
                p.wait("tensor", "v", mv)
            for kb in range(nkb):
                st = not (diag and kb == nkb - 1)
                fn = (lambda eng, qt=qt, kb=kb, kb0=kb0, buf=buf, slot=slot, st=st:
                      eng.matmul(ps[:, (buf * 2 + kb) * 512:(buf * 2 + kb + 1) * 512],
                                 qlb[slot][:, qt * 128:(qt + 1) * 128],
                                 ckv_fm[:, (kb0 + kb) * 512:(kb0 + kb + 1) * 512],
                                 start=st, stop=True, skip_group_check=True))
                if kb == nkb - 1:
                    p.op("tensor", fn, sem="t")
                else:
                    p.opn("tensor", fn)
            return p.mark("t")

        def emit_exp(si):
            c2, qt = stages[si]
            buf = si % 2
            kb0 = c2 * 2
            nkb = min(2, KB - kb0)
            ncols = nkb * 512
            qk_m = qk_marks[si]
            p.wait("scalar", "t", qk_m)
            p.wait("scalar", "v", numu_m)
            if si - 2 in trans_m:
                p.wait("scalar", "t", trans_m[si - 2])
            sm = p.op("scalar", lambda eng, qt=qt, ncols=ncols, buf=buf, S0=S0:
                      eng.activation(pexp[:, buf * 1024:buf * 1024 + ncols],
                                     ps[:, buf * 1024:buf * 1024 + ncols],
                                     AF.Exp, bias=numu[:, S0 + qt:S0 + qt + 1],
                                     scale=0.125, accum_out=zpart[:]))
            exp_m[si] = sm
            p.wait("vector", "s", sm)
            qtg = qb * 4 + qt
            if c2 == 0:
                vm = p.op("vector", lambda eng, hh=hh, qtg=qtg:
                          eng.tensor_copy(zc[:, hh * 32 + qtg:hh * 32 + qtg + 1],
                                          zpart[:]))
            else:
                vm = p.op("vector", lambda eng, hh=hh, qtg=qtg:
                          eng.tensor_add(zc[:, hh * 32 + qtg:hh * 32 + qtg + 1],
                                         zc[:, hh * 32 + qtg:hh * 32 + qtg + 1],
                                         zpart[:]))
            p.wait("scalar", "v", vm)

        def emit_trans(si):
            c2, qt = stages[si]
            buf = si % 2
            kb0 = c2 * 2
            nkb = min(2, KB - kb0)
            nkt = nkb * 4
            p.wait("tensor", "s", exp_m[si])
            if si - 1 in evac_m:
                p.wait("tensor", "v", evac_m[si - 1])
            for kt in range(nkt):
                fn = (lambda eng, kt=kt, buf=buf:
                      eng.matmul(ptb[:, kt * 128:(kt + 1) * 128],
                                 pexp[:, buf * 1024 + kt * 128:buf * 1024 + (kt + 1) * 128],
                                 ident[:], is_transpose=True,
                                 skip_group_check=True))
                if kt == nkt - 1:
                    p.op("tensor", fn, sem="t")
                else:
                    p.opn("tensor", fn)
            trans_m[si] = p.mark("t")
            p.wait("vector", "t", trans_m[si])
            base = (c2 % 2) * 8
            vm = p.op("vector", lambda eng, base=base, nkt=nkt, qt=qt:
                      eng.tensor_copy(
                          pT[:].rearrange("p (kt q) -> p kt q", q=512)
                          [:, base:base + nkt, qt * 128:(qt + 1) * 128],
                          ptb[:, 0:nkt * 128].rearrange("p (kt q) -> p kt q", q=128)))
            evac_m[si] = vm

        def emit_pv(si):
            c2, qt = stages[si]
            kb0 = c2 * 2
            nkb = min(2, KB - kb0)
            nkt_c = nkb * 4
            base = (c2 % 2) * 8
            p.wait("tensor", "v", evac_m[si])
            last_c2 = (c2 == nch2 - 1)
            for kt in range(nkt_c):
                fn = (lambda eng, kt=kt, kb0=kb0, base=base, first=first_pv[0],
                      stop=(last_c2 and kt == nkt_c - 1):
                      eng.matmul(ps[:, 2560:3072], ckv_tm[kb0 * 4 + kt][:],
                                 pT[:, (base + kt) * 512:(base + kt + 1) * 512],
                                 start=(first and kt == 0),
                                 stop=stop, skip_group_check=True))
                if kt == nkt_c - 1:
                    p.op("tensor", fn, sem="t")
                else:
                    p.opn("tensor", fn)
            first_pv[0] = False

        for si in range(ns + 1):
            if si < ns:
                qk_marks[si] = emit_qk(si)
            if si == 1:
                if prev_epi is not None:
                    prev_epi()
                if nextA is not None:
                    nextA()
            if si - 1 >= 0:
                sj = si - 1
                emit_exp(sj)
                emit_trans(sj)
                if stages[sj][1] == 3:
                    emit_pv(sj)
        end_s = p.mark("s")

        def epi():
            nonlocal dmy
            tm = p.mark("t")
            p.wait("vector", "t", tm)
            vc = p.op("vector", lambda eng: eng.tensor_copy(ctx_sb[:], ps[:, 2560:3072]))
            p.wait("tensor", "v", vc)
            off = 64 * (hh % 2)
            if epi_state["vy"]:
                p.wait("tensor", "v", epi_state["vy"])
            p.op("tensor", lambda eng, hh=hh, off=off:
                 eng.matmul(ps[off:off + 64, 3072:3584],
                            vown[:, hh * 64:(hh + 1) * 64], ctx_sb[:],
                            start=True, stop=True, skip_group_check=True), sem="t")
            tm = p.mark("t")
            p.wait("vector", "t", tm)
            if dmy:
                p.wait("vector", "qg", dmy)
            vy = p.op("vector", lambda eng, off=off: eng.tensor_copy(
                y_sb[off:off + 64, :], ps[off:off + 64, 3072:3584]))
            epi_state["vy"] = vy
            p.wait("gpsimd", "v", vy)
            dmy = p.dma("gpsimd",
                        cc_gy_i.ap()[hh * 64:(hh + 1) * 64, qb * 512:(qb + 1) * 512],
                        y_sb[off:off + 64, :])
        if nextB is not None:
            nextB()
        return end_s, epi

    epi_state = {"vy": 0}
    vqA = {0: emit_proA(0, 0)}
    numu_ms = {0: emit_proB(0, 0, vqA[0])}
    start_s = p.mark("s")
    prev_epi = None
    for it in range(len(iters)):
        slot = it % 2
        nxtA = nxtB = None
        if it + 1 < len(iters):
            def mkA(itn, sl):
                return lambda: vqA.__setitem__(itn, emit_proA(itn, sl))
            def mkB(itn, sl):
                return lambda: numu_ms.__setitem__(
                    itn, emit_proB(itn, sl, vqA[itn]))
            nxtA = mkA(it + 1, (it + 1) % 2)
            nxtB = mkB(it + 1, (it + 1) % 2)
        start_s, prev_epi = emit_body(it, slot, numu_ms[it], start_s,
                                      nxtA, nxtB, prev_epi)
    prev_epi()

    vm = p.op("vector", lambda eng: eng.reciprocal(zc[:], zc[:]))
    p.wait("tensor", "v", vm)
    for hh in range(8):
        p.op("tensor", lambda eng, hh=hh:
             eng.matmul(ps[0:32, 2048:2176], zc[:, hh * 32:(hh + 1) * 32],
                        identf[:], is_transpose=True,
                        skip_group_check=True), sem="t")
        tm = p.mark("t")
        p.wait("vector", "t", tm)
        vm = p.op("vector", lambda eng, hh=hh: eng.tensor_copy(
            rzt[:], ps[0:32, 2048:2176]))
        p.wait("gpsimd", "v", vm)
        dmz = p.dma("gpsimd", cc_gz_i.ap()[hh * 32:(hh + 1) * 32, :], rzt[:])
        p.wait("tensor", "qg", dmz)
        p.wait("vector", "qg", dmz)
    p.wait("gpsimd", "qg", p.mark("qg"))
    gather(cc_gy_i, cc_gy_o)
    gather(cc_gz_i, cc_gz_o)
    p.barrier()
    gather(wfcT_i, wfcT_o, GRP8)
    wpj_cm = gather(wpjT_i, wpjT_o, GRP8)
    P5.close()

    P6 = _ctx.ExitStack()
    NT = T // 128
    x3tm = [p.sbuf(f"p6x3tm{i}", [128, D], BF16, stack=P6) for i in range(NT)]
    xq = [p.sbuf(f"p6xq{i}", [128, T], BF16, stack=P6) for i in range(DC)]
    hqT = [p.sbuf(f"p6hqT{i}", [128, T], BF16, stack=P6) for i in range(4)]
    wfcqT = [p.sbuf(f"p6wfcqT{i}", [128, 1024], BF16, stack=P6) for i in range(DC)]
    wpqT = [p.sbuf(f"p6wpqT{i}", [128, D], BF16, stack=P6) for i in range(8)]
    f32buf = p.sbuf("p6f32", [128, 1024], F32, stack=P6)
    f32b2 = p.sbuf("p6f32b", [128, 1024], F32, stack=P6)
    rza_sb = p.sbuf("p6rza", [16, T], BF16, stack=P6)
    rzb_sb = p.sbuf("p6rzb", [16, T], BF16, stack=P6)
    ind_sb = p.sbuf("p6ind", [16, D], BF16, stack=P6)
    bfc_sb = p.sbuf("p6bfc", [1, HID], BF16, stack=P6)
    bproj_pp = p.sbuf("p6bproj", [128, DC], F32, stack=P6)
    am = p.sbuf("p6am", [128, 32], F32, stack=P6)
    inv = p.sbuf("p6inv", [128, 32], F32, stack=P6)
    tq = p.sbuf("p6tq", [128, 1024], F32, stack=P6)
    xhat = p.sbuf("p6xhat", [128, D], BF16, stack=P6)
    ln4 = p.sbuf("p6ln4", [128, 12], F32, stack=P6)
    yA = f32buf[:].bitcast(BF16)
    yB = sq[:]

    p.dma("sync", ind_sb[:], ind_h)
    rz_flat = cc_gz_o.ap().rearrange("(h q) w -> h (q w)", q=32)
    p.dma("sync", rza_sb[:], rz_flat[:, 0:T])
    p.dma("sync", rzb_sb[:], rz_flat[:, T:TT])
    lm = p.dma("sync", f32buf[0:1, 0:1024], bfc_h[0:1, 0:1024])
    p.wait("vector", "qs", p.mark("qs"))
    vm = p.op("vector", lambda eng: eng.tensor_copy(bfc_sb[0:1, 0:1024],
                                                    f32buf[0:1, 0:1024]))
    for piece in range(1, 4):
        p.wait("sync", "v", vm)
        lm = p.dma("sync", f32buf[0:1, 0:1024],
                   bfc_h[0:1, piece * 1024:(piece + 1) * 1024])
        p.wait("vector", "qs", lm)
        vm = p.op("vector", lambda eng, piece=piece: eng.tensor_copy(
            bfc_sb[0:1, piece * 1024:(piece + 1) * 1024], f32buf[0:1, 0:1024]))
    for r in range(DC):
        p.dma("sync", bproj_pp[:, r:r + 1], bproj_h[r:r + 1, :].rearrange("o d -> d o"))
    lm = p.mark("qs")
    p.wait("vector", "qs", lm)
    vm = p.op("vector", lambda eng: eng.tensor_sub(rzb_sb[:], rza_sb[:], rzb_sb[:]))
    p.wait("vector", "v", vm)
    vm = p.op("vector", lambda eng: eng.tensor_scalar(
        rzb_sb[:], rzb_sb[:], flag[0:16, 0:1], None, ALU.mult))
    p.wait("vector", "v", vm)
    vm = p.op("vector", lambda eng: eng.tensor_sub(rza_sb[:], rza_sb[:], rzb_sb[:]))
    p.wait("vector", "v", p.mark("v"))

    p.wait("tensor", "qs", p.mark("qs"))
    p.wait("tensor", "v", p.mark("v"))
    for dc in range(DC):
        for tb in range(T // 512):
            fn = (lambda eng, dc=dc, tb=tb:
                  eng.matmul(ps[:, tb * 512:(tb + 1) * 512],
                             ind_sb[:, dc * 128:(dc + 1) * 128],
                             rza_sb[:, tb * 512:(tb + 1) * 512],
                             start=True, stop=True, skip_group_check=True))
            if tb == T // 512 - 1:
                p.op("tensor", fn, sem="t")
            else:
                p.opn("tensor", fn)
        tm = p.mark("t")
        lmy = p.dma("sync", yA, cc_gy_o.ap()[dc * 128:(dc + 1) * 128, 0:T])
        lmy = p.dma("sync", yB, cc_gy_o.ap()[dc * 128:(dc + 1) * 128, T:TT])
        p.wait("vector", "t", tm)
        p.wait("vector", "qs", lmy)
        vm = p.op("vector", lambda eng: eng.tensor_sub(yB, yA, yB))
        p.wait("vector", "v", vm)
        vm = p.op("vector", lambda eng: eng.tensor_scalar(
            yB, yB, flag[:, 0:1], None, ALU.mult))
        p.wait("vector", "v", vm)
        vm = p.op("vector", lambda eng: eng.tensor_sub(yA, yA, yB))
        p.wait("vector", "v", vm)
        vm = p.op("vector", lambda eng: eng.tensor_mul(yA, yA, ps[:, 0:T]))
        p.wait("vector", "v", vm)
        vm = p.op("vector", lambda eng, dc=dc: eng.tensor_add(
            stream[dc][:], stream[dc][:], yA))
        p.wait("tensor", "v", vm)
        p.wait("sync", "v", vm)
    x3_done = p.mark("v")

    p.wait("tensor", "v", x3_done)
    x3tm_m = [0] * NT
    for tt in range(NT):
        for dc in range(DC):
            fn = (lambda eng, tt=tt, dc=dc:
                  eng.matmul(ptb[:, dc * 128:(dc + 1) * 128],
                             stream[dc][:, tt * 128:(tt + 1) * 128], ident[:],
                             is_transpose=True, skip_group_check=True))
            if dc == DC - 1:
                p.op("tensor", fn, sem="t")
            else:
                p.opn("tensor", fn)
        tm = p.mark("t")
        p.wait("vector", "t", tm)
        vm = p.op("vector", lambda eng, tt=tt: eng.tensor_copy(
            x3tm[tt][:], ptb[:, 0:1024]))
        p.wait("tensor", "v", vm)
        x3tm_m[tt] = vm

    qx_m = {}
    ap_m = {}
    tr_m = {}
    xh_b = [xhat, xhat]
    eps_sq = {}

    def ln4_stats(tt):
        L = (tt % 2) * 6
        fb = f32buf if tt % 2 == 0 else f32b2
        p.wait("scalar", "v", x3tm_m[tt])
        if tt - 2 in qx_m:
            p.wait("scalar", "v", qx_m[tt - 2])
        sm = p.op("scalar", lambda eng, tt=tt, fb=fb: eng.activation(
            fb[:], x3tm[tt][:], AF.Square))
        p.wait("vector", "v", ap_m.get(tt - 2, 0))
        vm = p.op("vector", lambda eng, tt=tt, L=L: eng.reduce_sum(
            ln4[:, L + 0:L + 1], x3tm[tt][:], axis=AX.X))
        p.wait("vector", "s", sm)
        vm = p.op("vector", lambda eng, fb=fb, L=L: eng.reduce_sum(
            ln4[:, L + 1:L + 2], fb[:], axis=AX.X))
        p.wait("vector", "v", vm)
        vm = p.op("vector", lambda eng, L=L: eng.tensor_scalar(
            ln4[:, L + 2:L + 3], ln4[:, L + 0:L + 1], 1.0 / D, None, ALU.mult))
        p.wait("scalar", "v", vm)
        sm = p.op("scalar", lambda eng, L=L: eng.activation(
            ln4[:, L + 3:L + 4], ln4[:, L + 2:L + 3], AF.Square))
        p.wait("vector", "s", sm)
        vm = p.op("vector", lambda eng, L=L: eng.scalar_tensor_tensor(
            ln4[:, L + 3:L + 4], ln4[:, L + 1:L + 2], 1.0 / D,
            ln4[:, L + 3:L + 4], ALU.mult, ALU.subtract))
        p.wait("scalar", "v", vm)
        sm = p.op("scalar", lambda eng, L=L: eng.activation(
            ln4[:, L + 4:L + 5], ln4[:, L + 3:L + 4], AF.Sqrt,
            bias=eps1c[:], scale=1.0))
        p.wait("vector", "s", sm)
        vm = p.op("vector", lambda eng, L=L: eng.reciprocal(
            ln4[:, L + 4:L + 5], ln4[:, L + 4:L + 5]))
        p.wait("vector", "v", vm)
        vm = p.op("vector", lambda eng, L=L: eng.scalar_tensor_tensor(
            ln4[:, L + 5:L + 6], ln4[:, L + 2:L + 3], -1.0,
            ln4[:, L + 4:L + 5], ALU.mult, ALU.mult))
        return vm

    def ln4_apply(tt, st_m):
        L = (tt % 2) * 6
        fb = f32buf if tt % 2 == 0 else f32b2
        xh = xh_b[tt % 2]
        p.wait("vector", "v", st_m)
        p.wait("vector", "s", p.mark("s"))
        if tt - 1 in tr_m:
            p.wait("vector", "t", tr_m[tt - 1])
        vm = p.op("vector", lambda eng, tt=tt, fb=fb, L=L: eng.scalar_tensor_tensor(
            fb[:], x3tm[tt][:], ln4[:, L + 4:L + 5],
            ln4[:, L + 5:L + 6].broadcast_to([128, D]), ALU.mult, ALU.add))
        ap_m[tt] = vm
        p.wait("vector", "v", vm)
        qx_m[tt] = emit_quant(p, xh, D, am, inv, tq, src=fb)
        p.wait("tensor", "v", qx_m[tt])
        for dc in range(DC):
            fn = (lambda eng, dc=dc, xh=xh:
                  eng.matmul(ptb[:, dc * 128:(dc + 1) * 128],
                             xh[:, dc * 128:(dc + 1) * 128], ident[:],
                             is_transpose=True, skip_group_check=True))
            if dc == DC - 1:
                p.op("tensor", fn, sem="t")
            else:
                p.opn("tensor", fn)
        tr_m[tt] = p.mark("t")
        p.wait("vector", "t", tr_m[tt])
        for dc in range(DC):
            vm = p.op("vector", lambda eng, tt=tt, dc=dc: eng.tensor_copy(
                xq[dc][:, tt * 128:(tt + 1) * 128],
                ptb[:, dc * 128:(dc + 1) * 128]))
        p.wait("tensor", "v", p.mark("v"))

    st_ms = {0: ln4_stats(0)}
    for tt in range(NT):
        if tt + 1 < NT:
            st_ms[tt + 1] = ln4_stats(tt + 1)
        ln4_apply(tt, st_ms[tt])

    hgrp = x3tm
    p.wait("sync", "c2", wpj_cm)

    def ldfc(grp2):
        for kc in range(DC):
            for hb in range(2):
                p.dma("sync", wfcqT[kc][:, hb * 512:(hb + 1) * 512],
                      wfcT_o.ap()[(2 * grp2 + hb) * 1024 + kc * 128:
                                  (2 * grp2 + hb) * 1024 + (kc + 1) * 128, :])
        return p.mark("qs")

    def ldpj(grp2):
        for j in range(DC):
            for es in range(DC):
                p.dma("sync", wpqT[j][:, es * 128:(es + 1) * 128],
                      wpjT_o.ap()[es * 4096 + grp2 * 1024 + j * 128:
                                  es * 4096 + grp2 * 1024 + (j + 1) * 128, :])
        return p.mark("qs")

    p.wait("sync", "t", p.mark("t"))
    fc_m = {0: ldfc(0)}
    pj_m = {0: ldpj(0)}
    for grp in range(4):
        p.wait("tensor", "qs", fc_m[grp])
        p.wait("tensor", "s", p.mark("s"))
        p.wait("scalar", "v", p.mark("v"))
        p.wait("vector", "t", p.mark("t"))
        sm_h = {}
        qv_h = {}
        for tt in range(NT):
            pb = (tt % 2) * 1024
            fb = f32buf if tt % 2 == 0 else f32b2
            if tt - 2 in sm_h:
                p.wait("tensor", "s", sm_h[tt - 2])
            for kc in range(DC):
                for nb in range(2):
                    p.opn("tensor", lambda eng, tt=tt, kc=kc, nb=nb, pb=pb:
                          eng.matmul(ps[:, pb + nb * 512:pb + (nb + 1) * 512],
                                     xq[kc][:, tt * 128:(tt + 1) * 128],
                                     wfcqT[kc][:, nb * 512:(nb + 1) * 512],
                                     start=(kc == 0), stop=False,
                                     skip_group_check=True))
            for nb in range(2):
                fn = (lambda eng, tt=tt, nb=nb, grp=grp, pb=pb:
                      eng.matmul(ps[:, pb + nb * 512:pb + (nb + 1) * 512],
                                 ones_rowb[:],
                                 bfc_sb[0:1, grp * 1024 + nb * 512:grp * 1024 + (nb + 1) * 512],
                                 start=False, stop=True,
                                 skip_group_check=True))
                if nb == 1:
                    p.op("tensor", fn, sem="t")
                else:
                    p.opn("tensor", fn)
            tm = p.mark("t")
            p.wait("scalar", "t", tm)
            if tt - 2 in qv_h:
                p.wait("scalar", "v", qv_h[tt - 2])
            sm = p.op("scalar", lambda eng, tt=tt, pb=pb, fb=fb: eng.activation(
                fb[:], ps[:, pb:pb + 1024], AF.Gelu))
            sm_h[tt] = sm
            p.wait("vector", "s", sm)
            qv_h[tt] = emit_quant(p, hgrp[tt], D, am, inv, tq, src=fb)
        hq_done = p.mark("v")
        if grp + 1 < 4:
            p.wait("sync", "t", p.mark("t"))
            fc_m[grp + 1] = ldfc(grp + 1)
        p.wait("tensor", "qs", pj_m[grp])
        for half in range(2):
            p.wait("tensor", "v", hq_done if half == 0 else p.mark("v"))
            for tt in range(NT):
                for nc4 in range(4):
                    p.op("tensor", lambda eng, tt=tt, nc4=nc4, half=half:
                         eng.matmul(ptb[:, nc4 * 128:(nc4 + 1) * 128],
                                    hgrp[tt][:, half * 512 + nc4 * 128:half * 512 + (nc4 + 1) * 128],
                                    ident[:], is_transpose=True,
                                    skip_group_check=True), sem="t")
                tm = p.mark("t")
                p.wait("vector", "t", tm)
                for nc4 in range(4):
                    vm = p.op("vector", lambda eng, tt=tt, nc4=nc4: eng.tensor_copy(
                        hqT[nc4][:, tt * 128:(tt + 1) * 128],
                        ptb[:, nc4 * 128:(nc4 + 1) * 128]))
                p.wait("tensor", "v", vm)
            for es in range(DC):
                for nc4 in range(4):
                    for tb in range(T // 512):
                        fn = (lambda eng, es=es, nc4=nc4, tb=tb, half=half:
                              eng.matmul(ps[:, 2048 + tb * 512:2048 + (tb + 1) * 512],
                                         wpqT[half * 4 + nc4][:, es * 128:(es + 1) * 128],
                                         hqT[nc4][:, tb * 512:(tb + 1) * 512],
                                         start=(nc4 == 0), stop=(nc4 == 3),
                                         skip_group_check=True))
                        if nc4 == 3 and tb == T // 512 - 1:
                            p.op("tensor", fn, sem="t")
                        else:
                            p.opn("tensor", fn)
                tm = p.mark("t")
                p.wait("vector", "t", tm)
                if grp == 0 and half == 0:
                    vm = p.op("vector", lambda eng, es=es: eng.scalar_tensor_tensor(
                        stream[es][:], ps[:, 2048:4096], bproj_pp[:, es:es + 1],
                        stream[es][:], ALU.add, ALU.add))
                else:
                    vm = p.op("vector", lambda eng, es=es: eng.tensor_add(
                        stream[es][:], stream[es][:], ps[:, 2048:4096]))
                p.wait("tensor", "v", vm)
        if grp + 1 < 4:
            p.wait("sync", "t", p.mark("t"))
            pj_m[grp + 1] = ldpj(grp + 1)

    p.wait("tensor", "v", p.mark("v"))
    dmo = 0
    for tt in range(NT):
        for dc in range(DC):
            fn = (lambda eng, tt=tt, dc=dc:
                  eng.matmul(ptb[:, dc * 128:(dc + 1) * 128],
                             stream[dc][:, tt * 128:(tt + 1) * 128], ident[:],
                             is_transpose=True, skip_group_check=True))
            if dc == DC - 1:
                p.op("tensor", fn, sem="t")
            else:
                p.opn("tensor", fn)
        tm = p.mark("t")
        p.wait("vector", "t", tm)
        if dmo:
            p.wait("vector", "qg", dmo)
        vm = p.op("vector", lambda eng: eng.tensor_copy(tq[:, 0:D], ptb[:, 0:1024]))
        p.wait("tensor", "v", vm)
        p.wait("gpsimd", "v", vm)
        dmo = p.dma("gpsimd", out_h[tt * 128:(tt + 1) * 128, :], tq[:, 0:D])
    p.wait("gpsimd", "qg", p.mark("qg"))
    P6.close()
    return p.finalize()



_CACHE = {}

def _get(name, builder):
    if name not in _CACHE:
        _CACHE[name] = builder()
    return _CACHE[name]


def _pc(a):
    return np.ascontiguousarray(np.concatenate([a[:, 512:], a[:, :512]], axis=1))


def _pr(a):
    return np.ascontiguousarray(np.concatenate([a[512:], a[:512]], axis=0))


def kernel_multi(**z):
    cores = list(range(8))
    B = np.asarray

    def shard(c):
        return c // 2, slice((c % 2) * T, (c % 2) * T + T)

    nc = _get("LA", build_LA)
    bias3 = np.concatenate([z["L0_attn_bg"], z["L0_attn_bv"],
                            z["L0_attn_bd"]]).reshape(3 * DC, 128)
    in_maps = []
    for c in cores:
        b, sl = shard(c)
        in_maps.append(dict(
            xT=np.ascontiguousarray(B(z["L0_attn_x"])[b, sl, :].T),
            Wg=B(z["L0_attn_Wg"]), Wv=B(z["L0_attn_Wv"]), Wd=B(z["L0_attn_Wd"]),
            wmix_row=B(z["L0_attn_w_rms_mix"])[None, :], bias3=bias3))
    rA = run_bass_kernel_spmd(nc, in_maps, core_ids=cores).results

    nc = _get("LB", build_LB)
    in_maps = []
    for c in cores:
        b, sl = shard(c)
        if c % 2 == 0:
            carry = np.zeros((128, DC), np.float32)
        else:
            carry = np.asarray(rA[c - 1]["h"]).astype(np.float32)[:, -1].reshape(DC, 128).T.copy()
        in_maps.append(dict(
            xT=np.ascontiguousarray(B(z["L0_attn_x"])[b, sl, :].T),
            xhat=np.asarray(rA[c]["xhat"]), h=np.asarray(rA[c]["h"]),
            Acum=np.asarray(rA[c]["Acum"]), carry=carry,
            wffn_row=B(z["L0_attn_w_rms_ffn"])[None, :],
            W_gate=B(z["L0_attn_W_gate"]), W_up=B(z["L0_attn_W_up"]),
            W_out=B(z["L0_attn_W_out"])))
    rB = run_bass_kernel_spmd(nc, in_maps, core_ids=cores).results

    nc = _get("LC", build_LC)
    mu2 = np.concatenate([z["L0_ffn_mu_r"], z["L0_ffn_mu_k"]]).reshape(2 * DC, 128)
    in_maps = []
    for c in cores:
        b, sl = shard(c)
        if c % 2 == 0:
            pcol = np.zeros((128, DC), dtype=NB)
        else:
            pcol = np.ascontiguousarray(
                np.asarray(rB[c - 1]["x1"])[:, -1].reshape(DC, 128).T)
        in_maps.append(dict(
            x1=np.asarray(rB[c]["x1"]), pcol=pcol,
            W_rT=np.ascontiguousarray(B(z["L0_ffn_W_r"]).T),
            W_kT=np.ascontiguousarray(B(z["L0_ffn_W_k"]).T),
            W_vT=np.ascontiguousarray(B(z["L0_ffn_W_v"]).T),
            mu2=mu2))
    rC = run_bass_kernel_spmd(nc, in_maps, core_ids=cores).results

    nc = _get("LD", build_LD)
    in_maps = []
    for c in cores:
        in_maps.append(dict(
            x2=np.asarray(rC[c]["x2"]),
            W_dkvT=np.ascontiguousarray(B(z["L1_attn_W_dkv"]).T)))
    rD = run_bass_kernel_spmd(nc, in_maps, core_ids=cores).results

    nc = _get("LE", build_LE)
    mask4 = np.zeros((4, 128, 512), np.float32)
    for qt in range(4):
        for r in range(128):
            mask4[qt, r, qt * 128 + r + 1:] = -1e30
    ident = np.eye(128, dtype=NB)
    identf = np.eye(128, dtype=np.float32)
    in_maps = []
    for c in cores:
        b, half = c // 2, c % 2
        xh3 = np.concatenate([np.asarray(rD[2 * b]["xhat3"]),
                              np.asarray(rD[2 * b + 1]["xhat3"])], axis=1)
        ckv = np.concatenate([np.asarray(rD[2 * b]["ckv"]),
                              np.asarray(rD[2 * b + 1]["ckv"])], axis=1)
        WuqT = np.ascontiguousarray(B(z["L1_attn_W_uq"]).T)
        WoT = np.ascontiguousarray(B(z["L1_attn_W_o"]).T)
        in_maps.append(dict(
            xhat3=xh3, ckv=ckv,
            W_dq=B(z["L1_attn_W_dq"]),
            W_uqT_p=WuqT if half == 0 else _pc(WuqT),
            W_dq_p=B(z["L1_attn_W_dq"]) if half == 0 else _pc(B(z["L1_attn_W_dq"])),
            W_uk_p=B(z["L1_attn_W_uk"]) if half == 0 else _pr(B(z["L1_attn_W_uk"])),
            W_uv=B(z["L1_attn_W_uv"]),
            W_oT_p=WoT if half == 0 else _pc(WoT),
            ident=ident, identf=identf, mask4=mask4))
    rE = run_bass_kernel_spmd(nc, in_maps, core_ids=cores).results

    nc = _get("LF", build_LF)
    ind = np.zeros((16, D), dtype=NB)
    for e in range(D):
        ind[e // 64, e] = 1.0
    in_maps = []
    for c in cores:
        b, half = c // 2, c % 2
        sl = slice(half * T, half * T + T)
        ym = np.concatenate([np.asarray(rE[2 * b]["y"])[:, sl],
                             np.asarray(rE[2 * b + 1]["y"])[:, sl]], axis=0)
        rza = np.concatenate([np.asarray(rE[2 * b]["rz"]).reshape(8, 4096)[:, sl],
                              np.asarray(rE[2 * b + 1]["rz"]).reshape(8, 4096)[:, sl]],
                             axis=0)
        in_maps.append(dict(
            x2=np.asarray(rC[c]["x2"]), ym=np.ascontiguousarray(ym),
            rza=np.ascontiguousarray(rza), ind=ind,
            w_fc=B(z["L1_ffn_w_fc"]), bfc_row=B(z["L1_ffn_b_fc"])[None, :],
            w_proj=B(z["L1_ffn_w_proj"]),
            bproj=B(z["L1_ffn_b_proj"]).reshape(DC, 128),
            ident=ident))
    rF = run_bass_kernel_spmd(nc, in_maps, core_ids=cores).results

    out = np.empty((4, 2 * T, D), np.float32)
    for c in cores:
        b, sl = shard(c)
        out[b, sl, :] = np.asarray(rF[c]["out"])
    return out


def kernel(**z):
    cores = list(range(8))
    B = np.asarray
    nc = _get("FUSED", build_fused)
    mask4 = np.zeros((4, 128, 512), np.float32)
    for qt in range(4):
        for r in range(128):
            mask4[qt, r, qt * 128 + r + 1:] = -1e30
    ident = np.eye(128, dtype=NB)
    identf = np.eye(128, dtype=np.float32)
    ind = np.zeros((16, D), dtype=NB)
    for e in range(D):
        ind[e // 64, e] = 1.0
    bias3 = np.concatenate([z["L0_attn_bg"], z["L0_attn_bv"],
                            z["L0_attn_bd"]]).reshape(3 * DC, 128)
    mu2 = np.concatenate([z["L0_ffn_mu_r"], z["L0_ffn_mu_k"]]).reshape(2 * DC, 128)
    WuqT = np.ascontiguousarray(B(z["L1_attn_W_uq"]).T)
    WoT = np.ascontiguousarray(B(z["L1_attn_W_o"]).T)
    Wdq = B(z["L1_attn_W_dq"])
    Wuk = B(z["L1_attn_W_uk"])
    in_maps = []
    for c in cores:
        b, half = c // 2, c % 2
        sl = slice(half * T, half * T + T)
        in_maps.append(dict(
            xT=np.ascontiguousarray(B(z["L0_attn_x"])[b, sl, :].T),
            Wg=B(z["L0_attn_Wg"]), Wv=B(z["L0_attn_Wv"]), Wd=B(z["L0_attn_Wd"]),
            wmix_row=B(z["L0_attn_w_rms_mix"])[None, :], bias3=bias3,
            wffn_row=B(z["L0_attn_w_rms_ffn"])[None, :],
            W_gate=B(z["L0_attn_W_gate"]), W_up=B(z["L0_attn_W_up"]),
            W_out=B(z["L0_attn_W_out"]),
            W_rT=np.ascontiguousarray(B(z["L0_ffn_W_r"]).T),
            W_kT=np.ascontiguousarray(B(z["L0_ffn_W_k"]).T),
            W_vT=np.ascontiguousarray(B(z["L0_ffn_W_v"]).T),
            mu2=mu2,
            W_dkvT=np.ascontiguousarray(B(z["L1_attn_W_dkv"]).T),
            W_dq=Wdq,
            W_uqT_p=WuqT if half == 0 else _pc(WuqT),
            W_dq_p=Wdq if half == 0 else _pc(Wdq),
            W_uk_p=Wuk if half == 0 else _pr(Wuk),
            W_uv=B(z["L1_attn_W_uv"]),
            W_oT_p=WoT if half == 0 else _pc(WoT),
            ident=ident, identf=identf, mask4=mask4,
            wfc_sh=np.ascontiguousarray(B(z["L1_ffn_w_fc"])[c * 512:(c + 1) * 512, :]),
            bfc_row=B(z["L1_ffn_b_fc"])[None, :],
            wpj_sh=np.ascontiguousarray(B(z["L1_ffn_w_proj"])[c * 128:(c + 1) * 128, :]),
            bproj=B(z["L1_ffn_b_proj"]).reshape(DC, 128),
            ind=ind,
            flag=np.full((128, 1), float(half), np.float32),
        ))
    res = run_bass_kernel_spmd(nc, in_maps, core_ids=cores).results
    out = np.empty((4, 2 * T, D), np.float32)
    for c in cores:
        b, sl = c // 2, slice((c % 2) * T, (c % 2) * T + T)
        out[b, sl, :] = np.asarray(res[c]["out"])
    return out
